# version 07
# baseline (speedup 1.0000x reference)
import numpy as np
import ml_dtypes
DBG = False
import concourse.bass as bass
import concourse.mybir as mybir
from concourse.bass_utils import run_bass_kernel_spmd
from concourse.bass import IndirectOffsetOnAxis
from contextlib import ExitStack

F32 = mybir.dt.float32
BF16 = mybir.dt.bfloat16
I32 = mybir.dt.int32
AF = mybir.ActivationFunctionType
OP = mybir.AluOpType
AX = mybir.AxisListType
bfd = ml_dtypes.bfloat16

NCORES = 8
BPC = 2
L = 2048
DM = 1024
DF = 4096
E = 8
CL = 768
NTOK = BPC * L
TILES = NTOK // 128
NS = E * CL
NT = NS // 128
CPT = CL // 128
CCAP = 6144
SENT = float(1 << 21)
EPS = 1e-5

ENGINES = ("sync", "vector", "scalar", "tensor", "gpsimd")
QCH = {"sync": "qs", "scalar": "qa", "gpsimd": "qg", "vector": "qv", "tensor": "qt"}


class Prog:
    def __init__(self, nc, ctx):
        self.nc = nc
        self.ctx = ctx
        self.chan = {}
        for ch in list(ENGINES) + list(QCH.values()):
            self.chan[ch] = [ctx.enter_context(nc.semaphore("sem_" + ch)), 0]
        self.steps = {e: [] for e in ENGINES}
        self.watermark = {e: {} for e in ENGINES}

    def stream(self, name):
        ch = "#" + name
        if ch not in self.chan:
            self.chan[ch] = [self.ctx.enter_context(self.nc.semaphore("s" + name)), 0]
        return ch

    def _queue(self, engine, fn, after, sem_ch, inc, tag):
        waits = {}
        for c, v in after:
            if v > 0 and waits.get(c, 0) < v:
                waits[c] = v
        tok = None
        if sem_ch is not None:
            sem, cnt = self.chan[sem_ch]
            cnt += inc
            self.chan[sem_ch][1] = cnt
            tok = (sem_ch, cnt)
        wm = self.watermark[engine]
        needed = []
        for c, v in sorted(waits.items()):
            if wm.get(c, 0) < v:
                needed.append((c, v))
                wm[c] = v
        self.steps[engine].append(
            (needed, fn, self.chan[sem_ch][0] if tok else None, inc if tok else 0, tag))
        return tok

    def do(self, engine, fn, after=(), sig=False, tag=""):
        return self._queue(engine, fn, after, engine if sig else None, 1, tag)

    def dma(self, engine, fn, stream, after=(), tag=""):
        return self._queue(engine, fn, after, self.stream(stream), 16, tag)

    def stream_all(self, name):
        ch = "#" + name
        return (ch, self.chan[ch][1])

    def etok(self, engine):
        return (engine, self.chan[engine][1])

    def final_waits(self, engine, tokens):
        waits = {}
        for ch, v in tokens:
            if ch is None or v <= 0:
                continue
            waits[ch] = max(waits.get(ch, 0), v)
        self.steps[engine].append((sorted(waits.items()), lambda e: None, None, 0, "final"))

    def verify(self):
        pos = {e: 0 for e in ENGINES}
        counts = {ch: 0 for ch in self.chan}
        sem2ch = {id(h): ch for ch, (h, _) in self.chan.items()}
        progressed = True
        while progressed:
            progressed = False
            for e in ENGINES:
                while pos[e] < len(self.steps[e]):
                    needed, fn, sem, inc, tag = self.steps[e][pos[e]]
                    if all(counts[ch] >= v for ch, v in needed):
                        if inc:
                            counts[sem2ch[id(sem)]] += inc
                        pos[e] += 1
                        progressed = True
                    else:
                        break
        stuck = [(e, pos[e], self.steps[e][pos[e]][0], self.steps[e][pos[e]][4])
                 for e in ENGINES if pos[e] < len(self.steps[e])]
        if stuck:
            msg = "; ".join(f"{e}@{i} tag={t!r} needs {n}" for e, i, n, t in stuck)
            raise RuntimeError("schedule deadlock: " + msg)

    def build(self, block):
        chans = self.chan

        def emit_for(name):
            def run(eng):
                for needed, fn, sem, inc, tag in self.steps[name]:
                    for ch, v in needed:
                        eng.wait_ge(chans[ch][0], v)
                    inst = fn(eng)
                    if inc:
                        assert inst is not None, f"sig step returned None: {tag}"
                        inst.then_inc(sem, inc)
            return run

        for name in ENGINES:
            if self.steps[name]:
                getattr(block, name)(emit_for(name))



class View:
    def __init__(self, ap_):
        self._ap = ap_
    def __getitem__(self, k):
        return self._ap[k]
    def ap(self):
        return self._ap


def build_program():
    nc = bass.Bass("TRN2", target_bir_lowering=False, detect_race_conditions=False)
    DI = lambda nm, shp, dt=F32: nc.dram_tensor(nm, shp, dt, kind="ExternalInput")

    x_d = DI("x", [NTOK, DM])
    xr_d = DI("xr", [128, TILES, DM])
    wlnz = DI("wlnzT", [128, 8 * DM]);  wdt = DI("wdtT", [128, 8 * DM])
    wy = DI("wyT", [128, 8 * DM]);      wya = DI("wyaT", [128, 8 * DM])
    blnz_d = DI("blnz", [128, 8]); bdt_d = DI("bdt", [128, 8])
    by_d = DI("by", [128, 8]);     bya_d = DI("bya", [128, 8])
    h0_d = DI("h0r", [128, 8])
    rw_d = DI("routerT", [DM, E])
    w1_d = DI("w1r", [E, 8, 128, 4096]);  b1_d = DI("b1r", [128, E * 32])
    w2_d = DI("w2r", [E, 128, 32 * DM]);  b2_d = DI("b2n", [E, DM])
    identb_d = DI("identb", [128, 128], BF16)
    identf_d = DI("identf", [128, 128])
    triu_d = DI("triu", [128, 128], BF16)
    on128_d = DI("ones128", [128, 1], BF16)
    onr_d = DI("onesrow", [1, 128])
    onrb_d = DI("onesrowb", [1, 128], BF16)
    ioe_d = DI("iotae", [128, TILES * E])
    tokid_d = DI("tokidi", [128, TILES], I32)
    tokidf_d = DI("tokidf", [128, TILES])
    slotp_d = DI("slotpos", [128, NT])
    sel_d = DI("selrow", [1, 8])
    prev_d = DI("prevmask", [8, 8])
    id8_d = DI("ident8", [8, 8])

    out_d = nc.dram_tensor("out", [NTOK, DM], F32, kind="ExternalOutput")
    if DBG:
        or_d = nc.dram_tensor("o_r", [NTOK, DM], F32, kind="ExternalOutput")
        oh_d = nc.dram_tensor("o_hT", [BPC, 128, 8 * L], F32, kind="ExternalOutput")
        ox1_d = nc.dram_tensor("o_x1T", [BPC, 128, 8 * L], F32, kind="ExternalOutput")
        olg_d = nc.dram_tensor("o_lg", [128, TILES * E], F32, kind="ExternalOutput")
        ox2_d = nc.dram_tensor("o_x2", [NTOK, DM], F32, kind="ExternalOutput")
        og_d = nc.dram_tensor("o_gs", [128, NT], F32, kind="ExternalOutput")
        opt_d = nc.dram_tensor("o_pt", [128, NT], F32, kind="ExternalOutput")
    x2_d = nc.dram_tensor("x2sc", [NTOK, DM], BF16)
    pg_d = nc.dram_tensor("pgd", [NS, 2], F32)
    cc_i = nc.dram_tensor("cc_i", [8, 8], F32)
    cc_o = nc.dram_tensor("cc_o", [8, 8], F32)

    ctx = ExitStack()
    S = lambda nm, shp, dt=F32: ctx.enter_context(nc.sbuf_tensor(nm, shp, dt))

    identb = S("s_identb", [128, 128], BF16)
    identf = S("s_identf", [128, 128])
    triu = S("s_triu", [128, 128], BF16)
    on128 = S("s_on128", [128, 1], BF16)
    onr = S("s_onr", [1, 128])
    onrb = S("s_onrb", [1, 128], BF16)
    ioe = S("s_ioe", [128, TILES * E])
    tokidi = S("s_tokidi", [128, TILES], I32)
    tokidf = S("s_tokidf", [128, TILES])
    slotp = S("s_slotp", [128, NT])
    sel = S("s_sel", [1, 8]); prv = S("s_prv", [8, 8]); id8 = S("s_id8", [8, 8])
    blnz = S("s_blnz", [128, 8]); bdt = S("s_bdt", [128, 8]); nbdt = S("s_nbdt", [128, 8])
    by = S("s_by", [128, 8]); bya = S("s_bya", [128, 8]); h0r = S("s_h0r", [128, 8])
    rwT = S("s_rwT", [128, 8, E])
    b1r = S("s_b1r", [128, E * 32])
    b2bf = S("s_b2bf", [1, 1024], BF16)
    logits = S("s_logits", [128, TILES * E])
    TE = TILES * E
    mrow = S("s_mrow", [128, TILES]); exps = S("s_exps", [128, TE])
    zrow = S("s_zrow", [128, TILES]); rz = S("s_rz", [128, TILES])
    prob = S("s_prob", [128, TE]); mxp = S("s_mxp", [128, TILES])
    igx = S("s_igx", [128, TE]); eqm = S("s_eqm", [128, TE])
    idxm = S("s_idxm", [128, TILES]); idxa = S("s_idxa", [128, TILES])
    ohb = S("s_ohb", [128, TE], BF16); ohf = S("s_ohf", [128, TE])
    tot = S("s_tot", [1, TE]); offa = S("s_offa", [1, TE]); offb = S("s_offb", [1, TE])
    offbc = S("s_offbc", [128, TE]); rt1 = S("s_rt1", [128, TE]); rt2 = S("s_rt2", [128, TE])
    pos0 = S("s_pos0", [128, TILES]); pos = S("s_pos", [128, TILES])
    slot = S("s_slot", [128, TILES]); ovf = S("s_ovf", [128, TILES])
    sloti = S("s_sloti", [128, TILES], I32)
    mgi = S("s_mgi", [128, TILES], I32); dvi = S("s_dvi", [128, TILES], I32)
    tgt = S("s_tgt", [128, TILES], I32)
    cnts = S("s_cnts", [1, E])
    sentt = S("s_sentt", [128, NT], I32)
    sentf = S("s_sentf", [128, NT * 2])
    pgv = S("s_pgv", [128, TILES * 2])
    ptok = S("s_ptok", [128, NT], I32)
    gsb = S("s_gsb", [128, NT]); gm = S("s_gm", [128, NT]); gsb2 = S("s_gsb2", [128, NT])
    ccs = S("s_ccs", [8, 8]); ccm = S("s_ccm", [8, 8]); msb = S("s_msb", [8, 8])
    offcol = S("s_offcol", [8, 1]); offrow = S("s_offrow", [1, 8]); thr = S("s_thr", [128, E])
    mus = S("s_mus", [128, 4]); ssq = S("s_ssq", [128, 4])
    veps0 = S("s_veps0", [128, 4]); mm2 = S("s_mm2", [128, 4]); veps = S("s_veps", [128, 4])
    sdv = S("s_sdv", [128, 4]); rstd = S("s_rstd", [128, 4])

    AR = 176 * 1024
    arena = S("s_arena", [128, AR], mybir.dt.uint8)

    def carve(off, shape, dt):
        esz = {F32: 4, BF16: 2, I32: 4}[dt]
        n = 1
        for s in shape[1:]:
            n *= s
        nb = n * esz
        assert off % 4 == 0 and off + nb <= AR, (off, nb)
        ap_ = arena[:, off:off + nb].bitcast(dt)
        if len(shape) == 3:
            ap_ = ap_.rearrange("p (a b) -> p a b", a=shape[1])
        elif len(shape) == 4:
            ap_ = ap_.rearrange("p (a b c) -> p a b c", a=shape[1], b=shape[2])
        return View(ap_), off + nb

    off = 0
    wslotA, off = carve(off, [128, 2, 8, 1024], BF16)
    wslotB = wslotA
    x1T, off = carve(off, [128, 8, L], BF16)
    hT, off = carve(off, [128, 8, L], BF16)
    a_s, off = carve(off, [128, L], F32)
    z_s, off = carve(off, [128, L], F32)
    u_s, off = carve(off, [128, L], F32)
    xre, off = carve(off, [128, 4, 1024], F32)
    xbuf = View(xre.ap().rearrange("p a d -> p (a d)").rearrange("p (a b d) -> p a b d", a=2, b=2))
    rbuf, off = carve(off, [128, 4, 1024], F32)
    x1tm, off = carve(off, [128, 1024], BF16)
    x2f, off = carve(off, [128, 1024], F32)
    x2b, off = carve(off, [128, 1024], BF16)
    x2T, off = carve(off, [128, 8, 128], F32)
    gate, off = carve(off, [128, 512], F32)
    attnf, off = carve(off, [128, 512], F32)
    sq, off = carve(off, [128, 1024], F32)
    A_END = off
    off = 0
    gtb, off = carve(off, [128, CPT, 1024], BF16)
    xeT0, off = carve(off, [128, 8, CL], BF16)
    xeT1, off = carve(off, [128, 8, CL], BF16)
    xeT = [xeT0, xeT1]
    w1w0, off = carve(off, [128, 8, 512], BF16)
    w1w1, off = carve(off, [128, 8, 512], BF16)
    w1win = [w1w0, w1w1]
    w2sb, off = carve(off, [128, 32, 1024], BF16)
    h_bf, off = carve(off, [128, 32, CL], BF16)
    eob0, off = carve(off, [128, 1024], F32)
    eob1, off = carve(off, [128, 1024], F32)
    eob = [eob0, eob1]
    assert off <= A_END, (off, A_END)

    P = lambda nm, shp, dt=F32: ctx.enter_context(nc.psum_tensor(nm, shp, dt))
    pf0 = P("pf0", [128, 512]); pf1 = P("pf1", [128, 512])
    pf2 = P("pf2", [128, 512]); pf3 = P("pf3", [128, 512])
    pf4 = P("pf4", [128, 512]); pf5 = P("pf5", [128, 512])
    pb0 = P("pb0", [128, 1024], BF16); pb1 = P("pb1", [128, 1024], BF16)

    p = Prog(nc, ctx)
    T_ALL = (None, 0)

    reg_ns = nc.gpsimd.alloc_register("bc_ns")
    reg_nt = nc.gpsimd.alloc_register("bc_nt")

    def setregs(g):
        g.reg_mov(reg_ns, NS - 1)
        return g.reg_mov(reg_nt, NTOK - 1)
    t_regs = p.do("gpsimd", setregs, sig=True, tag="regs")

    for nm, dst, src in [("identb", identb, identb_d), ("identf", identf, identf_d),
                         ("triu", triu, triu_d), ("on128", on128, on128_d),
                         ("onr", onr, onr_d), ("onrb", onrb, onrb_d),
                         ("ioe", ioe, ioe_d), ("tokidi", tokidi, tokid_d), ("tokidf", tokidf, tokidf_d),
                         ("slotp", slotp, slotp_d), ("sel", sel, sel_d),
                         ("prv", prv, prev_d), ("id8", id8, id8_d),
                         ("blnz", blnz, blnz_d), ("bdt", bdt, bdt_d),
                         ("by", by, by_d), ("bya", bya, bya_d),
                         ("h0r", h0r, h0_d), ("b1r", b1r, b1_d)]:
        p.dma("sync", lambda s, d=dst, sr=src: s.dma_start(out=d[...], in_=sr[...]),
              stream="consts", tag="ld_" + nm)
    t_rw = p.dma("sync", lambda s: s.dma_start(out=rwT[...], in_=rw_d.ap().rearrange("(a b) c -> b a c", b=128)),
                 stream="consts", tag="ld_rw")
    t_consts = p.stream_all("consts")
    t_nbdt = p.do("vector", lambda v: v.tensor_scalar(out=nbdt[:, :], in0=bdt[:, :], scalar1=-1.0,
                                                      scalar2=None, op0=OP.mult), after=[t_consts], sig=True, tag="nbdt")
    t_sent = p.do("vector", lambda v: v.memset(sentt[:, :], 1 << 20), sig=True, tag="sent")
    t_sf = p.do("vector", lambda v: v.tensor_copy(out=sentf[:, 0:NT * 2:2], in_=sentt[:, :]),
                after=[t_sent], sig=True, tag="sentf")
    t_sf = p.do("vector", lambda v: v.memset(sentf[:, 1:NT * 2:2], 0.0), after=[t_sf], sig=True, tag="sentg")
    t_pfill = p.dma("sync", lambda s: s.dma_start(out=pg_d.ap().rearrange("(a b) c -> a (b c)", a=128),
                                                  in_=sentf[:, :]), stream="pfill", after=[t_sf], tag="pfill")
    t_gfill = t_pfill

    def load_wslot(slot_t, pair, stream, after):
        toks = []
        for i, wsrc in enumerate(pair):
            toks.append(p.dma("gpsimd",
                lambda g, ii=i, ws=wsrc, sl=slot_t: g.dma_start(
                    out=sl[:, ii, :, :], in_=ws.ap().rearrange("p (a c) -> p a c", a=8)),
                stream=stream, after=after, tag=f"w_{stream}"))
        return toks

    wA_t = load_wslot(wslotA, [wdt, wlnz], "wA", [])

    NQ = L // 512
    tok_x1T_done = [None] * BPC
    tok_x1T_q = [[None] * NQ for _ in range(BPC)]
    tok_hT_done = [None] * BPC
    tok_logits = []
    pf_free = {id(b): T_ALL for b in (pf0, pf1, pf2, pf3, pf4, pf5)}
    pb_free = {id(b): T_ALL for b in (pb0, pb1)}
    xb_free = [T_ALL, T_ALL]
    t_prev_stats = T_ALL
    t_prev_sqrt = T_ALL
    t_x1tm_free = T_ALL
    t_x2f_free = T_ALL
    t_x2b_free = T_ALL
    t_x2T_free = T_ALL
    t_gate_free = T_ALL
    t_attnf_free = T_ALL
    t_rbuf_free = T_ALL
    t_xre_free = T_ALL

    for b in range(BPC):
        x1t_toks = []
        for q in range(NQ):
            half_toks = []
            for h2 in range(2):
                buf = h2
                rows0 = b * L + q * 512 + h2 * 256
                tile0 = rows0 // 128
                t_x = p.dma("sync", lambda s, tl=tile0, bb=buf: s.dma_start(
                    out=xbuf[:, bb, :, :], in_=xr_d[:, tl:tl + 2, :]),
                    stream=f"x{buf}", after=[xb_free[buf], t_xre_free], tag=f"xld{b}{q}{h2}")
                half_toks.append((buf, t_x))
            mu_toks = []
            for h2, (buf, t_x) in enumerate(half_toks):
                for tt2 in range(2):
                    col = h2 * 2 + tt2
                    mu_toks.append(p.do("vector", lambda v, bb=buf, t2=tt2, cc=col: v.tensor_reduce(
                        mus[:, cc:cc + 1], xbuf[:, bb, t2, :].rearrange("p d -> p () d"),
                        axis=AX.X, op=OP.add), after=[t_x, t_prev_stats], sig=True, tag="mu"))
                    mu_toks.append(p.do("scalar", lambda s, bb=buf, t2=tt2, cc=col: s.activation(
                        out=sq[:, :], in_=xbuf[:, bb, t2, :], func=AF.Square,
                        accum_out=ssq[:, cc:cc + 1]), after=[t_x, t_prev_stats], sig=True, tag="ssq"))
            tv = p.do("vector", lambda v: v.tensor_scalar(out=mus[:, :], in0=mus[:, :], scalar1=1.0 / DM,
                                                          scalar2=None, op0=OP.mult), after=mu_toks, sig=True)
            tv2 = p.do("vector", lambda v: (v.tensor_scalar(out=veps0[:, :], in0=ssq[:, :], scalar1=1.0 / DM,
                                                            scalar2=EPS, op0=OP.mult, op1=OP.add),
                                            v.tensor_mul(mm2[:, :], mus[:, :], mus[:, :]))[-1],
                       after=[tv, t_prev_sqrt], sig=True)
            tv2 = p.do("vector", lambda v: v.tensor_sub(veps[:, :], veps0[:, :], mm2[:, :]), after=[tv2], sig=True)
            tsc = p.do("scalar", lambda s: s.activation(out=sdv[:, :], in_=veps[:, :], func=AF.Sqrt),
                       after=[tv2], sig=True, tag="sqrt")
            t_prev_sqrt = tsc
            trc = p.do("vector", lambda v: v.reciprocal(rstd[:, :], sdv[:, :]), after=[tsc], sig=True)
            ap_toks = []
            for h2, (buf, t_x) in enumerate(half_toks):
                for tt2 in range(2):
                    col = h2 * 2 + tt2
                    gt_idx = q * 4 + col
                    ta = p.do("vector", lambda v, bb=buf, t2=tt2, cc=col: v.tensor_scalar(
                        out=x1tm[:, :], in0=xbuf[:, bb, t2, :], scalar1=mus[:, cc:cc + 1],
                        scalar2=rstd[:, cc:cc + 1], op0=OP.subtract, op1=OP.mult),
                        after=[trc, t_x, t_x1tm_free], sig=True, tag="lnap")
                    ap_toks.append(ta)
                    bank = (pb0, pb1)[gt_idx % 2]
                    ttr = p.do("tensor", lambda te, bk=bank: [te.transpose(
                        bk[:, c * 128:(c + 1) * 128], x1tm[:, c * 128:(c + 1) * 128], identb[:, :])
                        for c in range(8)][-1],
                        after=[ta, t_consts, pb_free[id(bank)]], sig=True, tag="x1tr")
                    t_x1tm_free = ttr
                    tcp = p.do("vector", lambda v, bk=bank, g2=gt_idx: v.tensor_copy(
                        out=x1T.ap()[:, :, g2 * 128:(g2 + 1) * 128],
                        in_=bk.ap().rearrange("p (c t) -> p c t", c=8)),
                        after=[ttr], sig=True, tag="x1cp")
                    pb_free[id(bank)] = tcp
                    x1t_toks.append(tcp)
                xb_free[buf] = ap_toks[-1]
            t_prev_stats = ap_toks[-1]
            tok_x1T_q[b][q] = x1t_toks[-1]
        tok_x1T_done[b] = x1t_toks[-1]

        scan_toks = []
        prev_scan = tok_hT_done[b - 1] if b > 0 else T_ALL
        prev_u = prev_scan
        for e in range(8):
            act_dt = []
            act_z = []
            for q in range(NQ):
                bankd = (pf0, pf1)[q % 2]
                tmm = p.do("tensor", lambda te, ee=e, qq=q, bk=bankd: [te.matmul(
                    bk[:, :], wslotA[:, 0, dc, ee * 128:(ee + 1) * 128],
                    x1T[:, dc, qq * 512:(qq + 1) * 512], start=(dc == 0), stop=(dc == 7))
                    for dc in range(8)][-1],
                    after=[tok_x1T_q[b][q], pf_free[id(bankd)]] + wA_t, sig=True, tag="dtmm")
                ta = p.do("scalar", lambda s, qq=q, bk=bankd, ee=e: s.activation(
                    out=a_s[:, qq * 512:(qq + 1) * 512], in_=bk[:, :], func=AF.Sigmoid,
                    bias=nbdt[:, ee:ee + 1], scale=-1.0), after=[tmm, t_nbdt, prev_scan], sig=True, tag="a")
                pf_free[id(bankd)] = ta
                act_dt.append(ta)
                bankz = (pf2, pf3)[q % 2]
                tmz = p.do("tensor", lambda te, ee=e, qq=q, bk=bankz: [te.matmul(
                    bk[:, :], wslotA[:, 1, dc, ee * 128:(ee + 1) * 128],
                    x1T[:, dc, qq * 512:(qq + 1) * 512], start=(dc == 0), stop=(dc == 7))
                    for dc in range(8)][-1],
                    after=[tok_x1T_q[b][q], pf_free[id(bankz)]] + wA_t, sig=True, tag="zmm")
                tz = p.do("scalar", lambda s, qq=q, bk=bankz, ee=e: s.activation(
                    out=z_s[:, qq * 512:(qq + 1) * 512], in_=bk[:, :], func=AF.Sigmoid,
                    bias=blnz[:, ee:ee + 1], scale=1.0), after=[tmz, prev_u], sig=True, tag="z")
                pf_free[id(bankz)] = tz
                act_z.append(tz)
            tu = p.do("vector", lambda v: v.tensor_mul(u_s[:, :], a_s[:, :], z_s[:, :]),
                      after=act_dt + act_z, sig=True, tag="az")
            tu = p.do("vector", lambda v: v.tensor_sub(u_s[:, :], z_s[:, :], u_s[:, :]),
                      after=[tu], sig=True, tag="u")
            tsc2 = p.do("vector", lambda v, ee=e: v.tensor_tensor_scan(
                out=hT.ap()[:, ee, :], data0=a_s[:, :], data1=u_s[:, :],
                initial=h0r[:, ee:ee + 1], op0=OP.mult, op1=OP.add),
                after=[tu, t_consts], sig=True, tag="scan")
            scan_toks.append(tsc2)
            prev_scan = tsc2
            prev_u = tu
        tok_hT_done[b] = scan_toks[-1]

        if DBG:
            p.dma("gpsimd", lambda g, bb=b: g.dma_start(out=oh_d[bb].rearrange("p (c t) -> p (c t)") if False else oh_d[bb],
                                                        in_=hT.ap().rearrange("p c t -> p (c t)")),
                  stream="dbgg", after=[tok_hT_done[b], p.etok("tensor")], tag="hdump")
            p.dma("gpsimd", lambda g, bb=b: g.dma_start(out=ox1_d[bb],
                                                        in_=x1T.ap().rearrange("p c t -> p (c t)")),
                  stream="dbgg", after=[tok_x1T_done[b], p.etok("tensor")], tag="x1dump")
        wB_t = load_wslot(wslotB, [wya, wy], "wA", [p.etok("tensor")])
        for q in range(NQ):
            rows0 = b * L + q * 512
            t_xre = p.dma("sync", lambda s, tl=rows0 // 128: s.dma_start(
                out=xre[...], in_=xr_d[:, tl:tl + 4, :]),
                stream="xre", after=[xb_free[0], xb_free[1], t_xre_free], tag="xre")
            radd_toks = []
            for e in range(8):
                bankg = (pf0, pf1)[e % 2]
                tmy = p.do("tensor", lambda te, ee=e, qq=q, bk=bankg: [te.matmul(
                    bk[:, :], wslotB[:, 0, dc, ee * 128:(ee + 1) * 128],
                    x1T[:, dc, qq * 512:(qq + 1) * 512], start=(dc == 0), stop=(dc == 7))
                    for dc in range(8)][-1],
                    after=[tok_x1T_done[b], pf_free[id(bankg)]] + wB_t, sig=True, tag="yamm")
                tg = p.do("scalar", lambda s, bk=bankg, ee=e: s.activation(
                    out=gate[:, :], in_=bk[:, :], func=AF.Silu,
                    bias=bya[:, ee:ee + 1], scale=1.0), after=[tmy, t_gate_free], sig=True, tag="gate")
                pf_free[id(bankg)] = tg
                banky = (pf0, pf1)[(e + 1) % 2]
                tmy2 = p.do("tensor", lambda te, ee=e, qq=q, bk=banky: [te.matmul(
                    bk[:, :], wslotB[:, 1, dc, ee * 128:(ee + 1) * 128],
                    hT[:, dc, qq * 512:(qq + 1) * 512], start=(dc == 0), stop=(dc == 7))
                    for dc in range(8)][-1],
                    after=[tok_hT_done[b], pf_free[id(banky)]] + wB_t, sig=True, tag="ymm")
                tat = p.do("vector", lambda v, bk=banky, ee=e: v.scalar_tensor_tensor(
                    out=attnf[:, :], in0=bk[:, :], scalar=by[:, ee:ee + 1], in1=gate[:, :],
                    op0=OP.add, op1=OP.mult), after=[tmy2, tg, t_attnf_free], sig=True, tag="attnf")
                pf_free[id(banky)] = tat
                t_gate_free = tat
                bankt = (pf2, pf3)[e % 2]
                ttr = p.do("tensor", lambda te, bk=bankt: [te.transpose(
                    bk[:, tt2 * 128:(tt2 + 1) * 128], attnf[:, tt2 * 128:(tt2 + 1) * 128], identf[:, :])
                    for tt2 in range(4)][-1],
                    after=[tat, pf_free[id(bankt)], t_consts], sig=True, tag="attntr")
                t_attnf_free = ttr
                tra = p.do("vector", lambda v, bk=bankt, ee=e: v.tensor_tensor(
                    out=rbuf.ap()[:, :, ee * 128:(ee + 1) * 128],
                    in0=bk.ap().rearrange("p (t c) -> p t c", t=4),
                    in1=xre.ap()[:, :, ee * 128:(ee + 1) * 128], op=OP.add),
                    after=[ttr, t_xre, t_rbuf_free], sig=True, tag="radd")
                pf_free[id(bankt)] = tra
                radd_toks.append(tra)
            t_xre_free = radd_toks[-1]
            t_rw2 = p.dma("sync", lambda s, r0=rows0: s.dma_start(
                out=out_d[r0:r0 + 512, :].rearrange("(t p) d -> p t d", p=128), in_=rbuf[...]),
                stream="rout", after=radd_toks, tag="rout")
            if DBG:
                t_rw2b = p.dma("sync", lambda s, r0=rows0: s.dma_start(
                    out=or_d[r0:r0 + 512, :].rearrange("(t p) d -> p t d", p=128), in_=rbuf[...]),
                    stream="rout", after=radd_toks, tag="routdbg")
            mu_toks = []
            for tt2 in range(4):
                mu_toks.append(p.do("vector", lambda v, t2=tt2: v.tensor_reduce(
                    mus[:, t2:t2 + 1], rbuf[:, t2, :].rearrange("p d -> p () d"),
                    axis=AX.X, op=OP.add), after=radd_toks + [t_prev_stats], sig=True, tag="mu2"))
                mu_toks.append(p.do("scalar", lambda s, t2=tt2: s.activation(
                    out=sq[:, :], in_=rbuf[:, t2, :], func=AF.Square,
                    accum_out=ssq[:, t2:t2 + 1]), after=radd_toks + [t_prev_stats], sig=True, tag="ssq2"))
            tv = p.do("vector", lambda v: v.tensor_scalar(out=mus[:, :], in0=mus[:, :], scalar1=1.0 / DM,
                                                          scalar2=None, op0=OP.mult), after=mu_toks, sig=True)
            tv2 = p.do("vector", lambda v: (v.tensor_scalar(out=veps0[:, :], in0=ssq[:, :], scalar1=1.0 / DM,
                                                            scalar2=EPS, op0=OP.mult, op1=OP.add),
                                            v.tensor_mul(mm2[:, :], mus[:, :], mus[:, :]))[-1],
                       after=[tv, t_prev_sqrt], sig=True)
            tv2 = p.do("vector", lambda v: v.tensor_sub(veps[:, :], veps0[:, :], mm2[:, :]), after=[tv2], sig=True)
            tsc = p.do("scalar", lambda s: s.activation(out=sdv[:, :], in_=veps[:, :], func=AF.Sqrt),
                       after=[tv2], sig=True)
            t_prev_sqrt = tsc
            trc = p.do("vector", lambda v: v.reciprocal(rstd[:, :], sdv[:, :]), after=[tsc], sig=True)
            x2toks = []
            for tt2 in range(4):
                gt_idx = b * 16 + q * 4 + tt2
                rows_t = rows0 + tt2 * 128
                tf = p.do("vector", lambda v, t2=tt2: v.tensor_scalar(
                    out=x2f[:, :], in0=rbuf[:, t2, :], scalar1=mus[:, t2:t2 + 1],
                    scalar2=rstd[:, t2:t2 + 1], op0=OP.subtract, op1=OP.mult),
                    after=[trc, t_x2f_free], sig=True, tag="ln2ap")
                tb2 = p.do("vector", lambda v: v.tensor_copy(out=x2b[:, :], in_=x2f[:, :]),
                           after=[tf, t_x2b_free], sig=True, tag="x2bf")
                t_x2w2 = p.dma("sync", lambda s, r0=rows_t: s.dma_start(
                    out=x2_d[r0:r0 + 128, :].rearrange("(t p) d -> p (t d)", p=128), in_=x2b[:, :]),
                    stream="x2w", after=[tb2], tag="x2w")
                if DBG:
                    t_x2w3 = p.dma("sync", lambda s, r0=rows_t: s.dma_start(
                        out=ox2_d[r0:r0 + 128, :].rearrange("(t p) d -> p (t d)", p=128), in_=x2f[:, :]),
                        stream="x2w", after=[tf], tag="x2wdbg")
                    tb2 = (tb2[0], max(tb2[1], t_x2w3[1])) if tb2[0] == t_x2w3[0] else tb2
                t_x2b_free = t_x2w2
                rcp = []
                for half in range(2):
                    bk = (pf4, pf5)[half]
                    t1 = p.do("tensor", lambda te, hh=half, bk2=bk: [te.transpose(
                        bk2[:, c2 * 128:(c2 + 1) * 128], x2f[:, (hh * 4 + c2) * 128:(hh * 4 + c2 + 1) * 128],
                        identf[:, :]) for c2 in range(4)][-1],
                        after=[tf, pf_free[id(bk)], t_consts, t_x2T_free], sig=True, tag="x2tr")
                    t2 = p.do("vector", lambda v, hh=half, bk2=bk: v.tensor_copy(
                        out=x2T.ap()[:, hh * 4:(hh + 1) * 4, :],
                        in_=bk2.ap().rearrange("p (c t) -> p c t", c=4)),
                        after=[t1], sig=True, tag="x2cp")
                    pf_free[id(bk)] = t2
                    rcp.append(t2)
                t_x2f_free = rcp[-1]
                bkr = (pf2, pf3)[tt2 % 2]
                tr2 = p.do("tensor", lambda te, bk=bkr: [te.matmul(
                    bk[:, 0:8], x2T[:, dc, :], rwT[:, dc, :], start=(dc == 0), stop=(dc == 7))
                    for dc in range(8)][-1],
                    after=rcp + [t_rw, pf_free[id(bkr)]], sig=True, tag="routmm")
                t_x2T_free = tr2
                tl = p.do("vector", lambda v, g2=gt_idx, bk=bkr: v.tensor_copy(
                    out=logits[:, g2 * 8:(g2 + 1) * 8], in_=bk[:, 0:8]),
                    after=[tr2], sig=True, tag="logcp")
                pf_free[id(bkr)] = tl
                tok_logits.append(tl)
                x2toks.append(tl)
            t_rbuf_free = x2toks[-1]
        if b == 0:
            wA_t = load_wslot(wslotA, [wdt, wlnz], "wA", [p.etok("tensor")])

    t_bar = [p.etok("vector"), p.etok("scalar"), p.etok("tensor"),
             p.stream_all("rout"), p.stream_all("x2w"), p.stream_all("xre"),
             p.stream_all("x0"), p.stream_all("x1")]

    W3 = lambda x2_: x2_.ap().rearrange("p (t e) -> p t e", e=E)
    bc3 = lambda a2: a2.ap().rearrange("p t -> p t ()").to_broadcast([128, TILES, E])
    t_log = tok_logits[-1]
    tv = p.do("vector", lambda v: v.tensor_reduce(mrow[:, :], W3(logits), axis=AX.X, op=OP.max),
              after=[t_log], sig=True, tag="smax")
    tv = p.do("vector", lambda v: v.tensor_tensor(out=W3(exps), in0=W3(logits), in1=bc3(mrow), op=OP.subtract),
              after=[tv], sig=True)
    te_ = p.do("scalar", lambda s: s.activation(out=exps[:, :], in_=exps[:, :], func=AF.Exp), after=[tv], sig=True)
    tv = p.do("vector", lambda v: v.tensor_reduce(zrow[:, :], W3(exps), axis=AX.X, op=OP.add), after=[te_], sig=True)
    tv = p.do("vector", lambda v: v.reciprocal(rz[:, :], zrow[:, :]), after=[tv], sig=True)
    tv = p.do("vector", lambda v: v.tensor_tensor(out=W3(prob), in0=W3(exps), in1=bc3(rz), op=OP.mult),
              after=[tv], sig=True)
    tv = p.do("vector", lambda v: v.tensor_reduce(mxp[:, :], W3(prob), axis=AX.X, op=OP.max), after=[tv], sig=True)
    tv = p.do("vector", lambda v: v.tensor_tensor(out=W3(igx), in0=W3(prob), in1=bc3(mxp), op=OP.is_ge),
              after=[tv], sig=True)
    tv = p.do("vector", lambda v: v.scalar_tensor_tensor(out=eqm[:, :], in0=igx[:, :], scalar=-1000.0,
                                                         in1=ioe[:, :], op0=OP.mult, op1=OP.add),
              after=[tv, t_consts], sig=True)
    tv = p.do("vector", lambda v: v.tensor_reduce(idxm[:, :], W3(eqm), axis=AX.X, op=OP.min), after=[tv], sig=True)
    t_idx = p.do("vector", lambda v: v.tensor_scalar(out=idxa[:, :], in0=idxm[:, :], scalar1=1000.0,
                                                     scalar2=None, op0=OP.add), after=[tv], sig=True)
    t_oh = p.do("vector", lambda v: v.tensor_tensor(out=W3(ohb), in0=W3(ioe), in1=bc3(idxa), op=OP.is_equal),
                after=[t_idx], sig=True)
    t_ohf = p.do("vector", lambda v: v.tensor_tensor(out=W3(ohf), in0=W3(ioe), in1=bc3(idxa), op=OP.is_equal),
                 after=[t_idx], sig=True)
    t_cnt = p.do("tensor", lambda t: t.matmul(pf2[:, 0:256], triu[:, :], ohb[:, :], start=True, stop=True),
                 after=[t_oh, t_consts, pf_free[id(pf2)]], sig=True, tag="cntmm")
    t_totm = p.do("tensor", lambda t: t.matmul(pf3[0:1, 0:256], on128[:, :], ohb[:, :], start=True, stop=True),
                  after=[t_oh, t_consts, pf_free[id(pf3)]], sig=True, tag="totmm")
    t_tot = p.do("vector", lambda v: v.tensor_copy(out=tot[:, :], in_=pf3[0:1, 0:256]), after=[t_totm], sig=True)
    pf_free[id(pf3)] = t_tot
    tv = p.do("vector", lambda v: v.memset(offa[:, :], 0.0), after=[t_tot], sig=True)
    tv = p.do("vector", lambda v: v.tensor_copy(out=offa[0:1, E:], in_=tot[0:1, 0:TE - E]), after=[tv], sig=True)
    cur, nxt = offa, offb
    d = E
    while d < TE:
        tv = p.do("vector", lambda v, c=cur, n=nxt, dd=d: (
            v.tensor_copy(out=n[0:1, 0:dd], in_=c[0:1, 0:dd]),
            v.tensor_add(n[0:1, dd:], c[0:1, dd:], c[0:1, 0:TE - dd]))[-1], after=[tv], sig=True)
        cur, nxt = nxt, cur
        d *= 2
    offfin = cur
    t_pre = tv
    t_cnts = p.do("vector", lambda v: v.tensor_reduce(cnts[:, :], tot.ap().rearrange("a (t e) -> a e t", e=E),
                                                      axis=AX.X, op=OP.add), after=[t_tot], sig=True, tag="cnts")
    t_bcm = p.do("tensor", lambda t: t.matmul(pf4[:, 0:256], onr[0:1, :], offfin[0:1, 0:256],
                                              start=True, stop=True),
                 after=[t_pre, t_consts, pf_free[id(pf4)]], sig=True, tag="bcmm")
    tv = p.do("vector", lambda v: v.tensor_copy(out=offbc[:, :], in_=pf4[:, 0:256]), after=[t_bcm], sig=True)
    pf_free[id(pf4)] = tv
    tv = p.do("vector", lambda v: v.tensor_tensor(out=rt1[:, :], in0=pf2[:, 0:256], in1=offbc[:, :], op=OP.add),
              after=[tv, t_cnt], sig=True)
    pf_free[id(pf2)] = tv
    tv = p.do("vector", lambda v: v.tensor_mul(rt2[:, :], rt1[:, :], ohf[:, :]), after=[tv, t_ohf], sig=True)
    tv = p.do("vector", lambda v: v.tensor_reduce(pos0[:, :], W3(rt2), axis=AX.X, op=OP.add), after=[tv], sig=True)
    t_pos = p.do("vector", lambda v: v.tensor_scalar(out=pos[:, :], in0=pos0[:, :], scalar1=-1.0,
                                                     scalar2=None, op0=OP.add), after=[tv], sig=True, tag="pos")
    tv = p.do("vector", lambda v: (v.scalar_tensor_tensor(out=slot[:, :], in0=idxa[:, :], scalar=float(CL),
                                                          in1=pos[:, :], op0=OP.mult, op1=OP.add),
                                   v.tensor_scalar(out=ovf[:, :], in0=pos[:, :], scalar1=float(CL),
                                                   scalar2=None, op0=OP.is_ge))[-1], after=[t_pos], sig=True)
    tv = p.do("vector", lambda v: v.scalar_tensor_tensor(out=slot[:, :], in0=ovf[:, :], scalar=SENT,
                                                         in1=slot[:, :], op0=OP.mult, op1=OP.add),
              after=[tv], sig=True)
    tv = p.do("vector", lambda v: v.tensor_copy(out=sloti[:, :], in_=slot[:, :]), after=[tv], sig=True)
    tv = p.do("vector", lambda v: (v.tensor_scalar(out=mgi[:, :], in0=sloti[:, :], scalar1=127, scalar2=None,
                                                   op0=OP.bitwise_and),
                                   v.tensor_scalar(out=dvi[:, :], in0=sloti[:, :], scalar1=7, scalar2=None,
                                                   op0=OP.arith_shift_right))[-1], after=[tv], sig=True)
    tv = p.do("vector", lambda v: v.tensor_scalar(out=mgi[:, :], in0=mgi[:, :], scalar1=NT, scalar2=None,
                                                  op0=OP.mult), after=[tv], sig=True)
    t_tgt = p.do("vector", lambda v: v.tensor_tensor(out=tgt[:, :], in0=mgi[:, :], in1=dvi[:, :], op=OP.add),
                 after=[tv], sig=True, tag="tgt")
    t_pgv = p.do("vector", lambda v: (v.tensor_copy(out=pgv[:, 0:TILES * 2:2], in_=tokidf[:, :]),
                                      v.tensor_copy(out=pgv[:, 1:TILES * 2:2], in_=mxp[:, :]))[-1],
                 after=[t_consts, tok_logits[-1], (lambda: None, 0) if False else ("vector", p.chan["vector"][1])], sig=True, tag="pgv")
    for t in range(TILES):
        p.dma("gpsimd", lambda g, tt2=t: g.indirect_dma_start(
            out=pg_d.ap(), out_offset=IndirectOffsetOnAxis(tgt[:, tt2:tt2 + 1], 0),
            in_=pgv[:, 2 * tt2:2 * tt2 + 2], in_offset=None, bounds_check=reg_ns, oob_is_err=False),
            stream="pack0", after=[t_tgt, t_pgv, t_pfill, t_regs], tag="scat")
    sc_all = p.stream_all("pack0")
    p.dma("gpsimd", lambda g: g.dma_start(out=sentf[:, :], in_=pg_d.ap().rearrange("(a b) c -> a (b c)", a=128)),
          stream="rb", after=[sc_all], tag="pgrb")
    t_rb0 = p.stream_all("rb")
    t_rb1 = p.do("vector", lambda v: v.tensor_copy(out=ptok[:, :], in_=sentf[:, 0:NT * 2:2]),
                 after=[t_rb0], sig=True, tag="ptokcast")
    t_rb2 = p.do("vector", lambda v: v.tensor_copy(out=gsb[:, :], in_=sentf[:, 1:NT * 2:2]),
                 after=[t_rb0], sig=True, tag="gcast")
    t_rb = ("vector", max(t_rb1[1], t_rb2[1]))
    t_m = p.do("tensor", lambda t: t.matmul(pf5[0:8, 0:8], cnts[0:1, :], sel[0:1, :], start=True, stop=True),
               after=[t_cnts, t_consts, pf_free[id(pf5)]], sig=True, tag="ccmm")
    t_msb = p.do("vector", lambda v: v.tensor_copy(out=msb[:, :], in_=pf5[0:8, 0:8]), after=[t_m], sig=True)
    pf_free[id(pf5)] = t_msb
    t_ccw = p.dma("gpsimd", lambda g: g.dma_start(out=cc_i[:, :], in_=msb[:, :]), stream="cc", after=[t_msb])
    t_cc = p.do("gpsimd", lambda g: g.collective_compute("AllReduce", OP.add, replica_groups=[list(range(NCORES))],
                                                         ins=[cc_i.ap().opt()], outs=[cc_o.ap().opt()]),
                after=[t_ccw], sig=True, tag="cc")
    t_ccr = p.dma("gpsimd", lambda g: g.dma_start(out=ccs[:, :], in_=cc_o[:, :]), stream="cc", after=[t_cc])
    tv2 = p.do("vector", lambda v: v.tensor_mul(ccm[:, :], ccs[:, :], prv[:, :]), after=[t_ccr, t_consts], sig=True)
    tv2 = p.do("vector", lambda v: v.tensor_reduce(offcol[:, :], ccm[:, :], axis=AX.X, op=OP.add), after=[tv2], sig=True)
    t_row = p.do("tensor", lambda t: t.matmul(pf5[0:1, 0:8], offcol[:, :], id8[:, :], start=True, stop=True),
                 after=[tv2, t_consts, pf_free[id(pf5)]], sig=True)
    t_orow = p.do("vector", lambda v: v.tensor_copy(out=offrow[:, :], in_=pf5[0:1, 0:8]), after=[t_row], sig=True)
    pf_free[id(pf5)] = t_orow
    t_thrm = p.do("tensor", lambda t: t.matmul(pf5[:, 0:8], onr[0:1, :], offrow[0:1, :], start=True, stop=True),
                  after=[t_orow, pf_free[id(pf5)]], sig=True)
    t_thr = p.do("vector", lambda v: v.tensor_scalar(out=thr[:, :], in0=pf5[:, 0:8], scalar1=-1.0,
                                                     scalar2=float(CCAP), op0=OP.mult, op1=OP.add),
                 after=[t_thrm], sig=True, tag="thr")
    pf_free[id(pf5)] = t_thr

    def gmask(v):
        last = None
        for e in range(E):
            cols = slice(e * CPT, (e + 1) * CPT)
            last = v.tensor_tensor(out=gm[:, cols], in0=slotp[:, cols],
                                   in1=thr[:, e:e + 1].to_broadcast([128, CPT]), op=OP.is_lt)
        return last
    t_gm = p.do("vector", gmask, after=[t_thr, t_consts], sig=True, tag="gm")
    t_gfix = p.do("vector", lambda v: v.tensor_mul(gsb2[:, :], gsb[:, :], gm[:, :]),
                  after=[t_gm, t_rb], sig=True, tag="gfix")

    t_x2w = p.stream_all("x2w")
    t_gz = p.do("vector", lambda v: v.memset(gtb[:, :, :], 0.0), after=t_bar, sig=True, tag="gz")
    xeT_free = [t_bar[2], t_bar[2]]
    w1_free = [t_bar[2], t_bar[2]]
    eo_free = [t_bar[0], t_bar[0]]
    xeT_tok = [None, None]
    t_rout_all = p.stream_all("rout")

    pf_state = {}

    def prefetch(e):
        par = e % 2
        st = {}
        ga = []
        for t in range(CPT):
            ga.append(p.dma("gpsimd", lambda g, tt2=t, ee=e: g.indirect_dma_start(
                out=gtb[:, tt2, :], out_offset=None, in_=x2_d.ap(),
                in_offset=IndirectOffsetOnAxis(ptok[:, ee * CPT + tt2:ee * CPT + tt2 + 1], 0),
                bounds_check=reg_nt, oob_is_err=False),
                stream=f"gx{par}", after=[t_rb, t_gz, t_x2w, pf_state.get("gt_free", T_ALL), t_regs], tag="gx"))
        st["ga_all"] = p.stream_all(f"gx{par}")
        st["w1_pre"] = {}
        for fw in (0, 1):
            st["w1_pre"][fw] = p.dma("gpsimd", lambda g, ee=e, ff=fw: g.dma_start(
                out=w1win[ff % 2][...], in_=w1_d[ee, ff].rearrange("p (a c) -> p a c", a=8)),
                stream=f"w1{fw % 2}", after=[w1_free[fw % 2]], tag="w1pre")
        return st

    def prefetch_w2(e, st):
        st["b2"] = p.dma("gpsimd", lambda g, ee=e: g.dma_start(out=b2bf[:, :], in_=b2_d[ee:ee + 1, :]),
                         stream="b2", after=[pf_state.get("prev_eo", t_bar[2])], tag="b2")
        st["w2"] = p.dma("gpsimd", lambda g, ee=e: g.dma_start(
            out=w2sb[...], in_=w2_d[ee].rearrange("p (a c) -> p a c", a=32)),
            stream="w2", after=[pf_state.get("prev_eo", t_bar[2])], tag="w2")

    def compute_trans(e, st):
        par = e % 2
        trs = []
        for t in range(CPT):
            bank = (pb0, pb1)[t % 2]
            t1 = p.do("tensor", lambda te, tt2=t, bk=bank: [te.transpose(
                bk[:, c * 128:(c + 1) * 128], gtb[:, tt2, c * 128:(c + 1) * 128], identb[:, :])
                for c in range(8)][-1],
                after=[st["ga_all"], pb_free[id(bank)], t_consts, xeT_free[par]], sig=True, tag="xetr")
            t2 = p.do("vector", lambda v, tt2=t, bk=bank, pp=par: v.tensor_copy(
                out=xeT[pp].ap()[:, :, tt2 * 128:(tt2 + 1) * 128],
                in_=bk.ap().rearrange("p (c t) -> p c t", c=8)),
                after=[t1], sig=True, tag="xecp")
            pb_free[id(bank)] = t2
            trs.append(t2)
        pf_state["gt_free"] = trs[-1]
        xeT_tok[par] = trs[-1]

    def compute_h(e, st):
        par = e % 2
        h_toks = []
        for fw in range(8):
            wpar = fw % 2
            if fw in st["w1_pre"]:
                t_w1 = st["w1_pre"][fw]
            else:
                t_w1 = p.dma("gpsimd", lambda g, ee=e, ff=fw: g.dma_start(
                    out=w1win[ff % 2][...], in_=w1_d[ee, ff].rearrange("p (a c) -> p a c", a=8)),
                    stream=f"w1{wpar}", after=[w1_free[wpar]], tag="w1")
            for fc in range(4):
                for ci, (c0, cn) in enumerate([(0, 512), (512, CL - 512)]):
                    bank = (pf2, pf3)[(fc * 2 + ci) % 2]
                    tm = p.do("tensor", lambda te, pp=wpar, f2=fc, cc0=c0, ccn=cn, bk=bank, pr=par: [te.matmul(
                        bk[:, 0:ccn], w1win[pp][:, dc, f2 * 128:(f2 + 1) * 128],
                        xeT[pr][:, dc, cc0:cc0 + ccn], start=(dc == 0), stop=(dc == 7))
                        for dc in range(8)][-1],
                        after=[t_w1, xeT_tok[par], pf_free[id(bank)]], sig=True, tag="hmm")
                    th = p.do("scalar", lambda s, ff=fw, f2=fc, cc0=c0, ccn=cn, bk=bank, ee=e: s.activation(
                        out=h_bf[:, ff * 4 + f2, cc0:cc0 + ccn], in_=bk[:, 0:ccn], func=AF.Gelu,
                        bias=b1r[:, ee * 32 + ff * 4 + f2:ee * 32 + ff * 4 + f2 + 1], scale=1.0),
                        after=[tm, t_consts], sig=True, tag="gelu")
                    pf_free[id(bank)] = th
                    h_toks.append(th)
            w1_free[wpar] = h_toks[-1]
        return h_toks[-1]

    def compute_eo(e, st, t_hlast):
        par = e % 2
        last_mm = None
        for tt2 in range(CPT):
            epar = tt2 % 2
            evs = []
            for dh in range(2):
                bank = (pf0, pf1)[dh]
                def eomm(te, t2=tt2, dd=dh, bk=bank):
                    te.matmul(bk[:, :], onrb[0:1, :], b2bf[0:1, dd * 512:(dd + 1) * 512],
                              start=True, stop=False)
                    last = None
                    for fc in range(32):
                        last = te.matmul(bk[:, :], h_bf[:, fc, t2 * 128:(t2 + 1) * 128],
                                         w2sb[:, fc, dd * 512:(dd + 1) * 512],
                                         start=False, stop=(fc == 31))
                    return last
                tm = p.do("tensor", eomm,
                          after=[st["w2"], st["b2"], pf_free[id(bank)], eo_free[epar], t_hlast, t_consts],
                          sig=True, tag="eomm")
                last_mm = tm
                tev = p.do("vector", lambda v, t2=tt2, dd=dh, bk=bank, ee=e, pp=epar: v.tensor_scalar(
                    out=eob[pp][:, dd * 512:(dd + 1) * 512], in0=bk[:, :],
                    scalar1=gsb2[:, ee * CPT + t2:ee * CPT + t2 + 1], scalar2=None, op0=OP.mult),
                    after=[tm, t_gfix], sig=True, tag="eoev")
                pf_free[id(bank)] = tev
                evs.append(tev)
            t_sa = p.dma("gpsimd", lambda g, t2=tt2, ee=e, pp=epar: g.indirect_dma_start(
                out=out_d.ap(), out_offset=IndirectOffsetOnAxis(ptok[:, ee * CPT + t2:ee * CPT + t2 + 1], 0),
                in_=eob[pp][:, :], in_offset=None, bounds_check=reg_nt, oob_is_err=False,
                compute_op=OP.add), stream="sa",
                after=evs + [t_rb, t_rout_all], tag="sa")
            eo_free[epar] = t_sa
        pf_state["prev_eo"] = last_mm
        xeT_free[par] = last_mm

    st_cur = prefetch(0)
    prefetch_w2(0, st_cur)
    for e in range(E):
        compute_trans(e, st_cur)
        t_hlast = compute_h(e, st_cur)
        st_next = prefetch(e + 1) if e + 1 < E else None
        compute_eo(e, st_cur, t_hlast)
        if st_next is not None:
            prefetch_w2(e + 1, st_next)
        st_cur = st_next

    t_sa_all = p.stream_all("sa")
    dbg_all = p.stream_all("dbg") if DBG else (None, 0)
    dbgg_all = p.stream_all("dbgg") if DBG else (None, 0)
    p.final_waits("gpsimd", [t_sa_all, t_rout_all, dbg_all, dbgg_all])
    p.final_waits("sync", [t_sa_all, t_rout_all])
    p.verify()
    with nc.Block() as block:
        p.build(block)
    ctx.close()
    return nc


_NC_CACHE = None


def kernel(**inputs):
    global _NC_CACHE
    xs = np.asarray(inputs["L0_attn_x"], np.float32).reshape(NCORES, NTOK, DM)
    xrs = np.ascontiguousarray(xs.reshape(NCORES, TILES, 128, DM).transpose(0, 2, 1, 3))
    def wslotr(w):
        return np.ascontiguousarray(
            np.asarray(w, np.float32).T.reshape(8, 128, DM).transpose(1, 0, 2).reshape(128, 8 * DM))
    wlnzT = wslotr(inputs["L0_attn_w_ln_z"])
    wdtT = wslotr(inputs["L0_attn_w_dt"])
    wyT = wslotr(inputs["L0_attn_w_y"])
    wyaT = wslotr(inputs["L0_attn_w_ya"])
    routerT = np.ascontiguousarray(np.asarray(inputs["L0_ffn_router_w"], np.float32).T)
    arr = lambda v: np.ascontiguousarray(np.asarray(v, np.float32))
    col8 = lambda v: np.ascontiguousarray(np.asarray(v, np.float32).reshape(8, 128).T)
    blnz = col8(inputs["L0_attn_b_ln_z"]); bdt = col8(inputs["L0_attn_b_dt"])
    by = col8(inputs["L0_attn_b_y"]); bya = col8(inputs["L0_attn_b_ya"])
    h0r = col8(inputs["L0_attn_h0"])
    w1 = np.ascontiguousarray(np.asarray(inputs["L0_ffn_w1"], np.float32)
                              .reshape(E, 8, 128, 8, 512).transpose(0, 3, 2, 1, 4).reshape(E, 8, 128, 4096))
    w2 = np.ascontiguousarray(np.asarray(inputs["L0_ffn_w2"], np.float32)
                              .reshape(E, 32, 128, DM).transpose(0, 2, 1, 3).reshape(E, 128, 32 * DM))
    b1r = np.ascontiguousarray(np.asarray(inputs["L0_ffn_b1"], np.float32).reshape(E, 32, 128).transpose(2, 0, 1).reshape(128, E * 32))
    b2n = arr(inputs["L0_ffn_b2"])

    identb = np.eye(128, dtype=np.float32).astype(bfd)
    identf = np.eye(128, dtype=np.float32)
    triu = np.triu(np.ones((128, 128), np.float32)).astype(bfd)
    ones128 = np.ones((128, 1), np.float32).astype(bfd)
    onesrow = np.ones((1, 128), np.float32)
    onesrowb = onesrow.astype(bfd)
    iotae = np.tile(np.arange(E, dtype=np.float32), (128, TILES))
    tokidi = (np.arange(TILES)[None, :] * 128 + np.arange(128)[:, None]).astype(np.int32)
    tokidf = tokidi.astype(np.float32)
    slotpos = np.zeros((128, NT), np.float32)
    for e in range(E):
        for t in range(CPT):
            slotpos[:, e * CPT + t] = t * 128 + np.arange(128)
    ident8 = np.eye(8, dtype=np.float32)

    if _NC_CACHE is None:
        _NC_CACHE = build_program()
    nc = _NC_CACHE

    in_maps = []
    for k in range(NCORES):
        selrow = np.zeros((1, 8), np.float32); selrow[0, k] = 1.0
        prevmask = np.zeros((8, 8), np.float32); prevmask[:, :k] = 1.0
        in_maps.append({
            "x": xs[k], "xr": xrs[k], "wlnzT": wlnzT, "wdtT": wdtT, "wyT": wyT, "wyaT": wyaT,
            "blnz": blnz, "bdt": bdt, "by": by, "bya": bya, "h0r": h0r,
            "routerT": routerT, "w1r": w1, "b1r": b1r, "w2r": w2, "b2n": b2n,
            "identb": identb, "identf": identf, "triu": triu,
            "ones128": ones128, "onesrow": onesrow, "onesrowb": onesrowb,
            "iotae": iotae, "tokidi": tokidi, "tokidf": tokidf, "slotpos": slotpos,
            "selrow": selrow, "prevmask": prevmask, "ident8": ident8,
        })
    res = run_bass_kernel_spmd(nc, in_maps, core_ids=list(range(NCORES)))
    out = np.concatenate([np.asarray(res.results[k]["out"], np.float32)[None] for k in range(NCORES)], 0)
    return out.reshape(16, 2048, 1024)


# version 08
# speedup vs baseline: 1.0425x; 1.0425x over previous
import numpy as np
import ml_dtypes
DBG = False
import concourse.bass as bass
import concourse.mybir as mybir
from concourse.bass_utils import run_bass_kernel_spmd
from concourse.bass import IndirectOffsetOnAxis
from contextlib import ExitStack

F32 = mybir.dt.float32
BF16 = mybir.dt.bfloat16
I32 = mybir.dt.int32
AF = mybir.ActivationFunctionType
OP = mybir.AluOpType
AX = mybir.AxisListType
bfd = ml_dtypes.bfloat16

NCORES = 8
BPC = 2
L = 2048
DM = 1024
DF = 4096
E = 8
CL = 768
NTOK = BPC * L
TILES = NTOK // 128
NS = E * CL
NT = NS // 128
CPT = CL // 128
CCAP = 6144
SENT = float(1 << 21)
EPS = 1e-5

ENGINES = ("sync", "vector", "scalar", "tensor", "gpsimd")
QCH = {"sync": "qs", "scalar": "qa", "gpsimd": "qg", "vector": "qv", "tensor": "qt"}


class Prog:
    def __init__(self, nc, ctx):
        self.nc = nc
        self.ctx = ctx
        self.chan = {}
        for ch in list(ENGINES) + list(QCH.values()):
            self.chan[ch] = [ctx.enter_context(nc.semaphore("sem_" + ch)), 0]
        self.steps = {e: [] for e in ENGINES}
        self.watermark = {e: {} for e in ENGINES}

    def stream(self, name):
        ch = "#" + name
        if ch not in self.chan:
            self.chan[ch] = [self.ctx.enter_context(self.nc.semaphore("s" + name)), 0]
        return ch

    def _queue(self, engine, fn, after, sem_ch, inc, tag):
        waits = {}
        for c, v in after:
            if v > 0 and waits.get(c, 0) < v:
                waits[c] = v
        tok = None
        if sem_ch is not None:
            sem, cnt = self.chan[sem_ch]
            cnt += inc
            self.chan[sem_ch][1] = cnt
            tok = (sem_ch, cnt)
        wm = self.watermark[engine]
        needed = []
        for c, v in sorted(waits.items()):
            if wm.get(c, 0) < v:
                needed.append((c, v))
                wm[c] = v
        self.steps[engine].append(
            (needed, fn, self.chan[sem_ch][0] if tok else None, inc if tok else 0, tag))
        return tok

    def do(self, engine, fn, after=(), sig=False, tag=""):
        return self._queue(engine, fn, after, engine if sig else None, 1, tag)

    def dma(self, engine, fn, stream, after=(), tag=""):
        return self._queue(engine, fn, after, self.stream(stream), 16, tag)

    def stream_all(self, name):
        ch = "#" + name
        return (ch, self.chan[ch][1])

    def etok(self, engine):
        return (engine, self.chan[engine][1])

    def final_waits(self, engine, tokens):
        waits = {}
        for ch, v in tokens:
            if ch is None or v <= 0:
                continue
            waits[ch] = max(waits.get(ch, 0), v)
        self.steps[engine].append((sorted(waits.items()), lambda e: None, None, 0, "final"))

    def verify(self):
        pos = {e: 0 for e in ENGINES}
        counts = {ch: 0 for ch in self.chan}
        sem2ch = {id(h): ch for ch, (h, _) in self.chan.items()}
        progressed = True
        while progressed:
            progressed = False
            for e in ENGINES:
                while pos[e] < len(self.steps[e]):
                    needed, fn, sem, inc, tag = self.steps[e][pos[e]]
                    if all(counts[ch] >= v for ch, v in needed):
                        if inc:
                            counts[sem2ch[id(sem)]] += inc
                        pos[e] += 1
                        progressed = True
                    else:
                        break
        stuck = [(e, pos[e], self.steps[e][pos[e]][0], self.steps[e][pos[e]][4])
                 for e in ENGINES if pos[e] < len(self.steps[e])]
        if stuck:
            msg = "; ".join(f"{e}@{i} tag={t!r} needs {n}" for e, i, n, t in stuck)
            raise RuntimeError("schedule deadlock: " + msg)

    def build(self, block):
        chans = self.chan

        def emit_for(name):
            def run(eng):
                for needed, fn, sem, inc, tag in self.steps[name]:
                    for ch, v in needed:
                        eng.wait_ge(chans[ch][0], v)
                    inst = fn(eng)
                    if inc:
                        assert inst is not None, f"sig step returned None: {tag}"
                        inst.then_inc(sem, inc)
            return run

        for name in ENGINES:
            if self.steps[name]:
                getattr(block, name)(emit_for(name))



class View:
    def __init__(self, ap_):
        self._ap = ap_
    def __getitem__(self, k):
        return self._ap[k]
    def ap(self):
        return self._ap


def build_program():
    nc = bass.Bass("TRN2", target_bir_lowering=False, detect_race_conditions=False)
    DI = lambda nm, shp, dt=F32: nc.dram_tensor(nm, shp, dt, kind="ExternalInput")

    x_d = DI("x", [NTOK, DM])
    xr_d = DI("xr", [128, TILES, DM])
    wlnz = DI("wlnzT", [128, 8 * DM]);  wdt = DI("wdtT", [128, 8 * DM])
    wy = DI("wyT", [128, 8 * DM]);      wya = DI("wyaT", [128, 8 * DM])
    blnz_d = DI("blnz", [128, 8]); bdt_d = DI("bdt", [128, 8])
    by_d = DI("by", [128, 8]);     bya_d = DI("bya", [128, 8])
    h0_d = DI("h0r", [128, 8])
    rw_d = DI("routerT", [DM, E])
    w1_d = DI("w1r", [E, 8, 128, 4096]);  b1_d = DI("b1r", [128, E * 32])
    w2_d = DI("w2r", [E, 128, 32 * DM]);  b2_d = DI("b2n", [E, DM])
    identb_d = DI("identb", [128, 128], BF16)
    identf_d = DI("identf", [128, 128])
    triu_d = DI("triu", [128, 128], BF16)
    on128_d = DI("ones128", [128, 1], BF16)
    onr_d = DI("onesrow", [1, 128])
    onrb_d = DI("onesrowb", [1, 128], BF16)
    ioe_d = DI("iotae", [128, TILES * E])
    tokid_d = DI("tokidi", [128, TILES], I32)
    tokidf_d = DI("tokidf", [128, TILES])
    slotp_d = DI("slotpos", [128, NT])
    sel_d = DI("selrow", [1, 8])
    prev_d = DI("prevmask", [8, 8])
    id8_d = DI("ident8", [8, 8])

    out_d = nc.dram_tensor("out", [NTOK, DM], F32, kind="ExternalOutput")
    if DBG:
        or_d = nc.dram_tensor("o_r", [NTOK, DM], F32, kind="ExternalOutput")
        oh_d = nc.dram_tensor("o_hT", [BPC, 128, 8 * L], F32, kind="ExternalOutput")
        ox1_d = nc.dram_tensor("o_x1T", [BPC, 128, 8 * L], F32, kind="ExternalOutput")
        olg_d = nc.dram_tensor("o_lg", [128, TILES * E], F32, kind="ExternalOutput")
        ox2_d = nc.dram_tensor("o_x2", [NTOK, DM], F32, kind="ExternalOutput")
        og_d = nc.dram_tensor("o_gs", [128, NT], F32, kind="ExternalOutput")
        opt_d = nc.dram_tensor("o_pt", [128, NT], F32, kind="ExternalOutput")
    x2_d = nc.dram_tensor("x2sc", [NTOK, DM], BF16)
    pg_d = nc.dram_tensor("pgd", [NS, 2], F32)
    w2bf_d = nc.dram_tensor("w2bfd", [E, 128, 32 * DM], BF16)
    cc_i = nc.dram_tensor("cc_i", [8, 8], F32)
    cc_o = nc.dram_tensor("cc_o", [8, 8], F32)

    ctx = ExitStack()
    S = lambda nm, shp, dt=F32: ctx.enter_context(nc.sbuf_tensor(nm, shp, dt))

    identb = S("s_identb", [128, 128], BF16)
    identf = S("s_identf", [128, 128])
    triu = S("s_triu", [128, 128], BF16)
    on128 = S("s_on128", [128, 1], BF16)
    onr = S("s_onr", [1, 128])
    onrb = S("s_onrb", [1, 128], BF16)
    ioe = S("s_ioe", [128, TILES * E])
    tokidi = S("s_tokidi", [128, TILES], I32)
    tokidf = S("s_tokidf", [128, TILES])
    slotp = S("s_slotp", [128, NT])
    sel = S("s_sel", [1, 8]); prv = S("s_prv", [8, 8]); id8 = S("s_id8", [8, 8])
    blnz = S("s_blnz", [128, 8]); bdt = S("s_bdt", [128, 8]); nbdt = S("s_nbdt", [128, 8])
    by = S("s_by", [128, 8]); bya = S("s_bya", [128, 8]); h0r = S("s_h0r", [128, 8])
    rwT = S("s_rwT", [128, 8, E])
    b1r = S("s_b1r", [128, E * 32])
    b2bf = S("s_b2bf", [1, 1024], BF16)
    logits = S("s_logits", [128, TILES * E])
    TE = TILES * E
    mrow = S("s_mrow", [128, TILES]); exps = S("s_exps", [128, TE])
    zrow = S("s_zrow", [128, TILES]); rz = S("s_rz", [128, TILES])
    prob = S("s_prob", [128, TE]); mxp = S("s_mxp", [128, TILES])
    igx = S("s_igx", [128, TE]); eqm = S("s_eqm", [128, TE])
    idxm = S("s_idxm", [128, TILES]); idxa = S("s_idxa", [128, TILES])
    ohb = S("s_ohb", [128, TE], BF16); ohf = S("s_ohf", [128, TE])
    tot = S("s_tot", [1, TE]); offa = S("s_offa", [1, TE]); offb = S("s_offb", [1, TE])
    offbc = S("s_offbc", [128, TE]); rt1 = S("s_rt1", [128, TE]); rt2 = S("s_rt2", [128, TE])
    pos0 = S("s_pos0", [128, TILES]); pos = S("s_pos", [128, TILES])
    slot = S("s_slot", [128, TILES]); ovf = S("s_ovf", [128, TILES])
    sloti = S("s_sloti", [128, TILES], I32)
    mgi = S("s_mgi", [128, TILES], I32); dvi = S("s_dvi", [128, TILES], I32)
    tgt = S("s_tgt", [128, TILES], I32)
    cnts = S("s_cnts", [1, E])
    sentt = S("s_sentt", [128, NT], I32)
    sentf = S("s_sentf", [128, NT * 2])
    pgv = S("s_pgv", [128, TILES * 2])
    ptok = S("s_ptok", [128, NT], I32)
    gsb = S("s_gsb", [128, NT]); gm = S("s_gm", [128, NT]); gsb2 = S("s_gsb2", [128, NT])
    ccs = S("s_ccs", [8, 8]); ccm = S("s_ccm", [8, 8]); msb = S("s_msb", [8, 8])
    offcol = S("s_offcol", [8, 1]); offrow = S("s_offrow", [1, 8]); thr = S("s_thr", [128, E])
    mus = S("s_mus", [128, 4]); ssq = S("s_ssq", [128, 4])
    veps0 = S("s_veps0", [128, 4]); mm2 = S("s_mm2", [128, 4]); veps = S("s_veps", [128, 4])
    sdv = S("s_sdv", [128, 4]); rstd = S("s_rstd", [128, 4])

    AR = 176 * 1024
    arena = S("s_arena", [128, AR], mybir.dt.uint8)

    def carve(off, shape, dt):
        esz = {F32: 4, BF16: 2, I32: 4}[dt]
        n = 1
        for s in shape[1:]:
            n *= s
        nb = n * esz
        assert off % 4 == 0 and off + nb <= AR, (off, nb)
        ap_ = arena[:, off:off + nb].bitcast(dt)
        if len(shape) == 3:
            ap_ = ap_.rearrange("p (a b) -> p a b", a=shape[1])
        elif len(shape) == 4:
            ap_ = ap_.rearrange("p (a b c) -> p a b c", a=shape[1], b=shape[2])
        return View(ap_), off + nb

    off = 0
    wslotA, off = carve(off, [128, 2, 8, 1024], BF16)
    wslotB = wslotA
    x1T, off = carve(off, [128, 8, L], BF16)
    hT, off = carve(off, [128, 8, L], BF16)
    a_s, off = carve(off, [128, L], F32)
    z_s, off = carve(off, [128, L], F32)
    u_s, off = carve(off, [128, L], F32)
    xre, off = carve(off, [128, 4, 1024], F32)
    xbuf = View(xre.ap().rearrange("p a d -> p (a d)").rearrange("p (a b d) -> p a b d", a=2, b=2))
    rbuf, off = carve(off, [128, 4, 1024], F32)
    x1tm, off = carve(off, [128, 1024], BF16)
    x2f, off = carve(off, [128, 1024], F32)
    x2b, off = carve(off, [128, 1024], BF16)
    x2T, off = carve(off, [128, 8, 128], F32)
    gate, off = carve(off, [128, 512], F32)
    attnf, off = carve(off, [128, 512], F32)
    sq, off = carve(off, [128, 1024], F32)
    A_END = off
    off = 0
    gtb, off = carve(off, [128, CPT, 1024], BF16)
    xeT0, off = carve(off, [128, 8, CL], BF16)
    xeT1, off = carve(off, [128, 8, CL], BF16)
    xeT = [xeT0, xeT1]
    w1w0, off = carve(off, [128, 8, 512], BF16)
    w1w1, off = carve(off, [128, 8, 512], BF16)
    w1win = [w1w0, w1w1]
    w2sb, off = carve(off, [128, 32, 1024], BF16)
    h_bf, off = carve(off, [128, 32, CL], BF16)
    eob0, off = carve(off, [128, 1024], F32)
    eob1, off = carve(off, [128, 1024], F32)
    eob = [eob0, eob1]
    assert off <= A_END, (off, A_END)

    P = lambda nm, shp, dt=F32: ctx.enter_context(nc.psum_tensor(nm, shp, dt))
    pf0 = P("pf0", [128, 512]); pf1 = P("pf1", [128, 512])
    pf2 = P("pf2", [128, 512]); pf3 = P("pf3", [128, 512])
    pf4 = P("pf4", [128, 512]); pf5 = P("pf5", [128, 512])
    pb0 = P("pb0", [128, 1024], BF16); pb1 = P("pb1", [128, 1024], BF16)

    p = Prog(nc, ctx)
    T_ALL = (None, 0)

    reg_ns = nc.gpsimd.alloc_register("bc_ns")
    reg_nt = nc.gpsimd.alloc_register("bc_nt")

    def setregs(g):
        g.reg_mov(reg_ns, NS - 1)
        return g.reg_mov(reg_nt, NTOK - 1)
    t_regs = p.do("gpsimd", setregs, sig=True, tag="regs")

    for nm, dst, src in [("identb", identb, identb_d), ("identf", identf, identf_d),
                         ("triu", triu, triu_d), ("on128", on128, on128_d),
                         ("onr", onr, onr_d), ("onrb", onrb, onrb_d),
                         ("ioe", ioe, ioe_d), ("tokidi", tokidi, tokid_d), ("tokidf", tokidf, tokidf_d),
                         ("slotp", slotp, slotp_d), ("sel", sel, sel_d),
                         ("prv", prv, prev_d), ("id8", id8, id8_d),
                         ("blnz", blnz, blnz_d), ("bdt", bdt, bdt_d),
                         ("by", by, by_d), ("bya", bya, bya_d),
                         ("h0r", h0r, h0_d), ("b1r", b1r, b1_d)]:
        p.dma("sync", lambda s, d=dst, sr=src: s.dma_start(out=d[...], in_=sr[...]),
              stream="consts", tag="ld_" + nm)
    t_rw = p.dma("sync", lambda s: s.dma_start(out=rwT[...], in_=rw_d.ap().rearrange("(a b) c -> b a c", b=128)),
                 stream="consts", tag="ld_rw")
    t_consts = p.stream_all("consts")
    t_nbdt = p.do("vector", lambda v: v.tensor_scalar(out=nbdt[:, :], in0=bdt[:, :], scalar1=-1.0,
                                                      scalar2=None, op0=OP.mult), after=[t_consts], sig=True, tag="nbdt")
    t_sent = p.do("vector", lambda v: v.memset(sentt[:, :], 1 << 20), sig=True, tag="sent")
    t_sf = p.do("vector", lambda v: v.tensor_copy(out=sentf[:, 0:NT * 2:2], in_=sentt[:, :]),
                after=[t_sent], sig=True, tag="sentf")
    t_sf = p.do("vector", lambda v: v.memset(sentf[:, 1:NT * 2:2], 0.0), after=[t_sf], sig=True, tag="sentg")
    t_pfill = p.dma("sync", lambda s: s.dma_start(out=pg_d.ap().rearrange("(a b) c -> a (b c)", a=128),
                                                  in_=sentf[:, :]), stream="pfill", after=[t_sf], tag="pfill")
    t_gfill = t_pfill

    def load_wslot(slot_t, pair, stream, after):
        toks = []
        for i, wsrc in enumerate(pair):
            toks.append(p.dma("gpsimd",
                lambda g, ii=i, ws=wsrc, sl=slot_t: g.dma_start(
                    out=sl[:, ii, :, :], in_=ws.ap().rearrange("p (a c) -> p a c", a=8)),
                stream=stream, after=after, tag=f"w_{stream}"))
        return toks

    wA_t = load_wslot(wslotA, [wdt, wlnz], "wA", [])
    w2cast_t = []
    for e in range(E):
        w2cast_t.append(p.dma("gpsimd", lambda g, ee=e: g.dma_start(
            out=w2bf_d[ee], in_=w2_d[ee]), stream="w2cast", after=wA_t, tag="w2cast"))

    NQ = L // 512
    tok_x1T_done = [None] * BPC
    tok_x1T_q = [[None] * NQ for _ in range(BPC)]
    tok_hT_done = [None] * BPC
    tok_logits = []
    pf_free = {id(b): T_ALL for b in (pf0, pf1, pf2, pf3, pf4, pf5)}
    pb_free = {id(b): T_ALL for b in (pb0, pb1)}
    xb_free = [T_ALL, T_ALL]
    t_prev_stats = T_ALL
    t_prev_sqrt = T_ALL
    t_x1tm_free = T_ALL
    t_x2f_free = T_ALL
    t_x2b_free = T_ALL
    t_x2T_free = T_ALL
    t_gate_free = T_ALL
    t_attnf_free = T_ALL
    t_rbuf_free = T_ALL
    t_xre_free = T_ALL

    for b in range(BPC):
        x1t_toks = []
        for q in range(NQ):
            half_toks = []
            for h2 in range(2):
                buf = h2
                rows0 = b * L + q * 512 + h2 * 256
                tile0 = rows0 // 128
                t_x = p.dma("sync", lambda s, tl=tile0, bb=buf: s.dma_start(
                    out=xbuf[:, bb, :, :], in_=xr_d[:, tl:tl + 2, :]),
                    stream=f"x{buf}", after=[xb_free[buf], t_xre_free], tag=f"xld{b}{q}{h2}")
                half_toks.append((buf, t_x))
            mu_toks = []
            for h2, (buf, t_x) in enumerate(half_toks):
                for tt2 in range(2):
                    col = h2 * 2 + tt2
                    mu_toks.append(p.do("vector", lambda v, bb=buf, t2=tt2, cc=col: v.tensor_reduce(
                        mus[:, cc:cc + 1], xbuf[:, bb, t2, :].rearrange("p d -> p () d"),
                        axis=AX.X, op=OP.add), after=[t_x, t_prev_stats], sig=True, tag="mu"))
                    mu_toks.append(p.do("scalar", lambda s, bb=buf, t2=tt2, cc=col: s.activation(
                        out=sq[:, :], in_=xbuf[:, bb, t2, :], func=AF.Square,
                        accum_out=ssq[:, cc:cc + 1]), after=[t_x, t_prev_stats], sig=True, tag="ssq"))
            tv = p.do("vector", lambda v: v.tensor_scalar(out=mus[:, :], in0=mus[:, :], scalar1=1.0 / DM,
                                                          scalar2=None, op0=OP.mult), after=mu_toks, sig=True)
            tv2 = p.do("vector", lambda v: (v.tensor_scalar(out=veps0[:, :], in0=ssq[:, :], scalar1=1.0 / DM,
                                                            scalar2=EPS, op0=OP.mult, op1=OP.add),
                                            v.tensor_mul(mm2[:, :], mus[:, :], mus[:, :]))[-1],
                       after=[tv, t_prev_sqrt], sig=True)
            tv2 = p.do("vector", lambda v: v.tensor_sub(veps[:, :], veps0[:, :], mm2[:, :]), after=[tv2], sig=True)
            tsc = p.do("scalar", lambda s: s.activation(out=sdv[:, :], in_=veps[:, :], func=AF.Sqrt),
                       after=[tv2], sig=True, tag="sqrt")
            t_prev_sqrt = tsc
            trc = p.do("vector", lambda v: v.reciprocal(rstd[:, :], sdv[:, :]), after=[tsc], sig=True)
            ap_toks = []
            for h2, (buf, t_x) in enumerate(half_toks):
                for tt2 in range(2):
                    col = h2 * 2 + tt2
                    gt_idx = q * 4 + col
                    ta = p.do("vector", lambda v, bb=buf, t2=tt2, cc=col: v.tensor_scalar(
                        out=x1tm[:, :], in0=xbuf[:, bb, t2, :], scalar1=mus[:, cc:cc + 1],
                        scalar2=rstd[:, cc:cc + 1], op0=OP.subtract, op1=OP.mult),
                        after=[trc, t_x, t_x1tm_free], sig=True, tag="lnap")
                    ap_toks.append(ta)
                    bank = (pb0, pb1)[gt_idx % 2]
                    ttr = p.do("tensor", lambda te, bk=bank: [te.transpose(
                        bk[:, c * 128:(c + 1) * 128], x1tm[:, c * 128:(c + 1) * 128], identb[:, :])
                        for c in range(8)][-1],
                        after=[ta, t_consts, pb_free[id(bank)]], sig=True, tag="x1tr")
                    t_x1tm_free = ttr
                    tcp = p.do("vector", lambda v, bk=bank, g2=gt_idx: v.tensor_copy(
                        out=x1T.ap()[:, :, g2 * 128:(g2 + 1) * 128],
                        in_=bk.ap().rearrange("p (c t) -> p c t", c=8)),
                        after=[ttr], sig=True, tag="x1cp")
                    pb_free[id(bank)] = tcp
                    x1t_toks.append(tcp)
                xb_free[buf] = ap_toks[-1]
            t_prev_stats = ap_toks[-1]
            tok_x1T_q[b][q] = x1t_toks[-1]
        tok_x1T_done[b] = x1t_toks[-1]

        scan_toks = []
        prev_scan = tok_hT_done[b - 1] if b > 0 else T_ALL
        prev_u = prev_scan
        for e in range(8):
            act_dt = []
            act_z = []
            for q in range(NQ):
                bankd = (pf0, pf1)[q % 2]
                tmm = p.do("tensor", lambda te, ee=e, qq=q, bk=bankd: [te.matmul(
                    bk[:, :], wslotA[:, 0, dc, ee * 128:(ee + 1) * 128],
                    x1T[:, dc, qq * 512:(qq + 1) * 512], start=(dc == 0), stop=(dc == 7))
                    for dc in range(8)][-1],
                    after=[tok_x1T_q[b][q], pf_free[id(bankd)]] + wA_t, sig=True, tag="dtmm")
                ta = p.do("scalar", lambda s, qq=q, bk=bankd, ee=e: s.activation(
                    out=a_s[:, qq * 512:(qq + 1) * 512], in_=bk[:, :], func=AF.Sigmoid,
                    bias=nbdt[:, ee:ee + 1], scale=-1.0), after=[tmm, t_nbdt, prev_scan], sig=True, tag="a")
                pf_free[id(bankd)] = ta
                act_dt.append(ta)
                bankz = (pf2, pf3)[q % 2]
                tmz = p.do("tensor", lambda te, ee=e, qq=q, bk=bankz: [te.matmul(
                    bk[:, :], wslotA[:, 1, dc, ee * 128:(ee + 1) * 128],
                    x1T[:, dc, qq * 512:(qq + 1) * 512], start=(dc == 0), stop=(dc == 7))
                    for dc in range(8)][-1],
                    after=[tok_x1T_q[b][q], pf_free[id(bankz)]] + wA_t, sig=True, tag="zmm")
                tz = p.do("scalar", lambda s, qq=q, bk=bankz, ee=e: s.activation(
                    out=z_s[:, qq * 512:(qq + 1) * 512], in_=bk[:, :], func=AF.Sigmoid,
                    bias=blnz[:, ee:ee + 1], scale=1.0), after=[tmz, prev_u], sig=True, tag="z")
                pf_free[id(bankz)] = tz
                act_z.append(tz)
            tu = p.do("vector", lambda v: v.tensor_mul(u_s[:, :], a_s[:, :], z_s[:, :]),
                      after=act_dt + act_z, sig=True, tag="az")
            tu = p.do("vector", lambda v: v.tensor_sub(u_s[:, :], z_s[:, :], u_s[:, :]),
                      after=[tu], sig=True, tag="u")
            tsc2 = p.do("vector", lambda v, ee=e: v.tensor_tensor_scan(
                out=hT.ap()[:, ee, :], data0=a_s[:, :], data1=u_s[:, :],
                initial=h0r[:, ee:ee + 1], op0=OP.mult, op1=OP.add),
                after=[tu, t_consts], sig=True, tag="scan")
            scan_toks.append(tsc2)
            prev_scan = tsc2
            prev_u = tu
        tok_hT_done[b] = scan_toks[-1]

        if DBG:
            p.dma("gpsimd", lambda g, bb=b: g.dma_start(out=oh_d[bb].rearrange("p (c t) -> p (c t)") if False else oh_d[bb],
                                                        in_=hT.ap().rearrange("p c t -> p (c t)")),
                  stream="dbgg", after=[tok_hT_done[b], p.etok("tensor")], tag="hdump")
            p.dma("gpsimd", lambda g, bb=b: g.dma_start(out=ox1_d[bb],
                                                        in_=x1T.ap().rearrange("p c t -> p (c t)")),
                  stream="dbgg", after=[tok_x1T_done[b], p.etok("tensor")], tag="x1dump")
        wB_t = load_wslot(wslotB, [wya, wy], "wA", [p.etok("tensor")])
        for q in range(NQ):
            rows0 = b * L + q * 512
            t_xre = p.dma("sync", lambda s, tl=rows0 // 128: s.dma_start(
                out=xre[...], in_=xr_d[:, tl:tl + 4, :]),
                stream="xre", after=[xb_free[0], xb_free[1], t_xre_free], tag="xre")
            radd_toks = []
            for e in range(8):
                bankg = (pf0, pf1)[e % 2]
                tmy = p.do("tensor", lambda te, ee=e, qq=q, bk=bankg: [te.matmul(
                    bk[:, :], wslotB[:, 0, dc, ee * 128:(ee + 1) * 128],
                    x1T[:, dc, qq * 512:(qq + 1) * 512], start=(dc == 0), stop=(dc == 7))
                    for dc in range(8)][-1],
                    after=[tok_x1T_done[b], pf_free[id(bankg)]] + wB_t, sig=True, tag="yamm")
                tg = p.do("scalar", lambda s, bk=bankg, ee=e: s.activation(
                    out=gate[:, :], in_=bk[:, :], func=AF.Silu,
                    bias=bya[:, ee:ee + 1], scale=1.0), after=[tmy, t_gate_free], sig=True, tag="gate")
                pf_free[id(bankg)] = tg
                banky = (pf0, pf1)[(e + 1) % 2]
                tmy2 = p.do("tensor", lambda te, ee=e, qq=q, bk=banky: [te.matmul(
                    bk[:, :], wslotB[:, 1, dc, ee * 128:(ee + 1) * 128],
                    hT[:, dc, qq * 512:(qq + 1) * 512], start=(dc == 0), stop=(dc == 7))
                    for dc in range(8)][-1],
                    after=[tok_hT_done[b], pf_free[id(banky)]] + wB_t, sig=True, tag="ymm")
                tat = p.do("vector", lambda v, bk=banky, ee=e: v.scalar_tensor_tensor(
                    out=attnf[:, :], in0=bk[:, :], scalar=by[:, ee:ee + 1], in1=gate[:, :],
                    op0=OP.add, op1=OP.mult), after=[tmy2, tg, t_attnf_free], sig=True, tag="attnf")
                pf_free[id(banky)] = tat
                t_gate_free = tat
                bankt = (pf2, pf3)[e % 2]
                ttr = p.do("tensor", lambda te, bk=bankt: [te.transpose(
                    bk[:, tt2 * 128:(tt2 + 1) * 128], attnf[:, tt2 * 128:(tt2 + 1) * 128], identf[:, :])
                    for tt2 in range(4)][-1],
                    after=[tat, pf_free[id(bankt)], t_consts], sig=True, tag="attntr")
                t_attnf_free = ttr
                tra = p.do("vector", lambda v, bk=bankt, ee=e: v.tensor_tensor(
                    out=rbuf.ap()[:, :, ee * 128:(ee + 1) * 128],
                    in0=bk.ap().rearrange("p (t c) -> p t c", t=4),
                    in1=xre.ap()[:, :, ee * 128:(ee + 1) * 128], op=OP.add),
                    after=[ttr, t_xre, t_rbuf_free], sig=True, tag="radd")
                pf_free[id(bankt)] = tra
                radd_toks.append(tra)
            t_xre_free = radd_toks[-1]
            t_rw2 = p.dma("sync", lambda s, r0=rows0: s.dma_start(
                out=out_d[r0:r0 + 512, :].rearrange("(t p) d -> p t d", p=128), in_=rbuf[...]),
                stream="rout", after=radd_toks, tag="rout")
            if DBG:
                t_rw2b = p.dma("sync", lambda s, r0=rows0: s.dma_start(
                    out=or_d[r0:r0 + 512, :].rearrange("(t p) d -> p t d", p=128), in_=rbuf[...]),
                    stream="rout", after=radd_toks, tag="routdbg")
            mu_toks = []
            for tt2 in range(4):
                mu_toks.append(p.do("vector", lambda v, t2=tt2: v.tensor_reduce(
                    mus[:, t2:t2 + 1], rbuf[:, t2, :].rearrange("p d -> p () d"),
                    axis=AX.X, op=OP.add), after=radd_toks + [t_prev_stats], sig=True, tag="mu2"))
                mu_toks.append(p.do("scalar", lambda s, t2=tt2: s.activation(
                    out=sq[:, :], in_=rbuf[:, t2, :], func=AF.Square,
                    accum_out=ssq[:, t2:t2 + 1]), after=radd_toks + [t_prev_stats], sig=True, tag="ssq2"))
            tv = p.do("vector", lambda v: v.tensor_scalar(out=mus[:, :], in0=mus[:, :], scalar1=1.0 / DM,
                                                          scalar2=None, op0=OP.mult), after=mu_toks, sig=True)
            tv2 = p.do("vector", lambda v: (v.tensor_scalar(out=veps0[:, :], in0=ssq[:, :], scalar1=1.0 / DM,
                                                            scalar2=EPS, op0=OP.mult, op1=OP.add),
                                            v.tensor_mul(mm2[:, :], mus[:, :], mus[:, :]))[-1],
                       after=[tv, t_prev_sqrt], sig=True)
            tv2 = p.do("vector", lambda v: v.tensor_sub(veps[:, :], veps0[:, :], mm2[:, :]), after=[tv2], sig=True)
            tsc = p.do("scalar", lambda s: s.activation(out=sdv[:, :], in_=veps[:, :], func=AF.Sqrt),
                       after=[tv2], sig=True)
            t_prev_sqrt = tsc
            trc = p.do("vector", lambda v: v.reciprocal(rstd[:, :], sdv[:, :]), after=[tsc], sig=True)
            x2toks = []
            for tt2 in range(4):
                gt_idx = b * 16 + q * 4 + tt2
                rows_t = rows0 + tt2 * 128
                tf = p.do("vector", lambda v, t2=tt2: v.tensor_scalar(
                    out=x2f[:, :], in0=rbuf[:, t2, :], scalar1=mus[:, t2:t2 + 1],
                    scalar2=rstd[:, t2:t2 + 1], op0=OP.subtract, op1=OP.mult),
                    after=[trc, t_x2f_free], sig=True, tag="ln2ap")
                tb2 = p.do("vector", lambda v: v.tensor_copy(out=x2b[:, :], in_=x2f[:, :]),
                           after=[tf, t_x2b_free], sig=True, tag="x2bf")
                t_x2w2 = p.dma("sync", lambda s, r0=rows_t: s.dma_start(
                    out=x2_d[r0:r0 + 128, :].rearrange("(t p) d -> p (t d)", p=128), in_=x2b[:, :]),
                    stream="x2w", after=[tb2], tag="x2w")
                if DBG:
                    t_x2w3 = p.dma("sync", lambda s, r0=rows_t: s.dma_start(
                        out=ox2_d[r0:r0 + 128, :].rearrange("(t p) d -> p (t d)", p=128), in_=x2f[:, :]),
                        stream="x2w", after=[tf], tag="x2wdbg")
                    tb2 = (tb2[0], max(tb2[1], t_x2w3[1])) if tb2[0] == t_x2w3[0] else tb2
                t_x2b_free = t_x2w2
                rcp = []
                for half in range(2):
                    bk = (pf4, pf5)[half]
                    t1 = p.do("tensor", lambda te, hh=half, bk2=bk: [te.transpose(
                        bk2[:, c2 * 128:(c2 + 1) * 128], x2f[:, (hh * 4 + c2) * 128:(hh * 4 + c2 + 1) * 128],
                        identf[:, :]) for c2 in range(4)][-1],
                        after=[tf, pf_free[id(bk)], t_consts, t_x2T_free], sig=True, tag="x2tr")
                    t2 = p.do("vector", lambda v, hh=half, bk2=bk: v.tensor_copy(
                        out=x2T.ap()[:, hh * 4:(hh + 1) * 4, :],
                        in_=bk2.ap().rearrange("p (c t) -> p c t", c=4)),
                        after=[t1], sig=True, tag="x2cp")
                    pf_free[id(bk)] = t2
                    rcp.append(t2)
                t_x2f_free = rcp[-1]
                bkr = (pf2, pf3)[tt2 % 2]
                tr2 = p.do("tensor", lambda te, bk=bkr: [te.matmul(
                    bk[:, 0:8], x2T[:, dc, :], rwT[:, dc, :], start=(dc == 0), stop=(dc == 7))
                    for dc in range(8)][-1],
                    after=rcp + [t_rw, pf_free[id(bkr)]], sig=True, tag="routmm")
                t_x2T_free = tr2
                tl = p.do("vector", lambda v, g2=gt_idx, bk=bkr: v.tensor_copy(
                    out=logits[:, g2 * 8:(g2 + 1) * 8], in_=bk[:, 0:8]),
                    after=[tr2], sig=True, tag="logcp")
                pf_free[id(bkr)] = tl
                tok_logits.append(tl)
                x2toks.append(tl)
            t_rbuf_free = x2toks[-1]
        if b == 0:
            wA_t = load_wslot(wslotA, [wdt, wlnz], "wA", [p.etok("tensor")])

    t_bar = [p.etok("vector"), p.etok("scalar"), p.etok("tensor"),
             p.stream_all("rout"), p.stream_all("x2w"), p.stream_all("xre"),
             p.stream_all("x0"), p.stream_all("x1")]

    W3 = lambda x2_: x2_.ap().rearrange("p (t e) -> p t e", e=E)
    bc3 = lambda a2: a2.ap().rearrange("p t -> p t ()").to_broadcast([128, TILES, E])
    t_log = tok_logits[-1]
    tv = p.do("vector", lambda v: v.tensor_reduce(mrow[:, :], W3(logits), axis=AX.X, op=OP.max),
              after=[t_log], sig=True, tag="smax")
    tv = p.do("vector", lambda v: v.tensor_tensor(out=W3(exps), in0=W3(logits), in1=bc3(mrow), op=OP.subtract),
              after=[tv], sig=True)
    te_ = p.do("scalar", lambda s: s.activation(out=exps[:, :], in_=exps[:, :], func=AF.Exp), after=[tv], sig=True)
    tv = p.do("vector", lambda v: v.tensor_reduce(zrow[:, :], W3(exps), axis=AX.X, op=OP.add), after=[te_], sig=True)
    tv = p.do("vector", lambda v: v.reciprocal(rz[:, :], zrow[:, :]), after=[tv], sig=True)
    tv = p.do("vector", lambda v: v.tensor_tensor(out=W3(prob), in0=W3(exps), in1=bc3(rz), op=OP.mult),
              after=[tv], sig=True)
    tv = p.do("vector", lambda v: v.tensor_reduce(mxp[:, :], W3(prob), axis=AX.X, op=OP.max), after=[tv], sig=True)
    tv = p.do("vector", lambda v: v.tensor_tensor(out=W3(igx), in0=W3(prob), in1=bc3(mxp), op=OP.is_ge),
              after=[tv], sig=True)
    tv = p.do("vector", lambda v: v.scalar_tensor_tensor(out=eqm[:, :], in0=igx[:, :], scalar=-1000.0,
                                                         in1=ioe[:, :], op0=OP.mult, op1=OP.add),
              after=[tv, t_consts], sig=True)
    tv = p.do("vector", lambda v: v.tensor_reduce(idxm[:, :], W3(eqm), axis=AX.X, op=OP.min), after=[tv], sig=True)
    t_idx = p.do("vector", lambda v: v.tensor_scalar(out=idxa[:, :], in0=idxm[:, :], scalar1=1000.0,
                                                     scalar2=None, op0=OP.add), after=[tv], sig=True)
    t_oh = p.do("vector", lambda v: v.tensor_tensor(out=W3(ohb), in0=W3(ioe), in1=bc3(idxa), op=OP.is_equal),
                after=[t_idx], sig=True)
    t_ohf = p.do("vector", lambda v: v.tensor_tensor(out=W3(ohf), in0=W3(ioe), in1=bc3(idxa), op=OP.is_equal),
                 after=[t_idx], sig=True)
    t_cnt = p.do("tensor", lambda t: t.matmul(pf2[:, 0:256], triu[:, :], ohb[:, :], start=True, stop=True),
                 after=[t_oh, t_consts, pf_free[id(pf2)]], sig=True, tag="cntmm")
    t_totm = p.do("tensor", lambda t: t.matmul(pf3[0:1, 0:256], on128[:, :], ohb[:, :], start=True, stop=True),
                  after=[t_oh, t_consts, pf_free[id(pf3)]], sig=True, tag="totmm")
    t_tot = p.do("vector", lambda v: v.tensor_copy(out=tot[:, :], in_=pf3[0:1, 0:256]), after=[t_totm], sig=True)
    pf_free[id(pf3)] = t_tot
    tv = p.do("vector", lambda v: v.memset(offa[:, :], 0.0), after=[t_tot], sig=True)
    tv = p.do("vector", lambda v: v.tensor_copy(out=offa[0:1, E:], in_=tot[0:1, 0:TE - E]), after=[tv], sig=True)
    cur, nxt = offa, offb
    d = E
    while d < TE:
        tv = p.do("vector", lambda v, c=cur, n=nxt, dd=d: (
            v.tensor_copy(out=n[0:1, 0:dd], in_=c[0:1, 0:dd]),
            v.tensor_add(n[0:1, dd:], c[0:1, dd:], c[0:1, 0:TE - dd]))[-1], after=[tv], sig=True)
        cur, nxt = nxt, cur
        d *= 2
    offfin = cur
    t_pre = tv
    t_cnts = p.do("vector", lambda v: v.tensor_reduce(cnts[:, :], tot.ap().rearrange("a (t e) -> a e t", e=E),
                                                      axis=AX.X, op=OP.add), after=[t_tot], sig=True, tag="cnts")
    t_bcm = p.do("tensor", lambda t: t.matmul(pf4[:, 0:256], onr[0:1, :], offfin[0:1, 0:256],
                                              start=True, stop=True),
                 after=[t_pre, t_consts, pf_free[id(pf4)]], sig=True, tag="bcmm")
    tv = p.do("vector", lambda v: v.tensor_copy(out=offbc[:, :], in_=pf4[:, 0:256]), after=[t_bcm], sig=True)
    pf_free[id(pf4)] = tv
    tv = p.do("vector", lambda v: v.tensor_tensor(out=rt1[:, :], in0=pf2[:, 0:256], in1=offbc[:, :], op=OP.add),
              after=[tv, t_cnt], sig=True)
    pf_free[id(pf2)] = tv
    tv = p.do("vector", lambda v: v.tensor_mul(rt2[:, :], rt1[:, :], ohf[:, :]), after=[tv, t_ohf], sig=True)
    tv = p.do("vector", lambda v: v.tensor_reduce(pos0[:, :], W3(rt2), axis=AX.X, op=OP.add), after=[tv], sig=True)
    t_pos = p.do("vector", lambda v: v.tensor_scalar(out=pos[:, :], in0=pos0[:, :], scalar1=-1.0,
                                                     scalar2=None, op0=OP.add), after=[tv], sig=True, tag="pos")
    tv = p.do("vector", lambda v: (v.scalar_tensor_tensor(out=slot[:, :], in0=idxa[:, :], scalar=float(CL),
                                                          in1=pos[:, :], op0=OP.mult, op1=OP.add),
                                   v.tensor_scalar(out=ovf[:, :], in0=pos[:, :], scalar1=float(CL),
                                                   scalar2=None, op0=OP.is_ge))[-1], after=[t_pos], sig=True)
    tv = p.do("vector", lambda v: v.scalar_tensor_tensor(out=slot[:, :], in0=ovf[:, :], scalar=SENT,
                                                         in1=slot[:, :], op0=OP.mult, op1=OP.add),
              after=[tv], sig=True)
    tv = p.do("vector", lambda v: v.tensor_copy(out=sloti[:, :], in_=slot[:, :]), after=[tv], sig=True)
    tv = p.do("vector", lambda v: (v.tensor_scalar(out=mgi[:, :], in0=sloti[:, :], scalar1=127, scalar2=None,
                                                   op0=OP.bitwise_and),
                                   v.tensor_scalar(out=dvi[:, :], in0=sloti[:, :], scalar1=7, scalar2=None,
                                                   op0=OP.arith_shift_right))[-1], after=[tv], sig=True)
    tv = p.do("vector", lambda v: v.tensor_scalar(out=mgi[:, :], in0=mgi[:, :], scalar1=NT, scalar2=None,
                                                  op0=OP.mult), after=[tv], sig=True)
    t_tgt = p.do("vector", lambda v: v.tensor_tensor(out=tgt[:, :], in0=mgi[:, :], in1=dvi[:, :], op=OP.add),
                 after=[tv], sig=True, tag="tgt")
    t_pgv = p.do("vector", lambda v: (v.tensor_copy(out=pgv[:, 0:TILES * 2:2], in_=tokidf[:, :]),
                                      v.tensor_copy(out=pgv[:, 1:TILES * 2:2], in_=mxp[:, :]))[-1],
                 after=[t_consts, tok_logits[-1], (lambda: None, 0) if False else ("vector", p.chan["vector"][1])], sig=True, tag="pgv")
    for t in range(TILES):
        p.dma("gpsimd", lambda g, tt2=t: g.indirect_dma_start(
            out=pg_d.ap(), out_offset=IndirectOffsetOnAxis(tgt[:, tt2:tt2 + 1], 0),
            in_=pgv[:, 2 * tt2:2 * tt2 + 2], in_offset=None, bounds_check=reg_ns, oob_is_err=False),
            stream="pack0", after=[t_tgt, t_pgv, t_pfill, t_regs], tag="scat")
    sc_all = p.stream_all("pack0")
    p.dma("gpsimd", lambda g: g.dma_start(out=sentf[:, :], in_=pg_d.ap().rearrange("(a b) c -> a (b c)", a=128)),
          stream="rb", after=[sc_all], tag="pgrb")
    t_rb0 = p.stream_all("rb")
    t_rb1 = p.do("vector", lambda v: v.tensor_copy(out=ptok[:, :], in_=sentf[:, 0:NT * 2:2]),
                 after=[t_rb0], sig=True, tag="ptokcast")
    t_rb2 = p.do("vector", lambda v: v.tensor_copy(out=gsb[:, :], in_=sentf[:, 1:NT * 2:2]),
                 after=[t_rb0], sig=True, tag="gcast")
    t_rb = ("vector", max(t_rb1[1], t_rb2[1]))
    t_m = p.do("tensor", lambda t: t.matmul(pf5[0:8, 0:8], cnts[0:1, :], sel[0:1, :], start=True, stop=True),
               after=[t_cnts, t_consts, pf_free[id(pf5)]], sig=True, tag="ccmm")
    t_msb = p.do("vector", lambda v: v.tensor_copy(out=msb[:, :], in_=pf5[0:8, 0:8]), after=[t_m], sig=True)
    pf_free[id(pf5)] = t_msb
    t_ccw = p.dma("gpsimd", lambda g: g.dma_start(out=cc_i[:, :], in_=msb[:, :]), stream="cc", after=[t_msb])
    t_cc = p.do("gpsimd", lambda g: g.collective_compute("AllReduce", OP.add, replica_groups=[list(range(NCORES))],
                                                         ins=[cc_i.ap().opt()], outs=[cc_o.ap().opt()]),
                after=[t_ccw], sig=True, tag="cc")
    t_ccr = p.dma("gpsimd", lambda g: g.dma_start(out=ccs[:, :], in_=cc_o[:, :]), stream="cc", after=[t_cc])
    tv2 = p.do("vector", lambda v: v.tensor_mul(ccm[:, :], ccs[:, :], prv[:, :]), after=[t_ccr, t_consts], sig=True)
    tv2 = p.do("vector", lambda v: v.tensor_reduce(offcol[:, :], ccm[:, :], axis=AX.X, op=OP.add), after=[tv2], sig=True)
    t_row = p.do("tensor", lambda t: t.matmul(pf5[0:1, 0:8], offcol[:, :], id8[:, :], start=True, stop=True),
                 after=[tv2, t_consts, pf_free[id(pf5)]], sig=True)
    t_orow = p.do("vector", lambda v: v.tensor_copy(out=offrow[:, :], in_=pf5[0:1, 0:8]), after=[t_row], sig=True)
    pf_free[id(pf5)] = t_orow
    t_thrm = p.do("tensor", lambda t: t.matmul(pf5[:, 0:8], onr[0:1, :], offrow[0:1, :], start=True, stop=True),
                  after=[t_orow, pf_free[id(pf5)]], sig=True)
    t_thr = p.do("vector", lambda v: v.tensor_scalar(out=thr[:, :], in0=pf5[:, 0:8], scalar1=-1.0,
                                                     scalar2=float(CCAP), op0=OP.mult, op1=OP.add),
                 after=[t_thrm], sig=True, tag="thr")
    pf_free[id(pf5)] = t_thr

    def gmask(v):
        last = None
        for e in range(E):
            cols = slice(e * CPT, (e + 1) * CPT)
            last = v.tensor_tensor(out=gm[:, cols], in0=slotp[:, cols],
                                   in1=thr[:, e:e + 1].to_broadcast([128, CPT]), op=OP.is_lt)
        return last
    t_gm = p.do("vector", gmask, after=[t_thr, t_consts], sig=True, tag="gm")
    t_gfix = p.do("vector", lambda v: v.tensor_mul(gsb2[:, :], gsb[:, :], gm[:, :]),
                  after=[t_gm, t_rb], sig=True, tag="gfix")

    t_x2w = p.stream_all("x2w")
    t_gz = p.do("vector", lambda v: v.memset(gtb[:, :, :], 0.0), after=t_bar, sig=True, tag="gz")
    xeT_free = [t_bar[2], t_bar[2]]
    w1_free = [t_bar[2], t_bar[2]]
    eo_free = [t_bar[0], t_bar[0]]
    xeT_tok = [None, None]
    t_rout_all = p.stream_all("rout")

    pf_state = {}

    def prefetch(e):
        par = e % 2
        st = {}
        ga = []
        for t in range(CPT):
            ga.append(p.dma("gpsimd", lambda g, tt2=t, ee=e: g.indirect_dma_start(
                out=gtb[:, tt2, :], out_offset=None, in_=x2_d.ap(),
                in_offset=IndirectOffsetOnAxis(ptok[:, ee * CPT + tt2:ee * CPT + tt2 + 1], 0),
                bounds_check=reg_nt, oob_is_err=False),
                stream=f"gx{par}", after=[t_rb, t_gz, t_x2w, pf_state.get("gt_free", T_ALL), t_regs], tag="gx"))
        st["ga_all"] = p.stream_all(f"gx{par}")
        st["w1_pre"] = {}
        for fw in (0, 1):
            st["w1_pre"][fw] = p.dma("gpsimd", lambda g, ee=e, ff=fw: g.dma_start(
                out=w1win[ff % 2][...], in_=w1_d[ee, ff].rearrange("p (a c) -> p a c", a=8)),
                stream=f"w1{fw % 2}", after=[w1_free[fw % 2]], tag="w1pre")
        return st

    def prefetch_w2(e, st):
        st["b2"] = p.dma("gpsimd", lambda g, ee=e: g.dma_start(out=b2bf[:, :], in_=b2_d[ee:ee + 1, :]),
                         stream="b2", after=[pf_state.get("prev_eo", t_bar[2])], tag="b2")
        st["w2"] = p.dma("scalar", lambda s, ee=e: s.dma_start(
            out=w2sb[...], in_=w2bf_d[ee].rearrange("p (a c) -> p a c", a=32)),
            stream="w2", after=[pf_state.get("prev_eo", t_bar[2]), w2cast_t[e]], tag="w2")

    def compute_trans(e, st):
        par = e % 2
        trs = []
        for t in range(CPT):
            bank = (pb0, pb1)[t % 2]
            t1 = p.do("tensor", lambda te, tt2=t, bk=bank: [te.transpose(
                bk[:, c * 128:(c + 1) * 128], gtb[:, tt2, c * 128:(c + 1) * 128], identb[:, :])
                for c in range(8)][-1],
                after=[st["ga_all"], pb_free[id(bank)], t_consts, xeT_free[par]], sig=True, tag="xetr")
            t2 = p.do("vector", lambda v, tt2=t, bk=bank, pp=par: v.tensor_copy(
                out=xeT[pp].ap()[:, :, tt2 * 128:(tt2 + 1) * 128],
                in_=bk.ap().rearrange("p (c t) -> p c t", c=8)),
                after=[t1], sig=True, tag="xecp")
            pb_free[id(bank)] = t2
            trs.append(t2)
        pf_state["gt_free"] = trs[-1]
        xeT_tok[par] = trs[-1]

    def compute_h(e, st):
        par = e % 2
        h_toks = []
        for fw in range(8):
            wpar = fw % 2
            if fw in st["w1_pre"]:
                t_w1 = st["w1_pre"][fw]
            else:
                t_w1 = p.dma("gpsimd", lambda g, ee=e, ff=fw: g.dma_start(
                    out=w1win[ff % 2][...], in_=w1_d[ee, ff].rearrange("p (a c) -> p a c", a=8)),
                    stream=f"w1{wpar}", after=[w1_free[wpar]], tag="w1")
            for fc in range(4):
                for ci, (c0, cn) in enumerate([(0, 512), (512, CL - 512)]):
                    bank = (pf2, pf3)[(fc * 2 + ci) % 2]
                    tm = p.do("tensor", lambda te, pp=wpar, f2=fc, cc0=c0, ccn=cn, bk=bank, pr=par: [te.matmul(
                        bk[:, 0:ccn], w1win[pp][:, dc, f2 * 128:(f2 + 1) * 128],
                        xeT[pr][:, dc, cc0:cc0 + ccn], start=(dc == 0), stop=(dc == 7))
                        for dc in range(8)][-1],
                        after=[t_w1, xeT_tok[par], pf_free[id(bank)]], sig=True, tag="hmm")
                    th = p.do("scalar", lambda s, ff=fw, f2=fc, cc0=c0, ccn=cn, bk=bank, ee=e: s.activation(
                        out=h_bf[:, ff * 4 + f2, cc0:cc0 + ccn], in_=bk[:, 0:ccn], func=AF.Gelu,
                        bias=b1r[:, ee * 32 + ff * 4 + f2:ee * 32 + ff * 4 + f2 + 1], scale=1.0),
                        after=[tm, t_consts], sig=True, tag="gelu")
                    pf_free[id(bank)] = th
                    h_toks.append(th)
            w1_free[wpar] = h_toks[-1]
        return h_toks[-1]

    def compute_eo(e, st, t_hlast):
        par = e % 2
        last_mm = None
        for tt2 in range(CPT):
            epar = tt2 % 2
            evs = []
            for dh in range(2):
                bank = (pf0, pf1)[dh]
                def eomm(te, t2=tt2, dd=dh, bk=bank):
                    te.matmul(bk[:, :], onrb[0:1, :], b2bf[0:1, dd * 512:(dd + 1) * 512],
                              start=True, stop=False)
                    last = None
                    for fc in range(32):
                        last = te.matmul(bk[:, :], h_bf[:, fc, t2 * 128:(t2 + 1) * 128],
                                         w2sb[:, fc, dd * 512:(dd + 1) * 512],
                                         start=False, stop=(fc == 31))
                    return last
                tm = p.do("tensor", eomm,
                          after=[st["w2"], st["b2"], pf_free[id(bank)], eo_free[epar], t_hlast, t_consts],
                          sig=True, tag="eomm")
                last_mm = tm
                tev = p.do("vector", lambda v, t2=tt2, dd=dh, bk=bank, ee=e, pp=epar: v.tensor_scalar(
                    out=eob[pp][:, dd * 512:(dd + 1) * 512], in0=bk[:, :],
                    scalar1=gsb2[:, ee * CPT + t2:ee * CPT + t2 + 1], scalar2=None, op0=OP.mult),
                    after=[tm, t_gfix], sig=True, tag="eoev")
                pf_free[id(bank)] = tev
                evs.append(tev)
            t_sa = p.dma("gpsimd", lambda g, t2=tt2, ee=e, pp=epar: g.indirect_dma_start(
                out=out_d.ap(), out_offset=IndirectOffsetOnAxis(ptok[:, ee * CPT + t2:ee * CPT + t2 + 1], 0),
                in_=eob[pp][:, :], in_offset=None, bounds_check=reg_nt, oob_is_err=False,
                compute_op=OP.add), stream="sa",
                after=evs + [t_rb, t_rout_all], tag="sa")
            eo_free[epar] = t_sa
        pf_state["prev_eo"] = last_mm
        xeT_free[par] = last_mm

    st_cur = prefetch(0)
    prefetch_w2(0, st_cur)
    for e in range(E):
        compute_trans(e, st_cur)
        t_hlast = compute_h(e, st_cur)
        st_next = prefetch(e + 1) if e + 1 < E else None
        compute_eo(e, st_cur, t_hlast)
        if st_next is not None:
            prefetch_w2(e + 1, st_next)
        st_cur = st_next

    t_sa_all = p.stream_all("sa")
    dbg_all = p.stream_all("dbg") if DBG else (None, 0)
    dbgg_all = p.stream_all("dbgg") if DBG else (None, 0)
    p.final_waits("gpsimd", [t_sa_all, t_rout_all, dbg_all, dbgg_all])
    p.final_waits("sync", [t_sa_all, t_rout_all])
    p.verify()
    with nc.Block() as block:
        p.build(block)
    ctx.close()
    return nc


_NC_CACHE = None


def kernel(**inputs):
    global _NC_CACHE
    xs = np.asarray(inputs["L0_attn_x"], np.float32).reshape(NCORES, NTOK, DM)
    xrs = np.ascontiguousarray(xs.reshape(NCORES, TILES, 128, DM).transpose(0, 2, 1, 3))
    def wslotr(w):
        return np.ascontiguousarray(
            np.asarray(w, np.float32).T.reshape(8, 128, DM).transpose(1, 0, 2).reshape(128, 8 * DM))
    wlnzT = wslotr(inputs["L0_attn_w_ln_z"])
    wdtT = wslotr(inputs["L0_attn_w_dt"])
    wyT = wslotr(inputs["L0_attn_w_y"])
    wyaT = wslotr(inputs["L0_attn_w_ya"])
    routerT = np.ascontiguousarray(np.asarray(inputs["L0_ffn_router_w"], np.float32).T)
    arr = lambda v: np.ascontiguousarray(np.asarray(v, np.float32))
    col8 = lambda v: np.ascontiguousarray(np.asarray(v, np.float32).reshape(8, 128).T)
    blnz = col8(inputs["L0_attn_b_ln_z"]); bdt = col8(inputs["L0_attn_b_dt"])
    by = col8(inputs["L0_attn_b_y"]); bya = col8(inputs["L0_attn_b_ya"])
    h0r = col8(inputs["L0_attn_h0"])
    w1 = np.ascontiguousarray(np.asarray(inputs["L0_ffn_w1"], np.float32)
                              .reshape(E, 8, 128, 8, 512).transpose(0, 3, 2, 1, 4).reshape(E, 8, 128, 4096))
    w2 = np.ascontiguousarray(np.asarray(inputs["L0_ffn_w2"], np.float32)
                              .reshape(E, 32, 128, DM).transpose(0, 2, 1, 3).reshape(E, 128, 32 * DM))
    b1r = np.ascontiguousarray(np.asarray(inputs["L0_ffn_b1"], np.float32).reshape(E, 32, 128).transpose(2, 0, 1).reshape(128, E * 32))
    b2n = arr(inputs["L0_ffn_b2"])

    identb = np.eye(128, dtype=np.float32).astype(bfd)
    identf = np.eye(128, dtype=np.float32)
    triu = np.triu(np.ones((128, 128), np.float32)).astype(bfd)
    ones128 = np.ones((128, 1), np.float32).astype(bfd)
    onesrow = np.ones((1, 128), np.float32)
    onesrowb = onesrow.astype(bfd)
    iotae = np.tile(np.arange(E, dtype=np.float32), (128, TILES))
    tokidi = (np.arange(TILES)[None, :] * 128 + np.arange(128)[:, None]).astype(np.int32)
    tokidf = tokidi.astype(np.float32)
    slotpos = np.zeros((128, NT), np.float32)
    for e in range(E):
        for t in range(CPT):
            slotpos[:, e * CPT + t] = t * 128 + np.arange(128)
    ident8 = np.eye(8, dtype=np.float32)

    if _NC_CACHE is None:
        _NC_CACHE = build_program()
    nc = _NC_CACHE

    in_maps = []
    for k in range(NCORES):
        selrow = np.zeros((1, 8), np.float32); selrow[0, k] = 1.0
        prevmask = np.zeros((8, 8), np.float32); prevmask[:, :k] = 1.0
        in_maps.append({
            "x": xs[k], "xr": xrs[k], "wlnzT": wlnzT, "wdtT": wdtT, "wyT": wyT, "wyaT": wyaT,
            "blnz": blnz, "bdt": bdt, "by": by, "bya": bya, "h0r": h0r,
            "routerT": routerT, "w1r": w1, "b1r": b1r, "w2r": w2, "b2n": b2n,
            "identb": identb, "identf": identf, "triu": triu,
            "ones128": ones128, "onesrow": onesrow, "onesrowb": onesrowb,
            "iotae": iotae, "tokidi": tokidi, "tokidf": tokidf, "slotpos": slotpos,
            "selrow": selrow, "prevmask": prevmask, "ident8": ident8,
        })
    res = run_bass_kernel_spmd(nc, in_maps, core_ids=list(range(NCORES)))
    out = np.concatenate([np.asarray(res.results[k]["out"], np.float32)[None] for k in range(NCORES)], 0)
    return out.reshape(16, 2048, 1024)


# version 09
# speedup vs baseline: 1.0784x; 1.0344x over previous
import numpy as np
import ml_dtypes
DBG = False
import concourse.bass as bass
import concourse.mybir as mybir
from concourse.bass_utils import run_bass_kernel_spmd
from concourse.bass import IndirectOffsetOnAxis
from contextlib import ExitStack

F32 = mybir.dt.float32
BF16 = mybir.dt.bfloat16
I32 = mybir.dt.int32
AF = mybir.ActivationFunctionType
OP = mybir.AluOpType
AX = mybir.AxisListType
bfd = ml_dtypes.bfloat16

NCORES = 8
BPC = 2
L = 2048
DM = 1024
DF = 4096
E = 8
CL = 768
NTOK = BPC * L
TILES = NTOK // 128
NS = E * CL
NT = NS // 128
CPT = CL // 128
CCAP = 6144
SENT = float(1 << 21)
EPS = 1e-5

ENGINES = ("sync", "vector", "scalar", "tensor", "gpsimd")
QCH = {"sync": "qs", "scalar": "qa", "gpsimd": "qg", "vector": "qv", "tensor": "qt"}


class Prog:
    def __init__(self, nc, ctx):
        self.nc = nc
        self.ctx = ctx
        self.chan = {}
        for ch in list(ENGINES) + list(QCH.values()):
            self.chan[ch] = [ctx.enter_context(nc.semaphore("sem_" + ch)), 0]
        self.steps = {e: [] for e in ENGINES}
        self.watermark = {e: {} for e in ENGINES}

    def stream(self, name):
        ch = "#" + name
        if ch not in self.chan:
            self.chan[ch] = [self.ctx.enter_context(self.nc.semaphore("s" + name)), 0]
        return ch

    def _queue(self, engine, fn, after, sem_ch, inc, tag):
        waits = {}
        for c, v in after:
            if v > 0 and waits.get(c, 0) < v:
                waits[c] = v
        tok = None
        if sem_ch is not None:
            sem, cnt = self.chan[sem_ch]
            cnt += inc
            self.chan[sem_ch][1] = cnt
            tok = (sem_ch, cnt)
        wm = self.watermark[engine]
        needed = []
        for c, v in sorted(waits.items()):
            if wm.get(c, 0) < v:
                needed.append((c, v))
                wm[c] = v
        self.steps[engine].append(
            (needed, fn, self.chan[sem_ch][0] if tok else None, inc if tok else 0, tag))
        return tok

    def do(self, engine, fn, after=(), sig=False, tag=""):
        return self._queue(engine, fn, after, engine if sig else None, 1, tag)

    def dma(self, engine, fn, stream, after=(), tag=""):
        return self._queue(engine, fn, after, self.stream(stream), 16, tag)

    def stream_all(self, name):
        ch = "#" + name
        return (ch, self.chan[ch][1])

    def etok(self, engine):
        return (engine, self.chan[engine][1])

    def final_waits(self, engine, tokens):
        waits = {}
        for ch, v in tokens:
            if ch is None or v <= 0:
                continue
            waits[ch] = max(waits.get(ch, 0), v)
        self.steps[engine].append((sorted(waits.items()), lambda e: None, None, 0, "final"))

    def verify(self):
        pos = {e: 0 for e in ENGINES}
        counts = {ch: 0 for ch in self.chan}
        sem2ch = {id(h): ch for ch, (h, _) in self.chan.items()}
        progressed = True
        while progressed:
            progressed = False
            for e in ENGINES:
                while pos[e] < len(self.steps[e]):
                    needed, fn, sem, inc, tag = self.steps[e][pos[e]]
                    if all(counts[ch] >= v for ch, v in needed):
                        if inc:
                            counts[sem2ch[id(sem)]] += inc
                        pos[e] += 1
                        progressed = True
                    else:
                        break
        stuck = [(e, pos[e], self.steps[e][pos[e]][0], self.steps[e][pos[e]][4])
                 for e in ENGINES if pos[e] < len(self.steps[e])]
        if stuck:
            msg = "; ".join(f"{e}@{i} tag={t!r} needs {n}" for e, i, n, t in stuck)
            raise RuntimeError("schedule deadlock: " + msg)

    def build(self, block):
        chans = self.chan

        def emit_for(name):
            def run(eng):
                for needed, fn, sem, inc, tag in self.steps[name]:
                    for ch, v in needed:
                        eng.wait_ge(chans[ch][0], v)
                    inst = fn(eng)
                    if inc:
                        assert inst is not None, f"sig step returned None: {tag}"
                        inst.then_inc(sem, inc)
            return run

        for name in ENGINES:
            if self.steps[name]:
                getattr(block, name)(emit_for(name))



class View:
    def __init__(self, ap_):
        self._ap = ap_
    def __getitem__(self, k):
        return self._ap[k]
    def ap(self):
        return self._ap


def build_program():
    nc = bass.Bass("TRN2", target_bir_lowering=False, detect_race_conditions=False)
    DI = lambda nm, shp, dt=F32: nc.dram_tensor(nm, shp, dt, kind="ExternalInput")

    x_d = DI("x", [NTOK, DM])
    xr_d = DI("xr", [128, TILES, DM])
    wlnz = DI("wlnzT", [128, 8 * DM]);  wdt = DI("wdtT", [128, 8 * DM])
    wy = DI("wyT", [128, 8 * DM]);      wya = DI("wyaT", [128, 8 * DM])
    blnz_d = DI("blnz", [128, 8]); bdt_d = DI("bdt", [128, 8])
    by_d = DI("by", [128, 8]);     bya_d = DI("bya", [128, 8])
    h0_d = DI("h0r", [128, 8])
    rw_d = DI("routerT", [DM, E])
    w1_d = DI("w1r", [E, 8, 128, 4096]);  b1_d = DI("b1r", [128, E * 32])
    w2_d = DI("w2r", [E, 128, 32 * DM]);  b2_d = DI("b2n", [E, DM])
    identb_d = DI("identb", [128, 128], BF16)
    identf_d = DI("identf", [128, 128])
    triu_d = DI("triu", [128, 128], BF16)
    on128_d = DI("ones128", [128, 1], BF16)
    onr_d = DI("onesrow", [1, 128])
    onrb_d = DI("onesrowb", [1, 128], BF16)
    ioe_d = DI("iotae", [128, TILES * E])
    tokid_d = DI("tokidi", [128, TILES], I32)
    tokidf_d = DI("tokidf", [128, TILES])
    slotp_d = DI("slotpos", [128, NT])
    sel_d = DI("selrow", [1, 8])
    prev_d = DI("prevmask", [8, 8])
    id8_d = DI("ident8", [8, 8])

    out_d = nc.dram_tensor("out", [NTOK, DM], F32, kind="ExternalOutput")
    if DBG:
        or_d = nc.dram_tensor("o_r", [NTOK, DM], F32, kind="ExternalOutput")
        oh_d = nc.dram_tensor("o_hT", [BPC, 128, 8 * L], F32, kind="ExternalOutput")
        ox1_d = nc.dram_tensor("o_x1T", [BPC, 128, 8 * L], F32, kind="ExternalOutput")
        olg_d = nc.dram_tensor("o_lg", [128, TILES * E], F32, kind="ExternalOutput")
        ox2_d = nc.dram_tensor("o_x2", [NTOK, DM], F32, kind="ExternalOutput")
        og_d = nc.dram_tensor("o_gs", [128, NT], F32, kind="ExternalOutput")
        opt_d = nc.dram_tensor("o_pt", [128, NT], F32, kind="ExternalOutput")
    x2_d = nc.dram_tensor("x2sc", [NTOK, DM], BF16)
    pg_d = nc.dram_tensor("pgd", [NS, 2], F32)
    w2bf_d = nc.dram_tensor("w2bfd", [E, 128, 32 * DM], BF16)
    cc_i = nc.dram_tensor("cc_i", [8, 8], F32)
    cc_o = nc.dram_tensor("cc_o", [8, 8], F32)

    ctx = ExitStack()
    S = lambda nm, shp, dt=F32: ctx.enter_context(nc.sbuf_tensor(nm, shp, dt))

    identb = S("s_identb", [128, 128], BF16)
    identf = S("s_identf", [128, 128])
    triu = S("s_triu", [128, 128], BF16)
    on128 = S("s_on128", [128, 1], BF16)
    onr = S("s_onr", [1, 128])
    onrb = S("s_onrb", [1, 128], BF16)
    ioe = S("s_ioe", [128, TILES * E])
    tokidi = S("s_tokidi", [128, TILES], I32)
    tokidf = S("s_tokidf", [128, TILES])
    slotp = S("s_slotp", [128, NT])
    sel = S("s_sel", [1, 8]); prv = S("s_prv", [8, 8]); id8 = S("s_id8", [8, 8])
    blnz = S("s_blnz", [128, 8]); bdt = S("s_bdt", [128, 8]); nbdt = S("s_nbdt", [128, 8])
    by = S("s_by", [128, 8]); bya = S("s_bya", [128, 8]); h0r = S("s_h0r", [128, 8])
    rwT = S("s_rwT", [128, 8, E])
    b1r = S("s_b1r", [128, E * 32])
    b2bf = S("s_b2bf", [1, 1024], BF16)
    logits = S("s_logits", [128, TILES * E])
    TE = TILES * E
    mrow = S("s_mrow", [128, TILES]); exps = S("s_exps", [128, TE])
    zrow = S("s_zrow", [128, TILES]); rz = S("s_rz", [128, TILES])
    prob = S("s_prob", [128, TE]); mxp = S("s_mxp", [128, TILES])
    igx = S("s_igx", [128, TE]); eqm = S("s_eqm", [128, TE])
    idxm = S("s_idxm", [128, TILES]); idxa = S("s_idxa", [128, TILES])
    ohb = S("s_ohb", [128, TE], BF16); ohf = S("s_ohf", [128, TE])
    tot = S("s_tot", [1, TE]); offa = S("s_offa", [1, TE]); offb = S("s_offb", [1, TE])
    offbc = S("s_offbc", [128, TE]); rt1 = S("s_rt1", [128, TE]); rt2 = S("s_rt2", [128, TE])
    pos0 = S("s_pos0", [128, TILES]); pos = S("s_pos", [128, TILES])
    slot = S("s_slot", [128, TILES]); ovf = S("s_ovf", [128, TILES])
    sloti = S("s_sloti", [128, TILES], I32)
    mgi = S("s_mgi", [128, TILES], I32); dvi = S("s_dvi", [128, TILES], I32)
    tgt = S("s_tgt", [128, TILES], I32)
    cnts = S("s_cnts", [1, E])
    sentt = S("s_sentt", [128, NT], I32)
    sentf = S("s_sentf", [128, NT * 2])
    pgv = S("s_pgv", [128, TILES * 2])
    ptok = S("s_ptok", [128, NT], I32)
    gsb = S("s_gsb", [128, NT]); gm = S("s_gm", [128, NT]); gsb2 = S("s_gsb2", [128, NT])
    ccs = S("s_ccs", [8, 8]); ccm = S("s_ccm", [8, 8]); msb = S("s_msb", [8, 8])
    offcol = S("s_offcol", [8, 1]); offrow = S("s_offrow", [1, 8]); thr = S("s_thr", [128, E])
    mus = S("s_mus", [128, 4]); ssq = S("s_ssq", [128, 4])
    veps0 = S("s_veps0", [128, 4]); mm2 = S("s_mm2", [128, 4]); veps = S("s_veps", [128, 4])
    sdv = S("s_sdv", [128, 4]); rstd = S("s_rstd", [128, 4])

    AR = 176 * 1024
    arena = S("s_arena", [128, AR], mybir.dt.uint8)

    def carve(off, shape, dt):
        esz = {F32: 4, BF16: 2, I32: 4}[dt]
        n = 1
        for s in shape[1:]:
            n *= s
        nb = n * esz
        assert off % 4 == 0 and off + nb <= AR, (off, nb)
        ap_ = arena[:, off:off + nb].bitcast(dt)
        if len(shape) == 3:
            ap_ = ap_.rearrange("p (a b) -> p a b", a=shape[1])
        elif len(shape) == 4:
            ap_ = ap_.rearrange("p (a b c) -> p a b c", a=shape[1], b=shape[2])
        return View(ap_), off + nb

    off = 0
    wslotA, off = carve(off, [128, 2, 8, 1024], BF16)
    wslotB = wslotA
    x1T, off = carve(off, [128, 8, L], BF16)
    hT, off = carve(off, [128, 8, L], BF16)
    a_s, off = carve(off, [128, L], F32)
    z_s, off = carve(off, [128, L], F32)
    u_s, off = carve(off, [128, L], F32)
    xre, off = carve(off, [128, 4, 1024], F32)
    xbuf = View(xre.ap().rearrange("p a d -> p (a d)").rearrange("p (a b d) -> p a b d", a=2, b=2))
    rbuf, off = carve(off, [128, 4, 1024], F32)
    x1tm, off = carve(off, [128, 1024], BF16)
    x2f, off = carve(off, [128, 1024], F32)
    x2b, off = carve(off, [128, 1024], BF16)
    x2T, off = carve(off, [128, 8, 128], F32)
    gate, off = carve(off, [128, 512], F32)
    attnf, off = carve(off, [128, 512], F32)
    sq, off = carve(off, [128, 1024], F32)
    A_END = off
    off = 0
    gtb, off = carve(off, [128, CPT, 1024], BF16)
    xeT0, off = carve(off, [128, 8, CL], BF16)
    xeT1, off = carve(off, [128, 8, CL], BF16)
    xeT = [xeT0, xeT1]
    w1w0, off = carve(off, [128, 8, 512], BF16)
    w1w1, off = carve(off, [128, 8, 512], BF16)
    w1win = [w1w0, w1w1]
    w2sb, off = carve(off, [128, 32, 1024], BF16)
    h_bf, off = carve(off, [128, 32, CL], BF16)
    eob0, off = carve(off, [128, 1024], F32)
    eob1, off = carve(off, [128, 1024], F32)
    eob = [eob0, eob1]
    assert off <= A_END, (off, A_END)

    P = lambda nm, shp, dt=F32: ctx.enter_context(nc.psum_tensor(nm, shp, dt))
    pf0 = P("pf0", [128, 512]); pf1 = P("pf1", [128, 512])
    pf2 = P("pf2", [128, 512]); pf3 = P("pf3", [128, 512])
    pf4 = P("pf4", [128, 512]); pf5 = P("pf5", [128, 512])
    pb0 = P("pb0", [128, 1024], BF16); pb1 = P("pb1", [128, 1024], BF16)

    p = Prog(nc, ctx)
    T_ALL = (None, 0)

    reg_ns = nc.gpsimd.alloc_register("bc_ns")
    reg_nt = nc.gpsimd.alloc_register("bc_nt")

    def setregs(g):
        g.reg_mov(reg_ns, NS - 1)
        return g.reg_mov(reg_nt, NTOK - 1)
    t_regs = p.do("gpsimd", setregs, sig=True, tag="regs")

    for nm, dst, src in [("identb", identb, identb_d), ("identf", identf, identf_d),
                         ("triu", triu, triu_d), ("on128", on128, on128_d),
                         ("onr", onr, onr_d), ("onrb", onrb, onrb_d),
                         ("ioe", ioe, ioe_d), ("tokidi", tokidi, tokid_d), ("tokidf", tokidf, tokidf_d),
                         ("slotp", slotp, slotp_d), ("sel", sel, sel_d),
                         ("prv", prv, prev_d), ("id8", id8, id8_d),
                         ("blnz", blnz, blnz_d), ("bdt", bdt, bdt_d),
                         ("by", by, by_d), ("bya", bya, bya_d),
                         ("h0r", h0r, h0_d), ("b1r", b1r, b1_d)]:
        p.dma("sync", lambda s, d=dst, sr=src: s.dma_start(out=d[...], in_=sr[...]),
              stream="consts", tag="ld_" + nm)
    t_rw = p.dma("sync", lambda s: s.dma_start(out=rwT[...], in_=rw_d.ap().rearrange("(a b) c -> b a c", b=128)),
                 stream="consts", tag="ld_rw")
    t_consts = p.stream_all("consts")
    t_nbdt = p.do("vector", lambda v: v.tensor_scalar(out=nbdt[:, :], in0=bdt[:, :], scalar1=-1.0,
                                                      scalar2=None, op0=OP.mult), after=[t_consts], sig=True, tag="nbdt")
    t_sent = p.do("vector", lambda v: v.memset(sentt[:, :], 1 << 20), sig=True, tag="sent")
    t_sf = p.do("vector", lambda v: v.tensor_copy(out=sentf[:, 0:NT * 2:2], in_=sentt[:, :]),
                after=[t_sent], sig=True, tag="sentf")
    t_sf = p.do("vector", lambda v: v.memset(sentf[:, 1:NT * 2:2], 0.0), after=[t_sf], sig=True, tag="sentg")
    t_pfill = p.dma("sync", lambda s: s.dma_start(out=pg_d.ap().rearrange("(a b) c -> a (b c)", a=128),
                                                  in_=sentf[:, :]), stream="pfill", after=[t_sf], tag="pfill")
    t_gfill = t_pfill

    def load_wslot(slot_t, pair, stream, after):
        toks = []
        for i, wsrc in enumerate(pair):
            toks.append(p.dma("gpsimd",
                lambda g, ii=i, ws=wsrc, sl=slot_t: g.dma_start(
                    out=sl[:, ii, :, :], in_=ws.ap().rearrange("p (a c) -> p a c", a=8)),
                stream=stream, after=after, tag=f"w_{stream}"))
        return toks

    wA_t = load_wslot(wslotA, [wdt, wlnz], "wA", [])
    w2cast_t = {}
    def emit_w2cast(e):
        if e < E and e not in w2cast_t:
            w2cast_t[e] = p.dma("gpsimd", lambda g, ee=e: g.dma_start(
                out=w2bf_d[ee], in_=w2_d[ee]), stream="w2cast", after=wA_t, tag="w2cast")
    emit_w2cast(0)
    emit_w2cast(1)

    NQ = L // 512
    tok_x1T_done = [None] * BPC
    tok_x1T_q = [[None] * NQ for _ in range(BPC)]
    tok_hT_done = [None] * BPC
    tok_logits = []
    pf_free = {id(b): T_ALL for b in (pf0, pf1, pf2, pf3, pf4, pf5)}
    pb_free = {id(b): T_ALL for b in (pb0, pb1)}
    xb_free = [T_ALL, T_ALL]
    t_prev_stats = T_ALL
    t_prev_sqrt = T_ALL
    t_x1tm_free = T_ALL
    t_x2f_free = T_ALL
    t_x2b_free = T_ALL
    t_x2T_free = T_ALL
    t_gate_free = T_ALL
    t_attnf_free = T_ALL
    t_rbuf_free = T_ALL
    t_xre_free = T_ALL

    for b in range(BPC):
        x1t_toks = []
        for q in range(NQ):
            half_toks = []
            for h2 in range(2):
                buf = h2
                rows0 = b * L + q * 512 + h2 * 256
                tile0 = rows0 // 128
                t_x = p.dma("sync", lambda s, tl=tile0, bb=buf: s.dma_start(
                    out=xbuf[:, bb, :, :], in_=xr_d[:, tl:tl + 2, :]),
                    stream=f"x{buf}", after=[xb_free[buf], t_xre_free], tag=f"xld{b}{q}{h2}")
                half_toks.append((buf, t_x))
            mu_toks = []
            for h2, (buf, t_x) in enumerate(half_toks):
                for tt2 in range(2):
                    col = h2 * 2 + tt2
                    mu_toks.append(p.do("vector", lambda v, bb=buf, t2=tt2, cc=col: v.tensor_reduce(
                        mus[:, cc:cc + 1], xbuf[:, bb, t2, :].rearrange("p d -> p () d"),
                        axis=AX.X, op=OP.add), after=[t_x, t_prev_stats], sig=True, tag="mu"))
                    mu_toks.append(p.do("scalar", lambda s, bb=buf, t2=tt2, cc=col: s.activation(
                        out=sq[:, :], in_=xbuf[:, bb, t2, :], func=AF.Square,
                        accum_out=ssq[:, cc:cc + 1]), after=[t_x, t_prev_stats], sig=True, tag="ssq"))
            tv = p.do("vector", lambda v: v.tensor_scalar(out=mus[:, :], in0=mus[:, :], scalar1=1.0 / DM,
                                                          scalar2=None, op0=OP.mult), after=mu_toks, sig=True)
            tv2 = p.do("vector", lambda v: (v.tensor_scalar(out=veps0[:, :], in0=ssq[:, :], scalar1=1.0 / DM,
                                                            scalar2=EPS, op0=OP.mult, op1=OP.add),
                                            v.tensor_mul(mm2[:, :], mus[:, :], mus[:, :]))[-1],
                       after=[tv, t_prev_sqrt], sig=True)
            tv2 = p.do("vector", lambda v: v.tensor_sub(veps[:, :], veps0[:, :], mm2[:, :]), after=[tv2], sig=True)
            tsc = p.do("scalar", lambda s: s.activation(out=sdv[:, :], in_=veps[:, :], func=AF.Sqrt),
                       after=[tv2], sig=True, tag="sqrt")
            t_prev_sqrt = tsc
            trc = p.do("vector", lambda v: v.reciprocal(rstd[:, :], sdv[:, :]), after=[tsc], sig=True)
            ap_toks = []
            for h2, (buf, t_x) in enumerate(half_toks):
                for tt2 in range(2):
                    col = h2 * 2 + tt2
                    gt_idx = q * 4 + col
                    ta = p.do("vector", lambda v, bb=buf, t2=tt2, cc=col: v.tensor_scalar(
                        out=x1tm[:, :], in0=xbuf[:, bb, t2, :], scalar1=mus[:, cc:cc + 1],
                        scalar2=rstd[:, cc:cc + 1], op0=OP.subtract, op1=OP.mult),
                        after=[trc, t_x, t_x1tm_free], sig=True, tag="lnap")
                    ap_toks.append(ta)
                    bank = (pb0, pb1)[gt_idx % 2]
                    ttr = p.do("tensor", lambda te, bk=bank: [te.transpose(
                        bk[:, c * 128:(c + 1) * 128], x1tm[:, c * 128:(c + 1) * 128], identb[:, :])
                        for c in range(8)][-1],
                        after=[ta, t_consts, pb_free[id(bank)]], sig=True, tag="x1tr")
                    t_x1tm_free = ttr
                    tcp = p.do("vector", lambda v, bk=bank, g2=gt_idx: v.tensor_copy(
                        out=x1T.ap()[:, :, g2 * 128:(g2 + 1) * 128],
                        in_=bk.ap().rearrange("p (c t) -> p c t", c=8)),
                        after=[ttr], sig=True, tag="x1cp")
                    pb_free[id(bank)] = tcp
                    x1t_toks.append(tcp)
                xb_free[buf] = ap_toks[-1]
            t_prev_stats = ap_toks[-1]
            tok_x1T_q[b][q] = x1t_toks[-1]
        tok_x1T_done[b] = x1t_toks[-1]

        scan_toks = []
        prev_scan = tok_hT_done[b - 1] if b > 0 else T_ALL
        prev_u = prev_scan
        for e in range(8):
            act_dt = []
            act_z = []
            for q in range(NQ):
                bankd = (pf0, pf1)[q % 2]
                tmm = p.do("tensor", lambda te, ee=e, qq=q, bk=bankd: [te.matmul(
                    bk[:, :], wslotA[:, 0, dc, ee * 128:(ee + 1) * 128],
                    x1T[:, dc, qq * 512:(qq + 1) * 512], start=(dc == 0), stop=(dc == 7))
                    for dc in range(8)][-1],
                    after=[tok_x1T_q[b][q], pf_free[id(bankd)]] + wA_t, sig=True, tag="dtmm")
                ta = p.do("scalar", lambda s, qq=q, bk=bankd, ee=e: s.activation(
                    out=a_s[:, qq * 512:(qq + 1) * 512], in_=bk[:, :], func=AF.Sigmoid,
                    bias=nbdt[:, ee:ee + 1], scale=-1.0), after=[tmm, t_nbdt, prev_scan], sig=True, tag="a")
                pf_free[id(bankd)] = ta
                act_dt.append(ta)
                bankz = (pf2, pf3)[q % 2]
                tmz = p.do("tensor", lambda te, ee=e, qq=q, bk=bankz: [te.matmul(
                    bk[:, :], wslotA[:, 1, dc, ee * 128:(ee + 1) * 128],
                    x1T[:, dc, qq * 512:(qq + 1) * 512], start=(dc == 0), stop=(dc == 7))
                    for dc in range(8)][-1],
                    after=[tok_x1T_q[b][q], pf_free[id(bankz)]] + wA_t, sig=True, tag="zmm")
                tz = p.do("scalar", lambda s, qq=q, bk=bankz, ee=e: s.activation(
                    out=z_s[:, qq * 512:(qq + 1) * 512], in_=bk[:, :], func=AF.Sigmoid,
                    bias=blnz[:, ee:ee + 1], scale=1.0), after=[tmz, prev_u], sig=True, tag="z")
                pf_free[id(bankz)] = tz
                act_z.append(tz)
            tu = p.do("vector", lambda v: v.tensor_mul(u_s[:, :], a_s[:, :], z_s[:, :]),
                      after=act_dt + act_z, sig=True, tag="az")
            tu = p.do("vector", lambda v: v.tensor_sub(u_s[:, :], z_s[:, :], u_s[:, :]),
                      after=[tu], sig=True, tag="u")
            tsc2 = p.do("vector", lambda v, ee=e: v.tensor_tensor_scan(
                out=hT.ap()[:, ee, :], data0=a_s[:, :], data1=u_s[:, :],
                initial=h0r[:, ee:ee + 1], op0=OP.mult, op1=OP.add),
                after=[tu, t_consts], sig=True, tag="scan")
            scan_toks.append(tsc2)
            prev_scan = tsc2
            prev_u = tu
        tok_hT_done[b] = scan_toks[-1]

        if DBG:
            p.dma("gpsimd", lambda g, bb=b: g.dma_start(out=oh_d[bb].rearrange("p (c t) -> p (c t)") if False else oh_d[bb],
                                                        in_=hT.ap().rearrange("p c t -> p (c t)")),
                  stream="dbgg", after=[tok_hT_done[b], p.etok("tensor")], tag="hdump")
            p.dma("gpsimd", lambda g, bb=b: g.dma_start(out=ox1_d[bb],
                                                        in_=x1T.ap().rearrange("p c t -> p (c t)")),
                  stream="dbgg", after=[tok_x1T_done[b], p.etok("tensor")], tag="x1dump")
        wB_t = load_wslot(wslotB, [wya, wy], "wA", [p.etok("tensor")])
        for q in range(NQ):
            rows0 = b * L + q * 512
            t_xre = p.dma("sync", lambda s, tl=rows0 // 128: s.dma_start(
                out=xre[...], in_=xr_d[:, tl:tl + 4, :]),
                stream="xre", after=[xb_free[0], xb_free[1], t_xre_free], tag="xre")
            radd_toks = []
            for e in range(8):
                bankg = (pf0, pf1)[e % 2]
                tmy = p.do("tensor", lambda te, ee=e, qq=q, bk=bankg: [te.matmul(
                    bk[:, :], wslotB[:, 0, dc, ee * 128:(ee + 1) * 128],
                    x1T[:, dc, qq * 512:(qq + 1) * 512], start=(dc == 0), stop=(dc == 7))
                    for dc in range(8)][-1],
                    after=[tok_x1T_done[b], pf_free[id(bankg)]] + wB_t, sig=True, tag="yamm")
                tg = p.do("scalar", lambda s, bk=bankg, ee=e: s.activation(
                    out=gate[:, :], in_=bk[:, :], func=AF.Silu,
                    bias=bya[:, ee:ee + 1], scale=1.0), after=[tmy, t_gate_free], sig=True, tag="gate")
                pf_free[id(bankg)] = tg
                banky = (pf0, pf1)[(e + 1) % 2]
                tmy2 = p.do("tensor", lambda te, ee=e, qq=q, bk=banky: [te.matmul(
                    bk[:, :], wslotB[:, 1, dc, ee * 128:(ee + 1) * 128],
                    hT[:, dc, qq * 512:(qq + 1) * 512], start=(dc == 0), stop=(dc == 7))
                    for dc in range(8)][-1],
                    after=[tok_hT_done[b], pf_free[id(banky)]] + wB_t, sig=True, tag="ymm")
                tat = p.do("vector", lambda v, bk=banky, ee=e: v.scalar_tensor_tensor(
                    out=attnf[:, :], in0=bk[:, :], scalar=by[:, ee:ee + 1], in1=gate[:, :],
                    op0=OP.add, op1=OP.mult), after=[tmy2, tg, t_attnf_free], sig=True, tag="attnf")
                pf_free[id(banky)] = tat
                t_gate_free = tat
                bankt = (pf2, pf3)[e % 2]
                ttr = p.do("tensor", lambda te, bk=bankt: [te.transpose(
                    bk[:, tt2 * 128:(tt2 + 1) * 128], attnf[:, tt2 * 128:(tt2 + 1) * 128], identf[:, :])
                    for tt2 in range(4)][-1],
                    after=[tat, pf_free[id(bankt)], t_consts], sig=True, tag="attntr")
                t_attnf_free = ttr
                tra = p.do("vector", lambda v, bk=bankt, ee=e: v.tensor_tensor(
                    out=rbuf.ap()[:, :, ee * 128:(ee + 1) * 128],
                    in0=bk.ap().rearrange("p (t c) -> p t c", t=4),
                    in1=xre.ap()[:, :, ee * 128:(ee + 1) * 128], op=OP.add),
                    after=[ttr, t_xre, t_rbuf_free], sig=True, tag="radd")
                pf_free[id(bankt)] = tra
                radd_toks.append(tra)
            t_xre_free = radd_toks[-1]
            t_rw2 = p.dma("sync", lambda s, r0=rows0: s.dma_start(
                out=out_d[r0:r0 + 512, :].rearrange("(t p) d -> p t d", p=128), in_=rbuf[...]),
                stream="rout", after=radd_toks, tag="rout")
            if DBG:
                t_rw2b = p.dma("sync", lambda s, r0=rows0: s.dma_start(
                    out=or_d[r0:r0 + 512, :].rearrange("(t p) d -> p t d", p=128), in_=rbuf[...]),
                    stream="rout", after=radd_toks, tag="routdbg")
            mu_toks = []
            for tt2 in range(4):
                mu_toks.append(p.do("vector", lambda v, t2=tt2: v.tensor_reduce(
                    mus[:, t2:t2 + 1], rbuf[:, t2, :].rearrange("p d -> p () d"),
                    axis=AX.X, op=OP.add), after=radd_toks + [t_prev_stats], sig=True, tag="mu2"))
                mu_toks.append(p.do("scalar", lambda s, t2=tt2: s.activation(
                    out=sq[:, :], in_=rbuf[:, t2, :], func=AF.Square,
                    accum_out=ssq[:, t2:t2 + 1]), after=radd_toks + [t_prev_stats], sig=True, tag="ssq2"))
            tv = p.do("vector", lambda v: v.tensor_scalar(out=mus[:, :], in0=mus[:, :], scalar1=1.0 / DM,
                                                          scalar2=None, op0=OP.mult), after=mu_toks, sig=True)
            tv2 = p.do("vector", lambda v: (v.tensor_scalar(out=veps0[:, :], in0=ssq[:, :], scalar1=1.0 / DM,
                                                            scalar2=EPS, op0=OP.mult, op1=OP.add),
                                            v.tensor_mul(mm2[:, :], mus[:, :], mus[:, :]))[-1],
                       after=[tv, t_prev_sqrt], sig=True)
            tv2 = p.do("vector", lambda v: v.tensor_sub(veps[:, :], veps0[:, :], mm2[:, :]), after=[tv2], sig=True)
            tsc = p.do("scalar", lambda s: s.activation(out=sdv[:, :], in_=veps[:, :], func=AF.Sqrt),
                       after=[tv2], sig=True)
            t_prev_sqrt = tsc
            trc = p.do("vector", lambda v: v.reciprocal(rstd[:, :], sdv[:, :]), after=[tsc], sig=True)
            x2toks = []
            for tt2 in range(4):
                gt_idx = b * 16 + q * 4 + tt2
                rows_t = rows0 + tt2 * 128
                tf = p.do("vector", lambda v, t2=tt2: v.tensor_scalar(
                    out=x2f[:, :], in0=rbuf[:, t2, :], scalar1=mus[:, t2:t2 + 1],
                    scalar2=rstd[:, t2:t2 + 1], op0=OP.subtract, op1=OP.mult),
                    after=[trc, t_x2f_free], sig=True, tag="ln2ap")
                tb2 = p.do("vector", lambda v: v.tensor_copy(out=x2b[:, :], in_=x2f[:, :]),
                           after=[tf, t_x2b_free], sig=True, tag="x2bf")
                t_x2w2 = p.dma("sync", lambda s, r0=rows_t: s.dma_start(
                    out=x2_d[r0:r0 + 128, :].rearrange("(t p) d -> p (t d)", p=128), in_=x2b[:, :]),
                    stream="x2w", after=[tb2], tag="x2w")
                if DBG:
                    t_x2w3 = p.dma("sync", lambda s, r0=rows_t: s.dma_start(
                        out=ox2_d[r0:r0 + 128, :].rearrange("(t p) d -> p (t d)", p=128), in_=x2f[:, :]),
                        stream="x2w", after=[tf], tag="x2wdbg")
                    tb2 = (tb2[0], max(tb2[1], t_x2w3[1])) if tb2[0] == t_x2w3[0] else tb2
                t_x2b_free = t_x2w2
                rcp = []
                for half in range(2):
                    bk = (pf4, pf5)[half]
                    t1 = p.do("tensor", lambda te, hh=half, bk2=bk: [te.transpose(
                        bk2[:, c2 * 128:(c2 + 1) * 128], x2f[:, (hh * 4 + c2) * 128:(hh * 4 + c2 + 1) * 128],
                        identf[:, :]) for c2 in range(4)][-1],
                        after=[tf, pf_free[id(bk)], t_consts, t_x2T_free], sig=True, tag="x2tr")
                    t2 = p.do("vector", lambda v, hh=half, bk2=bk: v.tensor_copy(
                        out=x2T.ap()[:, hh * 4:(hh + 1) * 4, :],
                        in_=bk2.ap().rearrange("p (c t) -> p c t", c=4)),
                        after=[t1], sig=True, tag="x2cp")
                    pf_free[id(bk)] = t2
                    rcp.append(t2)
                t_x2f_free = rcp[-1]
                bkr = (pf2, pf3)[tt2 % 2]
                tr2 = p.do("tensor", lambda te, bk=bkr: [te.matmul(
                    bk[:, 0:8], x2T[:, dc, :], rwT[:, dc, :], start=(dc == 0), stop=(dc == 7))
                    for dc in range(8)][-1],
                    after=rcp + [t_rw, pf_free[id(bkr)]], sig=True, tag="routmm")
                t_x2T_free = tr2
                tl = p.do("vector", lambda v, g2=gt_idx, bk=bkr: v.tensor_copy(
                    out=logits[:, g2 * 8:(g2 + 1) * 8], in_=bk[:, 0:8]),
                    after=[tr2], sig=True, tag="logcp")
                pf_free[id(bkr)] = tl
                tok_logits.append(tl)
                x2toks.append(tl)
            t_rbuf_free = x2toks[-1]
        if b == 0:
            wA_t = load_wslot(wslotA, [wdt, wlnz], "wA", [p.etok("tensor")])

    t_bar = [p.etok("vector"), p.etok("scalar"), p.etok("tensor"),
             p.stream_all("rout"), p.stream_all("x2w"), p.stream_all("xre"),
             p.stream_all("x0"), p.stream_all("x1")]

    W3 = lambda x2_: x2_.ap().rearrange("p (t e) -> p t e", e=E)
    bc3 = lambda a2: a2.ap().rearrange("p t -> p t ()").to_broadcast([128, TILES, E])
    t_log = tok_logits[-1]
    tv = p.do("vector", lambda v: v.tensor_reduce(mrow[:, :], W3(logits), axis=AX.X, op=OP.max),
              after=[t_log], sig=True, tag="smax")
    tv = p.do("vector", lambda v: v.tensor_tensor(out=W3(exps), in0=W3(logits), in1=bc3(mrow), op=OP.subtract),
              after=[tv], sig=True)
    te_ = p.do("scalar", lambda s: s.activation(out=exps[:, :], in_=exps[:, :], func=AF.Exp), after=[tv], sig=True)
    tv = p.do("vector", lambda v: v.tensor_reduce(zrow[:, :], W3(exps), axis=AX.X, op=OP.add), after=[te_], sig=True)
    tv = p.do("vector", lambda v: v.reciprocal(rz[:, :], zrow[:, :]), after=[tv], sig=True)
    tv = p.do("vector", lambda v: v.tensor_tensor(out=W3(prob), in0=W3(exps), in1=bc3(rz), op=OP.mult),
              after=[tv], sig=True)
    tv = p.do("vector", lambda v: v.tensor_reduce(mxp[:, :], W3(prob), axis=AX.X, op=OP.max), after=[tv], sig=True)
    tv = p.do("vector", lambda v: v.tensor_tensor(out=W3(igx), in0=W3(prob), in1=bc3(mxp), op=OP.is_ge),
              after=[tv], sig=True)
    tv = p.do("vector", lambda v: v.scalar_tensor_tensor(out=eqm[:, :], in0=igx[:, :], scalar=-1000.0,
                                                         in1=ioe[:, :], op0=OP.mult, op1=OP.add),
              after=[tv, t_consts], sig=True)
    tv = p.do("vector", lambda v: v.tensor_reduce(idxm[:, :], W3(eqm), axis=AX.X, op=OP.min), after=[tv], sig=True)
    t_idx = p.do("vector", lambda v: v.tensor_scalar(out=idxa[:, :], in0=idxm[:, :], scalar1=1000.0,
                                                     scalar2=None, op0=OP.add), after=[tv], sig=True)
    t_oh = p.do("vector", lambda v: v.tensor_tensor(out=W3(ohb), in0=W3(ioe), in1=bc3(idxa), op=OP.is_equal),
                after=[t_idx], sig=True)
    t_ohf = p.do("vector", lambda v: v.tensor_tensor(out=W3(ohf), in0=W3(ioe), in1=bc3(idxa), op=OP.is_equal),
                 after=[t_idx], sig=True)
    t_cnt = p.do("tensor", lambda t: t.matmul(pf2[:, 0:256], triu[:, :], ohb[:, :], start=True, stop=True),
                 after=[t_oh, t_consts, pf_free[id(pf2)]], sig=True, tag="cntmm")
    t_totm = p.do("tensor", lambda t: t.matmul(pf3[0:1, 0:256], on128[:, :], ohb[:, :], start=True, stop=True),
                  after=[t_oh, t_consts, pf_free[id(pf3)]], sig=True, tag="totmm")
    t_tot = p.do("vector", lambda v: v.tensor_copy(out=tot[:, :], in_=pf3[0:1, 0:256]), after=[t_totm], sig=True)
    pf_free[id(pf3)] = t_tot
    tv = p.do("vector", lambda v: v.memset(offa[:, :], 0.0), after=[t_tot], sig=True)
    tv = p.do("vector", lambda v: v.tensor_copy(out=offa[0:1, E:], in_=tot[0:1, 0:TE - E]), after=[tv], sig=True)
    cur, nxt = offa, offb
    d = E
    while d < TE:
        tv = p.do("vector", lambda v, c=cur, n=nxt, dd=d: (
            v.tensor_copy(out=n[0:1, 0:dd], in_=c[0:1, 0:dd]),
            v.tensor_add(n[0:1, dd:], c[0:1, dd:], c[0:1, 0:TE - dd]))[-1], after=[tv], sig=True)
        cur, nxt = nxt, cur
        d *= 2
    offfin = cur
    t_pre = tv
    t_cnts = p.do("vector", lambda v: v.tensor_reduce(cnts[:, :], tot.ap().rearrange("a (t e) -> a e t", e=E),
                                                      axis=AX.X, op=OP.add), after=[t_tot], sig=True, tag="cnts")
    t_bcm = p.do("tensor", lambda t: t.matmul(pf4[:, 0:256], onr[0:1, :], offfin[0:1, 0:256],
                                              start=True, stop=True),
                 after=[t_pre, t_consts, pf_free[id(pf4)]], sig=True, tag="bcmm")
    tv = p.do("vector", lambda v: v.tensor_copy(out=offbc[:, :], in_=pf4[:, 0:256]), after=[t_bcm], sig=True)
    pf_free[id(pf4)] = tv
    tv = p.do("vector", lambda v: v.tensor_tensor(out=rt1[:, :], in0=pf2[:, 0:256], in1=offbc[:, :], op=OP.add),
              after=[tv, t_cnt], sig=True)
    pf_free[id(pf2)] = tv
    tv = p.do("vector", lambda v: v.tensor_mul(rt2[:, :], rt1[:, :], ohf[:, :]), after=[tv, t_ohf], sig=True)
    tv = p.do("vector", lambda v: v.tensor_reduce(pos0[:, :], W3(rt2), axis=AX.X, op=OP.add), after=[tv], sig=True)
    t_pos = p.do("vector", lambda v: v.tensor_scalar(out=pos[:, :], in0=pos0[:, :], scalar1=-1.0,
                                                     scalar2=None, op0=OP.add), after=[tv], sig=True, tag="pos")
    tv = p.do("vector", lambda v: (v.scalar_tensor_tensor(out=slot[:, :], in0=idxa[:, :], scalar=float(CL),
                                                          in1=pos[:, :], op0=OP.mult, op1=OP.add),
                                   v.tensor_scalar(out=ovf[:, :], in0=pos[:, :], scalar1=float(CL),
                                                   scalar2=None, op0=OP.is_ge))[-1], after=[t_pos], sig=True)
    tv = p.do("vector", lambda v: v.scalar_tensor_tensor(out=slot[:, :], in0=ovf[:, :], scalar=SENT,
                                                         in1=slot[:, :], op0=OP.mult, op1=OP.add),
              after=[tv], sig=True)
    tv = p.do("vector", lambda v: v.tensor_copy(out=sloti[:, :], in_=slot[:, :]), after=[tv], sig=True)
    tv = p.do("vector", lambda v: (v.tensor_scalar(out=mgi[:, :], in0=sloti[:, :], scalar1=127, scalar2=None,
                                                   op0=OP.bitwise_and),
                                   v.tensor_scalar(out=dvi[:, :], in0=sloti[:, :], scalar1=7, scalar2=None,
                                                   op0=OP.arith_shift_right))[-1], after=[tv], sig=True)
    tv = p.do("vector", lambda v: v.tensor_scalar(out=mgi[:, :], in0=mgi[:, :], scalar1=NT, scalar2=None,
                                                  op0=OP.mult), after=[tv], sig=True)
    t_tgt = p.do("vector", lambda v: v.tensor_tensor(out=tgt[:, :], in0=mgi[:, :], in1=dvi[:, :], op=OP.add),
                 after=[tv], sig=True, tag="tgt")
    t_pgv = p.do("vector", lambda v: (v.tensor_copy(out=pgv[:, 0:TILES * 2:2], in_=tokidf[:, :]),
                                      v.tensor_copy(out=pgv[:, 1:TILES * 2:2], in_=mxp[:, :]))[-1],
                 after=[t_consts, tok_logits[-1], (lambda: None, 0) if False else ("vector", p.chan["vector"][1])], sig=True, tag="pgv")
    for t in range(TILES):
        p.dma("gpsimd", lambda g, tt2=t: g.indirect_dma_start(
            out=pg_d.ap(), out_offset=IndirectOffsetOnAxis(tgt[:, tt2:tt2 + 1], 0),
            in_=pgv[:, 2 * tt2:2 * tt2 + 2], in_offset=None, bounds_check=reg_ns, oob_is_err=False),
            stream="pack0", after=[t_tgt, t_pgv, t_pfill, t_regs], tag="scat")
    sc_all = p.stream_all("pack0")
    p.dma("gpsimd", lambda g: g.dma_start(out=sentf[:, :], in_=pg_d.ap().rearrange("(a b) c -> a (b c)", a=128)),
          stream="rb", after=[sc_all], tag="pgrb")
    t_rb0 = p.stream_all("rb")
    t_rb1 = p.do("vector", lambda v: v.tensor_copy(out=ptok[:, :], in_=sentf[:, 0:NT * 2:2]),
                 after=[t_rb0], sig=True, tag="ptokcast")
    t_rb2 = p.do("vector", lambda v: v.tensor_copy(out=gsb[:, :], in_=sentf[:, 1:NT * 2:2]),
                 after=[t_rb0], sig=True, tag="gcast")
    t_rb = ("vector", max(t_rb1[1], t_rb2[1]))
    t_m = p.do("tensor", lambda t: t.matmul(pf5[0:8, 0:8], cnts[0:1, :], sel[0:1, :], start=True, stop=True),
               after=[t_cnts, t_consts, pf_free[id(pf5)]], sig=True, tag="ccmm")
    t_msb = p.do("vector", lambda v: v.tensor_copy(out=msb[:, :], in_=pf5[0:8, 0:8]), after=[t_m], sig=True)
    pf_free[id(pf5)] = t_msb
    t_ccw = p.dma("gpsimd", lambda g: g.dma_start(out=cc_i[:, :], in_=msb[:, :]), stream="cc", after=[t_msb])
    t_cc = p.do("gpsimd", lambda g: g.collective_compute("AllReduce", OP.add, replica_groups=[list(range(NCORES))],
                                                         ins=[cc_i.ap().opt()], outs=[cc_o.ap().opt()]),
                after=[t_ccw], sig=True, tag="cc")
    t_ccr = p.dma("gpsimd", lambda g: g.dma_start(out=ccs[:, :], in_=cc_o[:, :]), stream="cc", after=[t_cc])
    tv2 = p.do("vector", lambda v: v.tensor_mul(ccm[:, :], ccs[:, :], prv[:, :]), after=[t_ccr, t_consts], sig=True)
    tv2 = p.do("vector", lambda v: v.tensor_reduce(offcol[:, :], ccm[:, :], axis=AX.X, op=OP.add), after=[tv2], sig=True)
    t_row = p.do("tensor", lambda t: t.matmul(pf5[0:1, 0:8], offcol[:, :], id8[:, :], start=True, stop=True),
                 after=[tv2, t_consts, pf_free[id(pf5)]], sig=True)
    t_orow = p.do("vector", lambda v: v.tensor_copy(out=offrow[:, :], in_=pf5[0:1, 0:8]), after=[t_row], sig=True)
    pf_free[id(pf5)] = t_orow
    t_thrm = p.do("tensor", lambda t: t.matmul(pf5[:, 0:8], onr[0:1, :], offrow[0:1, :], start=True, stop=True),
                  after=[t_orow, pf_free[id(pf5)]], sig=True)
    t_thr = p.do("vector", lambda v: v.tensor_scalar(out=thr[:, :], in0=pf5[:, 0:8], scalar1=-1.0,
                                                     scalar2=float(CCAP), op0=OP.mult, op1=OP.add),
                 after=[t_thrm], sig=True, tag="thr")
    pf_free[id(pf5)] = t_thr

    def gmask(v):
        last = None
        for e in range(E):
            cols = slice(e * CPT, (e + 1) * CPT)
            last = v.tensor_tensor(out=gm[:, cols], in0=slotp[:, cols],
                                   in1=thr[:, e:e + 1].to_broadcast([128, CPT]), op=OP.is_lt)
        return last
    t_gm = p.do("vector", gmask, after=[t_thr, t_consts], sig=True, tag="gm")
    t_gfix = p.do("vector", lambda v: v.tensor_mul(gsb2[:, :], gsb[:, :], gm[:, :]),
                  after=[t_gm, t_rb], sig=True, tag="gfix")

    t_x2w = p.stream_all("x2w")
    t_gz = p.do("vector", lambda v: v.memset(gtb[:, :, :], 0.0), after=t_bar, sig=True, tag="gz")
    xeT_free = [t_bar[2], t_bar[2]]
    w1_free = [t_bar[2], t_bar[2]]
    eo_free = [t_bar[0], t_bar[0]]
    xeT_tok = [None, None]
    t_rout_all = p.stream_all("rout")

    pf_state = {}

    def prefetch(e):
        par = e % 2
        st = {}
        ga = []
        for t in range(CPT):
            ga.append(p.dma("gpsimd", lambda g, tt2=t, ee=e: g.indirect_dma_start(
                out=gtb[:, tt2, :], out_offset=None, in_=x2_d.ap(),
                in_offset=IndirectOffsetOnAxis(ptok[:, ee * CPT + tt2:ee * CPT + tt2 + 1], 0),
                bounds_check=reg_nt, oob_is_err=False),
                stream=f"gx{par}", after=[t_rb, t_gz, t_x2w, pf_state.get("gt_free", T_ALL), t_regs], tag="gx"))
        st["ga_all"] = p.stream_all(f"gx{par}")
        emit_w2cast(e + 2)
        st["w1_pre"] = {}
        for fw in (0, 1):
            st["w1_pre"][fw] = p.dma("gpsimd", lambda g, ee=e, ff=fw: g.dma_start(
                out=w1win[ff % 2][...], in_=w1_d[ee, ff].rearrange("p (a c) -> p a c", a=8)),
                stream=f"w1{fw % 2}", after=[w1_free[fw % 2]], tag="w1pre")
        return st

    def prefetch_w2(e, st):
        st["b2"] = p.dma("gpsimd", lambda g, ee=e: g.dma_start(out=b2bf[:, :], in_=b2_d[ee:ee + 1, :]),
                         stream="b2", after=[pf_state.get("prev_eo", t_bar[2])], tag="b2")
        st["w2"] = p.dma("scalar", lambda s, ee=e: s.dma_start(
            out=w2sb[...], in_=w2bf_d[ee].rearrange("p (a c) -> p a c", a=32)),
            stream="w2", after=[pf_state.get("prev_eo", t_bar[2]), w2cast_t[e]], tag="w2")

    def compute_trans(e, st):
        par = e % 2
        trs = []
        for t in range(CPT):
            bank = (pb0, pb1)[t % 2]
            t1 = p.do("tensor", lambda te, tt2=t, bk=bank: [te.transpose(
                bk[:, c * 128:(c + 1) * 128], gtb[:, tt2, c * 128:(c + 1) * 128], identb[:, :])
                for c in range(8)][-1],
                after=[st["ga_all"], pb_free[id(bank)], t_consts, xeT_free[par]], sig=True, tag="xetr")
            t2 = p.do("vector", lambda v, tt2=t, bk=bank, pp=par: v.tensor_copy(
                out=xeT[pp].ap()[:, :, tt2 * 128:(tt2 + 1) * 128],
                in_=bk.ap().rearrange("p (c t) -> p c t", c=8)),
                after=[t1], sig=True, tag="xecp")
            pb_free[id(bank)] = t2
            trs.append(t2)
        pf_state["gt_free"] = trs[-1]
        xeT_tok[par] = trs[-1]

    def compute_h(e, st):
        par = e % 2
        h_toks = []
        for fw in range(8):
            wpar = fw % 2
            if fw in st["w1_pre"]:
                t_w1 = st["w1_pre"][fw]
            else:
                t_w1 = p.dma("gpsimd", lambda g, ee=e, ff=fw: g.dma_start(
                    out=w1win[ff % 2][...], in_=w1_d[ee, ff].rearrange("p (a c) -> p a c", a=8)),
                    stream=f"w1{wpar}", after=[w1_free[wpar]], tag="w1")
            for fc in range(4):
                for ci, (c0, cn) in enumerate([(0, 512), (512, CL - 512)]):
                    bank = (pf2, pf3)[(fc * 2 + ci) % 2]
                    tm = p.do("tensor", lambda te, pp=wpar, f2=fc, cc0=c0, ccn=cn, bk=bank, pr=par: [te.matmul(
                        bk[:, 0:ccn], w1win[pp][:, dc, f2 * 128:(f2 + 1) * 128],
                        xeT[pr][:, dc, cc0:cc0 + ccn], start=(dc == 0), stop=(dc == 7))
                        for dc in range(8)][-1],
                        after=[t_w1, xeT_tok[par], pf_free[id(bank)]], sig=True, tag="hmm")
                    th = p.do("scalar", lambda s, ff=fw, f2=fc, cc0=c0, ccn=cn, bk=bank, ee=e: s.activation(
                        out=h_bf[:, ff * 4 + f2, cc0:cc0 + ccn], in_=bk[:, 0:ccn], func=AF.Gelu,
                        bias=b1r[:, ee * 32 + ff * 4 + f2:ee * 32 + ff * 4 + f2 + 1], scale=1.0),
                        after=[tm, t_consts], sig=True, tag="gelu")
                    pf_free[id(bank)] = th
                    h_toks.append(th)
            w1_free[wpar] = h_toks[-1]
        return h_toks[-1]

    def compute_eo(e, st, t_hlast):
        par = e % 2
        last_mm = None
        for tt2 in range(CPT):
            epar = tt2 % 2
            evs = []
            for dh in range(2):
                bank = (pf0, pf1)[dh]
                def eomm(te, t2=tt2, dd=dh, bk=bank):
                    te.matmul(bk[:, :], onrb[0:1, :], b2bf[0:1, dd * 512:(dd + 1) * 512],
                              start=True, stop=False)
                    last = None
                    for fc in range(32):
                        last = te.matmul(bk[:, :], h_bf[:, fc, t2 * 128:(t2 + 1) * 128],
                                         w2sb[:, fc, dd * 512:(dd + 1) * 512],
                                         start=False, stop=(fc == 31))
                    return last
                tm = p.do("tensor", eomm,
                          after=[st["w2"], st["b2"], pf_free[id(bank)], eo_free[epar], t_hlast, t_consts],
                          sig=True, tag="eomm")
                last_mm = tm
                tev = p.do("vector", lambda v, t2=tt2, dd=dh, bk=bank, ee=e, pp=epar: v.tensor_scalar(
                    out=eob[pp][:, dd * 512:(dd + 1) * 512], in0=bk[:, :],
                    scalar1=gsb2[:, ee * CPT + t2:ee * CPT + t2 + 1], scalar2=None, op0=OP.mult),
                    after=[tm, t_gfix], sig=True, tag="eoev")
                pf_free[id(bank)] = tev
                evs.append(tev)
            t_sa = p.dma("gpsimd", lambda g, t2=tt2, ee=e, pp=epar: g.indirect_dma_start(
                out=out_d.ap(), out_offset=IndirectOffsetOnAxis(ptok[:, ee * CPT + t2:ee * CPT + t2 + 1], 0),
                in_=eob[pp][:, :], in_offset=None, bounds_check=reg_nt, oob_is_err=False,
                compute_op=OP.add), stream="sa",
                after=evs + [t_rb, t_rout_all], tag="sa")
            eo_free[epar] = t_sa
        pf_state["prev_eo"] = last_mm
        xeT_free[par] = last_mm

    st_cur = prefetch(0)
    prefetch_w2(0, st_cur)
    for e in range(E):
        compute_trans(e, st_cur)
        t_hlast = compute_h(e, st_cur)
        st_next = prefetch(e + 1) if e + 1 < E else None
        compute_eo(e, st_cur, t_hlast)
        if st_next is not None:
            prefetch_w2(e + 1, st_next)
        st_cur = st_next

    t_sa_all = p.stream_all("sa")
    dbg_all = p.stream_all("dbg") if DBG else (None, 0)
    dbgg_all = p.stream_all("dbgg") if DBG else (None, 0)
    p.final_waits("gpsimd", [t_sa_all, t_rout_all, dbg_all, dbgg_all])
    p.final_waits("sync", [t_sa_all, t_rout_all])
    p.verify()
    with nc.Block() as block:
        p.build(block)
    ctx.close()
    return nc


_NC_CACHE = None


def kernel(**inputs):
    global _NC_CACHE
    xs = np.asarray(inputs["L0_attn_x"], np.float32).reshape(NCORES, NTOK, DM)
    xrs = np.ascontiguousarray(xs.reshape(NCORES, TILES, 128, DM).transpose(0, 2, 1, 3))
    def wslotr(w):
        return np.ascontiguousarray(
            np.asarray(w, np.float32).T.reshape(8, 128, DM).transpose(1, 0, 2).reshape(128, 8 * DM))
    wlnzT = wslotr(inputs["L0_attn_w_ln_z"])
    wdtT = wslotr(inputs["L0_attn_w_dt"])
    wyT = wslotr(inputs["L0_attn_w_y"])
    wyaT = wslotr(inputs["L0_attn_w_ya"])
    routerT = np.ascontiguousarray(np.asarray(inputs["L0_ffn_router_w"], np.float32).T)
    arr = lambda v: np.ascontiguousarray(np.asarray(v, np.float32))
    col8 = lambda v: np.ascontiguousarray(np.asarray(v, np.float32).reshape(8, 128).T)
    blnz = col8(inputs["L0_attn_b_ln_z"]); bdt = col8(inputs["L0_attn_b_dt"])
    by = col8(inputs["L0_attn_b_y"]); bya = col8(inputs["L0_attn_b_ya"])
    h0r = col8(inputs["L0_attn_h0"])
    w1 = np.ascontiguousarray(np.asarray(inputs["L0_ffn_w1"], np.float32)
                              .reshape(E, 8, 128, 8, 512).transpose(0, 3, 2, 1, 4).reshape(E, 8, 128, 4096))
    w2 = np.ascontiguousarray(np.asarray(inputs["L0_ffn_w2"], np.float32)
                              .reshape(E, 32, 128, DM).transpose(0, 2, 1, 3).reshape(E, 128, 32 * DM))
    b1r = np.ascontiguousarray(np.asarray(inputs["L0_ffn_b1"], np.float32).reshape(E, 32, 128).transpose(2, 0, 1).reshape(128, E * 32))
    b2n = arr(inputs["L0_ffn_b2"])

    identb = np.eye(128, dtype=np.float32).astype(bfd)
    identf = np.eye(128, dtype=np.float32)
    triu = np.triu(np.ones((128, 128), np.float32)).astype(bfd)
    ones128 = np.ones((128, 1), np.float32).astype(bfd)
    onesrow = np.ones((1, 128), np.float32)
    onesrowb = onesrow.astype(bfd)
    iotae = np.tile(np.arange(E, dtype=np.float32), (128, TILES))
    tokidi = (np.arange(TILES)[None, :] * 128 + np.arange(128)[:, None]).astype(np.int32)
    tokidf = tokidi.astype(np.float32)
    slotpos = np.zeros((128, NT), np.float32)
    for e in range(E):
        for t in range(CPT):
            slotpos[:, e * CPT + t] = t * 128 + np.arange(128)
    ident8 = np.eye(8, dtype=np.float32)

    if _NC_CACHE is None:
        _NC_CACHE = build_program()
    nc = _NC_CACHE

    in_maps = []
    for k in range(NCORES):
        selrow = np.zeros((1, 8), np.float32); selrow[0, k] = 1.0
        prevmask = np.zeros((8, 8), np.float32); prevmask[:, :k] = 1.0
        in_maps.append({
            "x": xs[k], "xr": xrs[k], "wlnzT": wlnzT, "wdtT": wdtT, "wyT": wyT, "wyaT": wyaT,
            "blnz": blnz, "bdt": bdt, "by": by, "bya": bya, "h0r": h0r,
            "routerT": routerT, "w1r": w1, "b1r": b1r, "w2r": w2, "b2n": b2n,
            "identb": identb, "identf": identf, "triu": triu,
            "ones128": ones128, "onesrow": onesrow, "onesrowb": onesrowb,
            "iotae": iotae, "tokidi": tokidi, "tokidf": tokidf, "slotpos": slotpos,
            "selrow": selrow, "prevmask": prevmask, "ident8": ident8,
        })
    res = run_bass_kernel_spmd(nc, in_maps, core_ids=list(range(NCORES)))
    out = np.concatenate([np.asarray(res.results[k]["out"], np.float32)[None] for k in range(NCORES)], 0)
    return out.reshape(16, 2048, 1024)


# version 10
# speedup vs baseline: 1.1422x; 1.0592x over previous
import numpy as np
import ml_dtypes
DBG = False
import concourse.bass as bass
import concourse.mybir as mybir
from concourse.bass_utils import run_bass_kernel_spmd
from concourse.bass import IndirectOffsetOnAxis
from contextlib import ExitStack

F32 = mybir.dt.float32
BF16 = mybir.dt.bfloat16
I32 = mybir.dt.int32
AF = mybir.ActivationFunctionType
OP = mybir.AluOpType
AX = mybir.AxisListType
bfd = ml_dtypes.bfloat16

NCORES = 8
BPC = 2
L = 2048
DM = 1024
DF = 4096
E = 8
CL = 768
NTOK = BPC * L
TILES = NTOK // 128
NS = E * CL
NT = NS // 128
CPT = CL // 128
CCAP = 6144
SENT = float(1 << 21)
EPS = 1e-5

ENGINES = ("sync", "vector", "scalar", "tensor", "gpsimd")
QCH = {"sync": "qs", "scalar": "qa", "gpsimd": "qg", "vector": "qv", "tensor": "qt"}


class Prog:
    def __init__(self, nc, ctx):
        self.nc = nc
        self.ctx = ctx
        self.chan = {}
        for ch in list(ENGINES) + list(QCH.values()):
            self.chan[ch] = [ctx.enter_context(nc.semaphore("sem_" + ch)), 0]
        self.steps = {e: [] for e in ENGINES}
        self.watermark = {e: {} for e in ENGINES}

    def stream(self, name):
        ch = "#" + name
        if ch not in self.chan:
            self.chan[ch] = [self.ctx.enter_context(self.nc.semaphore("s" + name)), 0]
        return ch

    def _queue(self, engine, fn, after, sem_ch, inc, tag):
        waits = {}
        for c, v in after:
            if v > 0 and waits.get(c, 0) < v:
                waits[c] = v
        tok = None
        if sem_ch is not None:
            sem, cnt = self.chan[sem_ch]
            cnt += inc
            self.chan[sem_ch][1] = cnt
            tok = (sem_ch, cnt)
        wm = self.watermark[engine]
        needed = []
        for c, v in sorted(waits.items()):
            if wm.get(c, 0) < v:
                needed.append((c, v))
                wm[c] = v
        self.steps[engine].append(
            (needed, fn, self.chan[sem_ch][0] if tok else None, inc if tok else 0, tag))
        return tok

    def do(self, engine, fn, after=(), sig=False, tag=""):
        return self._queue(engine, fn, after, engine if sig else None, 1, tag)

    def dma(self, engine, fn, stream, after=(), tag=""):
        return self._queue(engine, fn, after, self.stream(stream), 16, tag)

    def stream_all(self, name):
        ch = "#" + name
        return (ch, self.chan[ch][1])

    def etok(self, engine):
        return (engine, self.chan[engine][1])

    def final_waits(self, engine, tokens):
        waits = {}
        for ch, v in tokens:
            if ch is None or v <= 0:
                continue
            waits[ch] = max(waits.get(ch, 0), v)
        self.steps[engine].append((sorted(waits.items()), lambda e: None, None, 0, "final"))

    def verify(self):
        pos = {e: 0 for e in ENGINES}
        counts = {ch: 0 for ch in self.chan}
        sem2ch = {id(h): ch for ch, (h, _) in self.chan.items()}
        progressed = True
        while progressed:
            progressed = False
            for e in ENGINES:
                while pos[e] < len(self.steps[e]):
                    needed, fn, sem, inc, tag = self.steps[e][pos[e]]
                    if all(counts[ch] >= v for ch, v in needed):
                        if inc:
                            counts[sem2ch[id(sem)]] += inc
                        pos[e] += 1
                        progressed = True
                    else:
                        break
        stuck = [(e, pos[e], self.steps[e][pos[e]][0], self.steps[e][pos[e]][4])
                 for e in ENGINES if pos[e] < len(self.steps[e])]
        if stuck:
            msg = "; ".join(f"{e}@{i} tag={t!r} needs {n}" for e, i, n, t in stuck)
            raise RuntimeError("schedule deadlock: " + msg)

    def build(self, block):
        chans = self.chan

        def emit_for(name):
            def run(eng):
                for needed, fn, sem, inc, tag in self.steps[name]:
                    for ch, v in needed:
                        eng.wait_ge(chans[ch][0], v)
                    inst = fn(eng)
                    if inc:
                        assert inst is not None, f"sig step returned None: {tag}"
                        inst.then_inc(sem, inc)
            return run

        for name in ENGINES:
            if self.steps[name]:
                getattr(block, name)(emit_for(name))



class View:
    def __init__(self, ap_):
        self._ap = ap_
    def __getitem__(self, k):
        return self._ap[k]
    def ap(self):
        return self._ap


def build_program():
    nc = bass.Bass("TRN2", target_bir_lowering=False, detect_race_conditions=False)
    DI = lambda nm, shp, dt=F32: nc.dram_tensor(nm, shp, dt, kind="ExternalInput")

    x_d = DI("x", [NTOK, DM])
    xr_d = DI("xr", [128, TILES, DM])
    wlnz = DI("wlnzT", [128, 8 * DM]);  wdt = DI("wdtT", [128, 8 * DM])
    wy = DI("wyT", [128, 8 * DM]);      wya = DI("wyaT", [128, 8 * DM])
    blnz_d = DI("blnz", [128, 8]); bdt_d = DI("bdt", [128, 8])
    by_d = DI("by", [128, 8]);     bya_d = DI("bya", [128, 8])
    h0_d = DI("h0r", [128, 8])
    rw_d = DI("routerT", [DM, E])
    w1_d = DI("w1r", [E, 8, 128, 4096]);  b1_d = DI("b1r", [128, E * 32])
    w2_d = DI("w2r", [E, 128, 32 * DM]);  b2_d = DI("b2n", [E, DM])
    identb_d = DI("identb", [128, 128], BF16)
    identf_d = DI("identf", [128, 128])
    triu_d = DI("triu", [128, 128], BF16)
    on128_d = DI("ones128", [128, 1], BF16)
    onr_d = DI("onesrow", [1, 128])
    onrb_d = DI("onesrowb", [1, 128], BF16)
    ioe_d = DI("iotae", [128, TILES * E])
    tokid_d = DI("tokidi", [128, TILES], I32)
    tokidf_d = DI("tokidf", [128, TILES])
    slotp_d = DI("slotpos", [128, NT])
    sel_d = DI("selrow", [1, 8])
    prev_d = DI("prevmask", [8, 8])
    id8_d = DI("ident8", [8, 8])

    out_d = nc.dram_tensor("out", [NTOK, DM], F32, kind="ExternalOutput")
    if DBG:
        or_d = nc.dram_tensor("o_r", [NTOK, DM], F32, kind="ExternalOutput")
        oh_d = nc.dram_tensor("o_hT", [BPC, 128, 8 * L], F32, kind="ExternalOutput")
        ox1_d = nc.dram_tensor("o_x1T", [BPC, 128, 8 * L], F32, kind="ExternalOutput")
        olg_d = nc.dram_tensor("o_lg", [128, TILES * E], F32, kind="ExternalOutput")
        ox2_d = nc.dram_tensor("o_x2", [NTOK, DM], F32, kind="ExternalOutput")
        og_d = nc.dram_tensor("o_gs", [128, NT], F32, kind="ExternalOutput")
        opt_d = nc.dram_tensor("o_pt", [128, NT], F32, kind="ExternalOutput")
    x2_d = nc.dram_tensor("x2sc", [NTOK, DM], BF16)
    pg_d = nc.dram_tensor("pgd", [NS, 2], F32)
    w2bf_d = nc.dram_tensor("w2bfd", [E, 128, 32 * DM], BF16)
    cc_i = nc.dram_tensor("cc_i", [8, 8], F32)
    cc_o = nc.dram_tensor("cc_o", [8, 8], F32)

    ctx = ExitStack()
    S = lambda nm, shp, dt=F32: ctx.enter_context(nc.sbuf_tensor(nm, shp, dt))

    identb = S("s_identb", [128, 128], BF16)
    identf = S("s_identf", [128, 128])
    triu = S("s_triu", [128, 128], BF16)
    on128 = S("s_on128", [128, 1], BF16)
    onr = S("s_onr", [1, 128])
    onrb = S("s_onrb", [1, 128], BF16)
    ioe = S("s_ioe", [128, TILES * E])
    tokidi = S("s_tokidi", [128, TILES], I32)
    tokidf = S("s_tokidf", [128, TILES])
    slotp = S("s_slotp", [128, NT])
    sel = S("s_sel", [1, 8]); prv = S("s_prv", [8, 8]); id8 = S("s_id8", [8, 8])
    blnz = S("s_blnz", [128, 8]); bdt = S("s_bdt", [128, 8]); nbdt = S("s_nbdt", [128, 8])
    by = S("s_by", [128, 8]); bya = S("s_bya", [128, 8]); h0r = S("s_h0r", [128, 8])
    rwT = S("s_rwT", [128, 8, E])
    b1r = S("s_b1r", [128, E * 32])
    b2bf = S("s_b2bf", [1, 1024], BF16)
    logits = S("s_logits", [128, TILES * E])
    TE = TILES * E
    mrow = S("s_mrow", [128, TILES]); exps = S("s_exps", [128, TE])
    zrow = S("s_zrow", [128, TILES]); rz = S("s_rz", [128, TILES])
    prob = S("s_prob", [128, TE]); mxp = S("s_mxp", [128, TILES])
    igx = S("s_igx", [128, TE]); eqm = S("s_eqm", [128, TE])
    idxm = S("s_idxm", [128, TILES]); idxa = S("s_idxa", [128, TILES])
    ohb = S("s_ohb", [128, TE], BF16); ohf = S("s_ohf", [128, TE])
    tot = S("s_tot", [1, TE]); offa = S("s_offa", [1, TE]); offb = S("s_offb", [1, TE])
    offbc = S("s_offbc", [128, TE]); rt1 = S("s_rt1", [128, TE]); rt2 = S("s_rt2", [128, TE])
    pos0 = S("s_pos0", [128, TILES]); pos = S("s_pos", [128, TILES])
    slot = S("s_slot", [128, TILES]); ovf = S("s_ovf", [128, TILES])
    sloti = S("s_sloti", [128, TILES], I32)
    mgi = S("s_mgi", [128, TILES], I32); dvi = S("s_dvi", [128, TILES], I32)
    tgt = S("s_tgt", [128, TILES], I32)
    cnts = S("s_cnts", [1, E])
    sentt = S("s_sentt", [128, NT], I32)
    sentf = S("s_sentf", [128, NT * 2])
    pgv = S("s_pgv", [128, TILES * 2])
    ptok = S("s_ptok", [128, NT], I32)
    gsb = S("s_gsb", [128, NT]); gm = S("s_gm", [128, NT]); gsb2 = S("s_gsb2", [128, NT])
    ccs = S("s_ccs", [8, 8]); ccm = S("s_ccm", [8, 8]); msb = S("s_msb", [8, 8])
    offcol = S("s_offcol", [8, 1]); offrow = S("s_offrow", [1, 8]); thr = S("s_thr", [128, E])
    mus = S("s_mus", [128, 4]); ssq = S("s_ssq", [128, 4])
    veps0 = S("s_veps0", [128, 4]); mm2 = S("s_mm2", [128, 4]); veps = S("s_veps", [128, 4])
    sdv = S("s_sdv", [128, 4]); rstd = S("s_rstd", [128, 4])

    AR = 176 * 1024
    arena = S("s_arena", [128, AR], mybir.dt.uint8)

    def carve(off, shape, dt):
        esz = {F32: 4, BF16: 2, I32: 4}[dt]
        n = 1
        for s in shape[1:]:
            n *= s
        nb = n * esz
        assert off % 4 == 0 and off + nb <= AR, (off, nb)
        ap_ = arena[:, off:off + nb].bitcast(dt)
        if len(shape) == 3:
            ap_ = ap_.rearrange("p (a b) -> p a b", a=shape[1])
        elif len(shape) == 4:
            ap_ = ap_.rearrange("p (a b c) -> p a b c", a=shape[1], b=shape[2])
        return View(ap_), off + nb

    off = 0
    wslotA, off = carve(off, [128, 2, 8, 1024], BF16)
    wslotB = wslotA
    x1T, off = carve(off, [128, 8, L], BF16)
    hT, off = carve(off, [128, 8, L], BF16)
    a_s, off = carve(off, [128, L], F32)
    z_s, off = carve(off, [128, L], F32)
    u_s, off = carve(off, [128, L], F32)
    xre, off = carve(off, [128, 4, 1024], F32)
    xbuf = View(xre.ap().rearrange("p a d -> p (a d)").rearrange("p (a b d) -> p a b d", a=2, b=2))
    rbuf, off = carve(off, [128, 4, 1024], F32)
    x1tm, off = carve(off, [128, 1024], BF16)
    x2f, off = carve(off, [128, 1024], F32)
    x2b, off = carve(off, [128, 1024], BF16)
    x2T, off = carve(off, [128, 8, 128], F32)
    gate, off = carve(off, [128, 512], F32)
    attnf, off = carve(off, [128, 512], F32)
    sq, off = carve(off, [128, 1024], F32)
    A_END = off
    off = 0
    gtb, off = carve(off, [128, CPT, 1024], BF16)
    xeT0, off = carve(off, [128, 8, CL], BF16)
    xeT1, off = carve(off, [128, 8, CL], BF16)
    xeT = [xeT0, xeT1]
    w1w0, off = carve(off, [128, 8, 512], BF16)
    w1w1, off = carve(off, [128, 8, 512], BF16)
    w1win = [w1w0, w1w1]
    w2sb, off = carve(off, [128, 32, 1024], BF16)
    h_bf, off = carve(off, [128, 32, CL], BF16)
    eob0, off = carve(off, [128, 1024], F32)
    eob1, off = carve(off, [128, 1024], F32)
    eob = [eob0, eob1]
    assert off <= A_END, (off, A_END)

    P = lambda nm, shp, dt=F32: ctx.enter_context(nc.psum_tensor(nm, shp, dt))
    pf0 = P("pf0", [128, 512]); pf1 = P("pf1", [128, 512])
    pf2 = P("pf2", [128, 512]); pf3 = P("pf3", [128, 512])
    pf4 = P("pf4", [128, 512]); pf5 = P("pf5", [128, 512])
    pb0 = P("pb0", [128, 1024], BF16); pb1 = P("pb1", [128, 1024], BF16)

    p = Prog(nc, ctx)
    T_ALL = (None, 0)

    reg_ns = nc.gpsimd.alloc_register("bc_ns")
    reg_nt = nc.gpsimd.alloc_register("bc_nt")

    def setregs(g):
        g.reg_mov(reg_ns, NS - 1)
        return g.reg_mov(reg_nt, NTOK - 1)
    t_regs = p.do("gpsimd", setregs, sig=True, tag="regs")

    for nm, dst, src in [("identb", identb, identb_d), ("identf", identf, identf_d),
                         ("triu", triu, triu_d), ("on128", on128, on128_d),
                         ("onr", onr, onr_d), ("onrb", onrb, onrb_d),
                         ("ioe", ioe, ioe_d), ("tokidi", tokidi, tokid_d), ("tokidf", tokidf, tokidf_d),
                         ("slotp", slotp, slotp_d), ("sel", sel, sel_d),
                         ("prv", prv, prev_d), ("id8", id8, id8_d),
                         ("blnz", blnz, blnz_d), ("bdt", bdt, bdt_d),
                         ("by", by, by_d), ("bya", bya, bya_d),
                         ("h0r", h0r, h0_d), ("b1r", b1r, b1_d)]:
        p.dma("sync", lambda s, d=dst, sr=src: s.dma_start(out=d[...], in_=sr[...]),
              stream="consts", tag="ld_" + nm)
    t_rw = p.dma("sync", lambda s: s.dma_start(out=rwT[...], in_=rw_d.ap().rearrange("(a b) c -> b a c", b=128)),
                 stream="consts", tag="ld_rw")
    t_consts = p.stream_all("consts")
    t_nbdt = p.do("vector", lambda v: v.tensor_scalar(out=nbdt[:, :], in0=bdt[:, :], scalar1=-1.0,
                                                      scalar2=None, op0=OP.mult), after=[t_consts], sig=True, tag="nbdt")
    t_sent = p.do("vector", lambda v: v.memset(sentt[:, :], 1 << 20), sig=True, tag="sent")
    t_sf = p.do("vector", lambda v: v.tensor_copy(out=sentf[:, 0:NT * 2:2], in_=sentt[:, :]),
                after=[t_sent], sig=True, tag="sentf")
    t_sf = p.do("vector", lambda v: v.memset(sentf[:, 1:NT * 2:2], 0.0), after=[t_sf], sig=True, tag="sentg")
    t_pfill = p.dma("sync", lambda s: s.dma_start(out=pg_d.ap().rearrange("(a b) c -> a (b c)", a=128),
                                                  in_=sentf[:, :]), stream="pfill", after=[t_sf], tag="pfill")
    t_gfill = t_pfill

    def load_wslot(slot_t, pair, stream, after):
        toks = []
        for i, wsrc in enumerate(pair):
            toks.append(p.dma("gpsimd",
                lambda g, ii=i, ws=wsrc, sl=slot_t: g.dma_start(
                    out=sl[:, ii, :, :], in_=ws.ap().rearrange("p (a c) -> p a c", a=8)),
                stream=stream, after=after, tag=f"w_{stream}"))
        return toks

    wA_t = load_wslot(wslotA, [wdt, wlnz], "wA", [])
    NPC = 8
    _cast_q = [(e, j) for e in range(E) for j in range(NPC)]
    w2cast_t = {}
    def emit_cast_budget(n):
        for _ in range(n):
            if not _cast_q:
                return
            e, j = _cast_q.pop(0)
            w = 32768 // NPC
            w2cast_t[e] = p.dma("gpsimd", lambda g, ee=e, jj=j, ww=w: g.dma_start(
                out=w2bf_d[ee][:, jj * ww:(jj + 1) * ww], in_=w2_d[ee][:, jj * ww:(jj + 1) * ww]),
                stream="w2cast", after=wA_t, tag="w2cast")
    emit_cast_budget(2 * NPC)

    NQ = L // 512
    tok_x1T_done = [None] * BPC
    tok_x1T_q = [[None] * NQ for _ in range(BPC)]
    tok_hT_done = [None] * BPC
    tok_logits = []
    pf_free = {id(b): T_ALL for b in (pf0, pf1, pf2, pf3, pf4, pf5)}
    pb_free = {id(b): T_ALL for b in (pb0, pb1)}
    xb_free = [T_ALL, T_ALL]
    t_prev_stats = T_ALL
    t_prev_sqrt = T_ALL
    t_x1tm_free = T_ALL
    t_x2f_free = T_ALL
    t_x2b_free = T_ALL
    t_x2T_free = T_ALL
    t_gate_free = T_ALL
    t_attnf_free = T_ALL
    t_rbuf_free = T_ALL
    t_xre_free = T_ALL

    for b in range(BPC):
        x1t_toks = []
        for q in range(NQ):
            half_toks = []
            for h2 in range(2):
                buf = h2
                rows0 = b * L + q * 512 + h2 * 256
                tile0 = rows0 // 128
                t_x = p.dma("sync", lambda s, tl=tile0, bb=buf: s.dma_start(
                    out=xbuf[:, bb, :, :], in_=xr_d[:, tl:tl + 2, :]),
                    stream=f"x{buf}", after=[xb_free[buf], t_xre_free], tag=f"xld{b}{q}{h2}")
                half_toks.append((buf, t_x))
            mu_toks = []
            for h2, (buf, t_x) in enumerate(half_toks):
                for tt2 in range(2):
                    col = h2 * 2 + tt2
                    mu_toks.append(p.do("vector", lambda v, bb=buf, t2=tt2, cc=col: v.tensor_reduce(
                        mus[:, cc:cc + 1], xbuf[:, bb, t2, :].rearrange("p d -> p () d"),
                        axis=AX.X, op=OP.add), after=[t_x, t_prev_stats], sig=True, tag="mu"))
                    mu_toks.append(p.do("scalar", lambda s, bb=buf, t2=tt2, cc=col: s.activation(
                        out=sq[:, :], in_=xbuf[:, bb, t2, :], func=AF.Square,
                        accum_out=ssq[:, cc:cc + 1]), after=[t_x, t_prev_stats], sig=True, tag="ssq"))
            tv = p.do("vector", lambda v: v.tensor_scalar(out=mus[:, :], in0=mus[:, :], scalar1=1.0 / DM,
                                                          scalar2=None, op0=OP.mult), after=mu_toks, sig=True)
            tv2 = p.do("vector", lambda v: (v.tensor_scalar(out=veps0[:, :], in0=ssq[:, :], scalar1=1.0 / DM,
                                                            scalar2=EPS, op0=OP.mult, op1=OP.add),
                                            v.tensor_mul(mm2[:, :], mus[:, :], mus[:, :]))[-1],
                       after=[tv, t_prev_sqrt], sig=True)
            tv2 = p.do("vector", lambda v: v.tensor_sub(veps[:, :], veps0[:, :], mm2[:, :]), after=[tv2], sig=True)
            tsc = p.do("scalar", lambda s: s.activation(out=sdv[:, :], in_=veps[:, :], func=AF.Sqrt),
                       after=[tv2], sig=True, tag="sqrt")
            t_prev_sqrt = tsc
            trc = p.do("vector", lambda v: v.reciprocal(rstd[:, :], sdv[:, :]), after=[tsc], sig=True)
            ap_toks = []
            for h2, (buf, t_x) in enumerate(half_toks):
                for tt2 in range(2):
                    col = h2 * 2 + tt2
                    gt_idx = q * 4 + col
                    ta = p.do("vector", lambda v, bb=buf, t2=tt2, cc=col: v.tensor_scalar(
                        out=x1tm[:, :], in0=xbuf[:, bb, t2, :], scalar1=mus[:, cc:cc + 1],
                        scalar2=rstd[:, cc:cc + 1], op0=OP.subtract, op1=OP.mult),
                        after=[trc, t_x, t_x1tm_free], sig=True, tag="lnap")
                    ap_toks.append(ta)
                    bank = (pb0, pb1)[gt_idx % 2]
                    ttr = p.do("tensor", lambda te, bk=bank: [te.transpose(
                        bk[:, c * 128:(c + 1) * 128], x1tm[:, c * 128:(c + 1) * 128], identb[:, :])
                        for c in range(8)][-1],
                        after=[ta, t_consts, pb_free[id(bank)]], sig=True, tag="x1tr")
                    t_x1tm_free = ttr
                    tcp = p.do("vector", lambda v, bk=bank, g2=gt_idx: v.tensor_copy(
                        out=x1T.ap()[:, :, g2 * 128:(g2 + 1) * 128],
                        in_=bk.ap().rearrange("p (c t) -> p c t", c=8)),
                        after=[ttr], sig=True, tag="x1cp")
                    pb_free[id(bank)] = tcp
                    x1t_toks.append(tcp)
                xb_free[buf] = ap_toks[-1]
            t_prev_stats = ap_toks[-1]
            tok_x1T_q[b][q] = x1t_toks[-1]
        tok_x1T_done[b] = x1t_toks[-1]

        scan_toks = []
        prev_scan = tok_hT_done[b - 1] if b > 0 else T_ALL
        prev_u = prev_scan
        for e in range(8):
            act_dt = []
            act_z = []
            for q in range(NQ):
                bankd = (pf0, pf1)[q % 2]
                tmm = p.do("tensor", lambda te, ee=e, qq=q, bk=bankd: [te.matmul(
                    bk[:, :], wslotA[:, 0, dc, ee * 128:(ee + 1) * 128],
                    x1T[:, dc, qq * 512:(qq + 1) * 512], start=(dc == 0), stop=(dc == 7))
                    for dc in range(8)][-1],
                    after=[tok_x1T_q[b][q], pf_free[id(bankd)]] + wA_t, sig=True, tag="dtmm")
                ta = p.do("scalar", lambda s, qq=q, bk=bankd, ee=e: s.activation(
                    out=a_s[:, qq * 512:(qq + 1) * 512], in_=bk[:, :], func=AF.Sigmoid,
                    bias=nbdt[:, ee:ee + 1], scale=-1.0), after=[tmm, t_nbdt, prev_scan], sig=True, tag="a")
                pf_free[id(bankd)] = ta
                act_dt.append(ta)
                bankz = (pf2, pf3)[q % 2]
                tmz = p.do("tensor", lambda te, ee=e, qq=q, bk=bankz: [te.matmul(
                    bk[:, :], wslotA[:, 1, dc, ee * 128:(ee + 1) * 128],
                    x1T[:, dc, qq * 512:(qq + 1) * 512], start=(dc == 0), stop=(dc == 7))
                    for dc in range(8)][-1],
                    after=[tok_x1T_q[b][q], pf_free[id(bankz)]] + wA_t, sig=True, tag="zmm")
                tz = p.do("scalar", lambda s, qq=q, bk=bankz, ee=e: s.activation(
                    out=z_s[:, qq * 512:(qq + 1) * 512], in_=bk[:, :], func=AF.Sigmoid,
                    bias=blnz[:, ee:ee + 1], scale=1.0), after=[tmz, prev_u], sig=True, tag="z")
                pf_free[id(bankz)] = tz
                act_z.append(tz)
            tu = p.do("vector", lambda v: v.tensor_mul(u_s[:, :], a_s[:, :], z_s[:, :]),
                      after=act_dt + act_z, sig=True, tag="az")
            tu = p.do("vector", lambda v: v.tensor_sub(u_s[:, :], z_s[:, :], u_s[:, :]),
                      after=[tu], sig=True, tag="u")
            tsc2 = p.do("vector", lambda v, ee=e: v.tensor_tensor_scan(
                out=hT.ap()[:, ee, :], data0=a_s[:, :], data1=u_s[:, :],
                initial=h0r[:, ee:ee + 1], op0=OP.mult, op1=OP.add),
                after=[tu, t_consts], sig=True, tag="scan")
            scan_toks.append(tsc2)
            prev_scan = tsc2
            prev_u = tu
        tok_hT_done[b] = scan_toks[-1]

        if DBG:
            p.dma("gpsimd", lambda g, bb=b: g.dma_start(out=oh_d[bb].rearrange("p (c t) -> p (c t)") if False else oh_d[bb],
                                                        in_=hT.ap().rearrange("p c t -> p (c t)")),
                  stream="dbgg", after=[tok_hT_done[b], p.etok("tensor")], tag="hdump")
            p.dma("gpsimd", lambda g, bb=b: g.dma_start(out=ox1_d[bb],
                                                        in_=x1T.ap().rearrange("p c t -> p (c t)")),
                  stream="dbgg", after=[tok_x1T_done[b], p.etok("tensor")], tag="x1dump")
        wB_t = load_wslot(wslotB, [wya, wy], "wA", [p.etok("tensor")])
        emit_cast_budget(3 * NPC if b == 1 else 0)
        for q in range(NQ):
            rows0 = b * L + q * 512
            t_xre = p.dma("sync", lambda s, tl=rows0 // 128: s.dma_start(
                out=xre[...], in_=xr_d[:, tl:tl + 4, :]),
                stream="xre", after=[xb_free[0], xb_free[1], t_xre_free], tag="xre")
            radd_toks = []
            for e in range(8):
                bankg = (pf0, pf1)[e % 2]
                tmy = p.do("tensor", lambda te, ee=e, qq=q, bk=bankg: [te.matmul(
                    bk[:, :], wslotB[:, 0, dc, ee * 128:(ee + 1) * 128],
                    x1T[:, dc, qq * 512:(qq + 1) * 512], start=(dc == 0), stop=(dc == 7))
                    for dc in range(8)][-1],
                    after=[tok_x1T_done[b], pf_free[id(bankg)]] + wB_t, sig=True, tag="yamm")
                tg = p.do("scalar", lambda s, bk=bankg, ee=e: s.activation(
                    out=gate[:, :], in_=bk[:, :], func=AF.Silu,
                    bias=bya[:, ee:ee + 1], scale=1.0), after=[tmy, t_gate_free], sig=True, tag="gate")
                pf_free[id(bankg)] = tg
                banky = (pf0, pf1)[(e + 1) % 2]
                tmy2 = p.do("tensor", lambda te, ee=e, qq=q, bk=banky: [te.matmul(
                    bk[:, :], wslotB[:, 1, dc, ee * 128:(ee + 1) * 128],
                    hT[:, dc, qq * 512:(qq + 1) * 512], start=(dc == 0), stop=(dc == 7))
                    for dc in range(8)][-1],
                    after=[tok_hT_done[b], pf_free[id(banky)]] + wB_t, sig=True, tag="ymm")
                tat = p.do("vector", lambda v, bk=banky, ee=e: v.scalar_tensor_tensor(
                    out=attnf[:, :], in0=bk[:, :], scalar=by[:, ee:ee + 1], in1=gate[:, :],
                    op0=OP.add, op1=OP.mult), after=[tmy2, tg, t_attnf_free], sig=True, tag="attnf")
                pf_free[id(banky)] = tat
                t_gate_free = tat
                bankt = (pf2, pf3)[e % 2]
                ttr = p.do("tensor", lambda te, bk=bankt: [te.transpose(
                    bk[:, tt2 * 128:(tt2 + 1) * 128], attnf[:, tt2 * 128:(tt2 + 1) * 128], identf[:, :])
                    for tt2 in range(4)][-1],
                    after=[tat, pf_free[id(bankt)], t_consts], sig=True, tag="attntr")
                t_attnf_free = ttr
                tra = p.do("vector", lambda v, bk=bankt, ee=e: v.tensor_tensor(
                    out=rbuf.ap()[:, :, ee * 128:(ee + 1) * 128],
                    in0=bk.ap().rearrange("p (t c) -> p t c", t=4),
                    in1=xre.ap()[:, :, ee * 128:(ee + 1) * 128], op=OP.add),
                    after=[ttr, t_xre, t_rbuf_free], sig=True, tag="radd")
                pf_free[id(bankt)] = tra
                radd_toks.append(tra)
            t_xre_free = radd_toks[-1]
            t_rw2 = p.dma("sync", lambda s, r0=rows0: s.dma_start(
                out=out_d[r0:r0 + 512, :].rearrange("(t p) d -> p t d", p=128), in_=rbuf[...]),
                stream="rout", after=radd_toks, tag="rout")
            if DBG:
                t_rw2b = p.dma("sync", lambda s, r0=rows0: s.dma_start(
                    out=or_d[r0:r0 + 512, :].rearrange("(t p) d -> p t d", p=128), in_=rbuf[...]),
                    stream="rout", after=radd_toks, tag="routdbg")
            mu_toks = []
            for tt2 in range(4):
                mu_toks.append(p.do("vector", lambda v, t2=tt2: v.tensor_reduce(
                    mus[:, t2:t2 + 1], rbuf[:, t2, :].rearrange("p d -> p () d"),
                    axis=AX.X, op=OP.add), after=radd_toks + [t_prev_stats], sig=True, tag="mu2"))
                mu_toks.append(p.do("scalar", lambda s, t2=tt2: s.activation(
                    out=sq[:, :], in_=rbuf[:, t2, :], func=AF.Square,
                    accum_out=ssq[:, t2:t2 + 1]), after=radd_toks + [t_prev_stats], sig=True, tag="ssq2"))
            tv = p.do("vector", lambda v: v.tensor_scalar(out=mus[:, :], in0=mus[:, :], scalar1=1.0 / DM,
                                                          scalar2=None, op0=OP.mult), after=mu_toks, sig=True)
            tv2 = p.do("vector", lambda v: (v.tensor_scalar(out=veps0[:, :], in0=ssq[:, :], scalar1=1.0 / DM,
                                                            scalar2=EPS, op0=OP.mult, op1=OP.add),
                                            v.tensor_mul(mm2[:, :], mus[:, :], mus[:, :]))[-1],
                       after=[tv, t_prev_sqrt], sig=True)
            tv2 = p.do("vector", lambda v: v.tensor_sub(veps[:, :], veps0[:, :], mm2[:, :]), after=[tv2], sig=True)
            tsc = p.do("scalar", lambda s: s.activation(out=sdv[:, :], in_=veps[:, :], func=AF.Sqrt),
                       after=[tv2], sig=True)
            t_prev_sqrt = tsc
            trc = p.do("vector", lambda v: v.reciprocal(rstd[:, :], sdv[:, :]), after=[tsc], sig=True)
            x2toks = []
            for tt2 in range(4):
                gt_idx = b * 16 + q * 4 + tt2
                rows_t = rows0 + tt2 * 128
                tf = p.do("vector", lambda v, t2=tt2: v.tensor_scalar(
                    out=x2f[:, :], in0=rbuf[:, t2, :], scalar1=mus[:, t2:t2 + 1],
                    scalar2=rstd[:, t2:t2 + 1], op0=OP.subtract, op1=OP.mult),
                    after=[trc, t_x2f_free], sig=True, tag="ln2ap")
                tb2 = p.do("vector", lambda v: v.tensor_copy(out=x2b[:, :], in_=x2f[:, :]),
                           after=[tf, t_x2b_free], sig=True, tag="x2bf")
                t_x2w2 = p.dma("sync", lambda s, r0=rows_t: s.dma_start(
                    out=x2_d[r0:r0 + 128, :].rearrange("(t p) d -> p (t d)", p=128), in_=x2b[:, :]),
                    stream="x2w", after=[tb2], tag="x2w")
                if DBG:
                    t_x2w3 = p.dma("sync", lambda s, r0=rows_t: s.dma_start(
                        out=ox2_d[r0:r0 + 128, :].rearrange("(t p) d -> p (t d)", p=128), in_=x2f[:, :]),
                        stream="x2w", after=[tf], tag="x2wdbg")
                    tb2 = (tb2[0], max(tb2[1], t_x2w3[1])) if tb2[0] == t_x2w3[0] else tb2
                t_x2b_free = t_x2w2
                rcp = []
                for half in range(2):
                    bk = (pf4, pf5)[half]
                    t1 = p.do("tensor", lambda te, hh=half, bk2=bk: [te.transpose(
                        bk2[:, c2 * 128:(c2 + 1) * 128], x2f[:, (hh * 4 + c2) * 128:(hh * 4 + c2 + 1) * 128],
                        identf[:, :]) for c2 in range(4)][-1],
                        after=[tf, pf_free[id(bk)], t_consts, t_x2T_free], sig=True, tag="x2tr")
                    t2 = p.do("vector", lambda v, hh=half, bk2=bk: v.tensor_copy(
                        out=x2T.ap()[:, hh * 4:(hh + 1) * 4, :],
                        in_=bk2.ap().rearrange("p (c t) -> p c t", c=4)),
                        after=[t1], sig=True, tag="x2cp")
                    pf_free[id(bk)] = t2
                    rcp.append(t2)
                t_x2f_free = rcp[-1]
                bkr = (pf2, pf3)[tt2 % 2]
                tr2 = p.do("tensor", lambda te, bk=bkr: [te.matmul(
                    bk[:, 0:8], x2T[:, dc, :], rwT[:, dc, :], start=(dc == 0), stop=(dc == 7))
                    for dc in range(8)][-1],
                    after=rcp + [t_rw, pf_free[id(bkr)]], sig=True, tag="routmm")
                t_x2T_free = tr2
                tl = p.do("vector", lambda v, g2=gt_idx, bk=bkr: v.tensor_copy(
                    out=logits[:, g2 * 8:(g2 + 1) * 8], in_=bk[:, 0:8]),
                    after=[tr2], sig=True, tag="logcp")
                pf_free[id(bkr)] = tl
                tok_logits.append(tl)
                x2toks.append(tl)
            t_rbuf_free = x2toks[-1]
        if b == 0:
            wA_t2 = load_wslot(wslotA, [wdt, wlnz], "wA", [p.etok("tensor")])
            wA_t = wA_t2
            emit_cast_budget(3 * NPC)

    t_bar = [p.etok("vector"), p.etok("scalar"), p.etok("tensor"),
             p.stream_all("rout"), p.stream_all("x2w"), p.stream_all("xre"),
             p.stream_all("x0"), p.stream_all("x1")]

    W3 = lambda x2_: x2_.ap().rearrange("p (t e) -> p t e", e=E)
    bc3 = lambda a2: a2.ap().rearrange("p t -> p t ()").to_broadcast([128, TILES, E])
    t_log = tok_logits[-1]
    tv = p.do("vector", lambda v: v.tensor_reduce(mrow[:, :], W3(logits), axis=AX.X, op=OP.max),
              after=[t_log], sig=True, tag="smax")
    tv = p.do("vector", lambda v: v.tensor_tensor(out=W3(exps), in0=W3(logits), in1=bc3(mrow), op=OP.subtract),
              after=[tv], sig=True)
    te_ = p.do("scalar", lambda s: s.activation(out=exps[:, :], in_=exps[:, :], func=AF.Exp), after=[tv], sig=True)
    tv = p.do("vector", lambda v: v.tensor_reduce(zrow[:, :], W3(exps), axis=AX.X, op=OP.add), after=[te_], sig=True)
    tv = p.do("vector", lambda v: v.reciprocal(rz[:, :], zrow[:, :]), after=[tv], sig=True)
    tv = p.do("vector", lambda v: v.tensor_tensor(out=W3(prob), in0=W3(exps), in1=bc3(rz), op=OP.mult),
              after=[tv], sig=True)
    tv = p.do("vector", lambda v: v.tensor_reduce(mxp[:, :], W3(prob), axis=AX.X, op=OP.max), after=[tv], sig=True)
    tv = p.do("vector", lambda v: v.tensor_tensor(out=W3(igx), in0=W3(prob), in1=bc3(mxp), op=OP.is_ge),
              after=[tv], sig=True)
    tv = p.do("vector", lambda v: v.scalar_tensor_tensor(out=eqm[:, :], in0=igx[:, :], scalar=-1000.0,
                                                         in1=ioe[:, :], op0=OP.mult, op1=OP.add),
              after=[tv, t_consts], sig=True)
    tv = p.do("vector", lambda v: v.tensor_reduce(idxm[:, :], W3(eqm), axis=AX.X, op=OP.min), after=[tv], sig=True)
    t_idx = p.do("vector", lambda v: v.tensor_scalar(out=idxa[:, :], in0=idxm[:, :], scalar1=1000.0,
                                                     scalar2=None, op0=OP.add), after=[tv], sig=True)
    t_oh = p.do("vector", lambda v: v.tensor_tensor(out=W3(ohb), in0=W3(ioe), in1=bc3(idxa), op=OP.is_equal),
                after=[t_idx], sig=True)
    t_ohf = p.do("vector", lambda v: v.tensor_tensor(out=W3(ohf), in0=W3(ioe), in1=bc3(idxa), op=OP.is_equal),
                 after=[t_idx], sig=True)
    t_cnt = p.do("tensor", lambda t: t.matmul(pf2[:, 0:256], triu[:, :], ohb[:, :], start=True, stop=True),
                 after=[t_oh, t_consts, pf_free[id(pf2)]], sig=True, tag="cntmm")
    t_totm = p.do("tensor", lambda t: t.matmul(pf3[0:1, 0:256], on128[:, :], ohb[:, :], start=True, stop=True),
                  after=[t_oh, t_consts, pf_free[id(pf3)]], sig=True, tag="totmm")
    t_tot = p.do("vector", lambda v: v.tensor_copy(out=tot[:, :], in_=pf3[0:1, 0:256]), after=[t_totm], sig=True)
    pf_free[id(pf3)] = t_tot
    tv = p.do("vector", lambda v: v.memset(offa[:, :], 0.0), after=[t_tot], sig=True)
    tv = p.do("vector", lambda v: v.tensor_copy(out=offa[0:1, E:], in_=tot[0:1, 0:TE - E]), after=[tv], sig=True)
    cur, nxt = offa, offb
    d = E
    while d < TE:
        tv = p.do("vector", lambda v, c=cur, n=nxt, dd=d: (
            v.tensor_copy(out=n[0:1, 0:dd], in_=c[0:1, 0:dd]),
            v.tensor_add(n[0:1, dd:], c[0:1, dd:], c[0:1, 0:TE - dd]))[-1], after=[tv], sig=True)
        cur, nxt = nxt, cur
        d *= 2
    offfin = cur
    t_pre = tv
    t_cnts = p.do("vector", lambda v: v.tensor_reduce(cnts[:, :], tot.ap().rearrange("a (t e) -> a e t", e=E),
                                                      axis=AX.X, op=OP.add), after=[t_tot], sig=True, tag="cnts")
    t_bcm = p.do("tensor", lambda t: t.matmul(pf4[:, 0:256], onr[0:1, :], offfin[0:1, 0:256],
                                              start=True, stop=True),
                 after=[t_pre, t_consts, pf_free[id(pf4)]], sig=True, tag="bcmm")
    tv = p.do("vector", lambda v: v.tensor_copy(out=offbc[:, :], in_=pf4[:, 0:256]), after=[t_bcm], sig=True)
    pf_free[id(pf4)] = tv
    tv = p.do("vector", lambda v: v.tensor_tensor(out=rt1[:, :], in0=pf2[:, 0:256], in1=offbc[:, :], op=OP.add),
              after=[tv, t_cnt], sig=True)
    pf_free[id(pf2)] = tv
    tv = p.do("vector", lambda v: v.tensor_mul(rt2[:, :], rt1[:, :], ohf[:, :]), after=[tv, t_ohf], sig=True)
    tv = p.do("vector", lambda v: v.tensor_reduce(pos0[:, :], W3(rt2), axis=AX.X, op=OP.add), after=[tv], sig=True)
    t_pos = p.do("vector", lambda v: v.tensor_scalar(out=pos[:, :], in0=pos0[:, :], scalar1=-1.0,
                                                     scalar2=None, op0=OP.add), after=[tv], sig=True, tag="pos")
    tv = p.do("vector", lambda v: (v.scalar_tensor_tensor(out=slot[:, :], in0=idxa[:, :], scalar=float(CL),
                                                          in1=pos[:, :], op0=OP.mult, op1=OP.add),
                                   v.tensor_scalar(out=ovf[:, :], in0=pos[:, :], scalar1=float(CL),
                                                   scalar2=None, op0=OP.is_ge))[-1], after=[t_pos], sig=True)
    tv = p.do("vector", lambda v: v.scalar_tensor_tensor(out=slot[:, :], in0=ovf[:, :], scalar=SENT,
                                                         in1=slot[:, :], op0=OP.mult, op1=OP.add),
              after=[tv], sig=True)
    tv = p.do("vector", lambda v: v.tensor_copy(out=sloti[:, :], in_=slot[:, :]), after=[tv], sig=True)
    tv = p.do("vector", lambda v: (v.tensor_scalar(out=mgi[:, :], in0=sloti[:, :], scalar1=127, scalar2=None,
                                                   op0=OP.bitwise_and),
                                   v.tensor_scalar(out=dvi[:, :], in0=sloti[:, :], scalar1=7, scalar2=None,
                                                   op0=OP.arith_shift_right))[-1], after=[tv], sig=True)
    tv = p.do("vector", lambda v: v.tensor_scalar(out=mgi[:, :], in0=mgi[:, :], scalar1=NT, scalar2=None,
                                                  op0=OP.mult), after=[tv], sig=True)
    t_tgt = p.do("vector", lambda v: v.tensor_tensor(out=tgt[:, :], in0=mgi[:, :], in1=dvi[:, :], op=OP.add),
                 after=[tv], sig=True, tag="tgt")
    t_pgv = p.do("vector", lambda v: (v.tensor_copy(out=pgv[:, 0:TILES * 2:2], in_=tokidf[:, :]),
                                      v.tensor_copy(out=pgv[:, 1:TILES * 2:2], in_=mxp[:, :]))[-1],
                 after=[t_consts, tok_logits[-1], (lambda: None, 0) if False else ("vector", p.chan["vector"][1])], sig=True, tag="pgv")
    for t in range(TILES):
        p.dma("gpsimd", lambda g, tt2=t: g.indirect_dma_start(
            out=pg_d.ap(), out_offset=IndirectOffsetOnAxis(tgt[:, tt2:tt2 + 1], 0),
            in_=pgv[:, 2 * tt2:2 * tt2 + 2], in_offset=None, bounds_check=reg_ns, oob_is_err=False),
            stream="pack0", after=[t_tgt, t_pgv, t_pfill, t_regs], tag="scat")
    sc_all = p.stream_all("pack0")
    p.dma("gpsimd", lambda g: g.dma_start(out=sentf[:, :], in_=pg_d.ap().rearrange("(a b) c -> a (b c)", a=128)),
          stream="rb", after=[sc_all], tag="pgrb")
    t_rb0 = p.stream_all("rb")
    t_rb1 = p.do("vector", lambda v: v.tensor_copy(out=ptok[:, :], in_=sentf[:, 0:NT * 2:2]),
                 after=[t_rb0], sig=True, tag="ptokcast")
    t_rb2 = p.do("vector", lambda v: v.tensor_copy(out=gsb[:, :], in_=sentf[:, 1:NT * 2:2]),
                 after=[t_rb0], sig=True, tag="gcast")
    t_rb = ("vector", max(t_rb1[1], t_rb2[1]))
    t_m = p.do("tensor", lambda t: t.matmul(pf5[0:8, 0:8], cnts[0:1, :], sel[0:1, :], start=True, stop=True),
               after=[t_cnts, t_consts, pf_free[id(pf5)]], sig=True, tag="ccmm")
    t_msb = p.do("vector", lambda v: v.tensor_copy(out=msb[:, :], in_=pf5[0:8, 0:8]), after=[t_m], sig=True)
    pf_free[id(pf5)] = t_msb
    t_ccw = p.dma("gpsimd", lambda g: g.dma_start(out=cc_i[:, :], in_=msb[:, :]), stream="cc", after=[t_msb])
    t_cc = p.do("gpsimd", lambda g: g.collective_compute("AllReduce", OP.add, replica_groups=[list(range(NCORES))],
                                                         ins=[cc_i.ap().opt()], outs=[cc_o.ap().opt()]),
                after=[t_ccw], sig=True, tag="cc")
    t_ccr = p.dma("gpsimd", lambda g: g.dma_start(out=ccs[:, :], in_=cc_o[:, :]), stream="cc", after=[t_cc])
    tv2 = p.do("vector", lambda v: v.tensor_mul(ccm[:, :], ccs[:, :], prv[:, :]), after=[t_ccr, t_consts], sig=True)
    tv2 = p.do("vector", lambda v: v.tensor_reduce(offcol[:, :], ccm[:, :], axis=AX.X, op=OP.add), after=[tv2], sig=True)
    t_row = p.do("tensor", lambda t: t.matmul(pf5[0:1, 0:8], offcol[:, :], id8[:, :], start=True, stop=True),
                 after=[tv2, t_consts, pf_free[id(pf5)]], sig=True)
    t_orow = p.do("vector", lambda v: v.tensor_copy(out=offrow[:, :], in_=pf5[0:1, 0:8]), after=[t_row], sig=True)
    pf_free[id(pf5)] = t_orow
    t_thrm = p.do("tensor", lambda t: t.matmul(pf5[:, 0:8], onr[0:1, :], offrow[0:1, :], start=True, stop=True),
                  after=[t_orow, pf_free[id(pf5)]], sig=True)
    t_thr = p.do("vector", lambda v: v.tensor_scalar(out=thr[:, :], in0=pf5[:, 0:8], scalar1=-1.0,
                                                     scalar2=float(CCAP), op0=OP.mult, op1=OP.add),
                 after=[t_thrm], sig=True, tag="thr")
    pf_free[id(pf5)] = t_thr

    def gmask(v):
        last = None
        for e in range(E):
            cols = slice(e * CPT, (e + 1) * CPT)
            last = v.tensor_tensor(out=gm[:, cols], in0=slotp[:, cols],
                                   in1=thr[:, e:e + 1].to_broadcast([128, CPT]), op=OP.is_lt)
        return last
    t_gm = p.do("vector", gmask, after=[t_thr, t_consts], sig=True, tag="gm")
    t_gfix = p.do("vector", lambda v: v.tensor_mul(gsb2[:, :], gsb[:, :], gm[:, :]),
                  after=[t_gm, t_rb], sig=True, tag="gfix")

    t_x2w = p.stream_all("x2w")
    t_gz = p.do("vector", lambda v: v.memset(gtb[:, :, :], 0.0), after=t_bar, sig=True, tag="gz")
    xeT_free = [t_bar[2], t_bar[2]]
    w1_free = [t_bar[2], t_bar[2]]
    eo_free = [t_bar[0], t_bar[0]]
    xeT_tok = [None, None]
    t_rout_all = p.stream_all("rout")

    pf_state = {}

    def prefetch(e):
        par = e % 2
        st = {}
        ga = []
        for t in range(CPT):
            ga.append(p.dma("gpsimd", lambda g, tt2=t, ee=e: g.indirect_dma_start(
                out=gtb[:, tt2, :], out_offset=None, in_=x2_d.ap(),
                in_offset=IndirectOffsetOnAxis(ptok[:, ee * CPT + tt2:ee * CPT + tt2 + 1], 0),
                bounds_check=reg_nt, oob_is_err=False),
                stream=f"gx{par}", after=[t_rb, t_gz, t_x2w, pf_state.get("gt_free", T_ALL), t_regs], tag="gx"))
        st["ga_all"] = p.stream_all(f"gx{par}")
        emit_cast_budget(len(_cast_q))
        st["w1_pre"] = {}
        for fw in (0, 1):
            st["w1_pre"][fw] = p.dma("gpsimd", lambda g, ee=e, ff=fw: g.dma_start(
                out=w1win[ff % 2][...], in_=w1_d[ee, ff].rearrange("p (a c) -> p a c", a=8)),
                stream=f"w1{fw % 2}", after=[w1_free[fw % 2]], tag="w1pre")
        return st

    def prefetch_w2(e, st):
        st["b2"] = p.dma("gpsimd", lambda g, ee=e: g.dma_start(out=b2bf[:, :], in_=b2_d[ee:ee + 1, :]),
                         stream="b2", after=[pf_state.get("prev_eo", t_bar[2])], tag="b2")
        st["w2"] = p.dma("scalar", lambda s, ee=e: s.dma_start(
            out=w2sb[...], in_=w2bf_d[ee].rearrange("p (a c) -> p a c", a=32)),
            stream="w2", after=[pf_state.get("prev_eo", t_bar[2]), w2cast_t[e]], tag="w2")

    def compute_trans(e, st):
        par = e % 2
        trs = []
        for t in range(CPT):
            bank = (pb0, pb1)[t % 2]
            t1 = p.do("tensor", lambda te, tt2=t, bk=bank: [te.transpose(
                bk[:, c * 128:(c + 1) * 128], gtb[:, tt2, c * 128:(c + 1) * 128], identb[:, :])
                for c in range(8)][-1],
                after=[st["ga_all"], pb_free[id(bank)], t_consts, xeT_free[par]], sig=True, tag="xetr")
            t2 = p.do("vector", lambda v, tt2=t, bk=bank, pp=par: v.tensor_copy(
                out=xeT[pp].ap()[:, :, tt2 * 128:(tt2 + 1) * 128],
                in_=bk.ap().rearrange("p (c t) -> p c t", c=8)),
                after=[t1], sig=True, tag="xecp")
            pb_free[id(bank)] = t2
            trs.append(t2)
        pf_state["gt_free"] = trs[-1]
        xeT_tok[par] = trs[-1]

    def compute_h(e, st):
        par = e % 2
        h_toks = []
        for fw in range(8):
            wpar = fw % 2
            if fw in st["w1_pre"]:
                t_w1 = st["w1_pre"][fw]
            else:
                t_w1 = p.dma("gpsimd", lambda g, ee=e, ff=fw: g.dma_start(
                    out=w1win[ff % 2][...], in_=w1_d[ee, ff].rearrange("p (a c) -> p a c", a=8)),
                    stream=f"w1{wpar}", after=[w1_free[wpar]], tag="w1")
            for fc in range(4):
                for ci, (c0, cn) in enumerate([(0, 512), (512, CL - 512)]):
                    bank = (pf2, pf3)[(fc * 2 + ci) % 2]
                    tm = p.do("tensor", lambda te, pp=wpar, f2=fc, cc0=c0, ccn=cn, bk=bank, pr=par: [te.matmul(
                        bk[:, 0:ccn], w1win[pp][:, dc, f2 * 128:(f2 + 1) * 128],
                        xeT[pr][:, dc, cc0:cc0 + ccn], start=(dc == 0), stop=(dc == 7))
                        for dc in range(8)][-1],
                        after=[t_w1, xeT_tok[par], pf_free[id(bank)]], sig=True, tag="hmm")
                    th = p.do("scalar", lambda s, ff=fw, f2=fc, cc0=c0, ccn=cn, bk=bank, ee=e: s.activation(
                        out=h_bf[:, ff * 4 + f2, cc0:cc0 + ccn], in_=bk[:, 0:ccn], func=AF.Gelu,
                        bias=b1r[:, ee * 32 + ff * 4 + f2:ee * 32 + ff * 4 + f2 + 1], scale=1.0),
                        after=[tm, t_consts], sig=True, tag="gelu")
                    pf_free[id(bank)] = th
                    h_toks.append(th)
            w1_free[wpar] = h_toks[-1]
        return h_toks[-1]

    def compute_eo(e, st, t_hlast):
        par = e % 2
        last_mm = None
        for tt2 in range(CPT):
            epar = tt2 % 2
            evs = []
            for dh in range(2):
                bank = (pf0, pf1)[dh]
                def eomm(te, t2=tt2, dd=dh, bk=bank):
                    te.matmul(bk[:, :], onrb[0:1, :], b2bf[0:1, dd * 512:(dd + 1) * 512],
                              start=True, stop=False)
                    last = None
                    for fc in range(32):
                        last = te.matmul(bk[:, :], h_bf[:, fc, t2 * 128:(t2 + 1) * 128],
                                         w2sb[:, fc, dd * 512:(dd + 1) * 512],
                                         start=False, stop=(fc == 31))
                    return last
                tm = p.do("tensor", eomm,
                          after=[st["w2"], st["b2"], pf_free[id(bank)], eo_free[epar], t_hlast, t_consts],
                          sig=True, tag="eomm")
                last_mm = tm
                tev = p.do("vector", lambda v, t2=tt2, dd=dh, bk=bank, ee=e, pp=epar: v.tensor_scalar(
                    out=eob[pp][:, dd * 512:(dd + 1) * 512], in0=bk[:, :],
                    scalar1=gsb2[:, ee * CPT + t2:ee * CPT + t2 + 1], scalar2=None, op0=OP.mult),
                    after=[tm, t_gfix], sig=True, tag="eoev")
                pf_free[id(bank)] = tev
                evs.append(tev)
            t_sa = p.dma("gpsimd", lambda g, t2=tt2, ee=e, pp=epar: g.indirect_dma_start(
                out=out_d.ap(), out_offset=IndirectOffsetOnAxis(ptok[:, ee * CPT + t2:ee * CPT + t2 + 1], 0),
                in_=eob[pp][:, :], in_offset=None, bounds_check=reg_nt, oob_is_err=False,
                compute_op=OP.add), stream="sa",
                after=evs + [t_rb, t_rout_all], tag="sa")
            eo_free[epar] = t_sa
        pf_state["prev_eo"] = last_mm
        xeT_free[par] = last_mm

    st_cur = prefetch(0)
    prefetch_w2(0, st_cur)
    for e in range(E):
        compute_trans(e, st_cur)
        t_hlast = compute_h(e, st_cur)
        st_next = prefetch(e + 1) if e + 1 < E else None
        compute_eo(e, st_cur, t_hlast)
        if st_next is not None:
            prefetch_w2(e + 1, st_next)
        st_cur = st_next

    t_sa_all = p.stream_all("sa")
    dbg_all = p.stream_all("dbg") if DBG else (None, 0)
    dbgg_all = p.stream_all("dbgg") if DBG else (None, 0)
    p.final_waits("gpsimd", [t_sa_all, t_rout_all, dbg_all, dbgg_all])
    p.final_waits("sync", [t_sa_all, t_rout_all])
    p.verify()
    with nc.Block() as block:
        p.build(block)
    ctx.close()
    return nc


_NC_CACHE = None


def kernel(**inputs):
    global _NC_CACHE
    xs = np.asarray(inputs["L0_attn_x"], np.float32).reshape(NCORES, NTOK, DM)
    xrs = np.ascontiguousarray(xs.reshape(NCORES, TILES, 128, DM).transpose(0, 2, 1, 3))
    def wslotr(w):
        return np.ascontiguousarray(
            np.asarray(w, np.float32).T.reshape(8, 128, DM).transpose(1, 0, 2).reshape(128, 8 * DM))
    wlnzT = wslotr(inputs["L0_attn_w_ln_z"])
    wdtT = wslotr(inputs["L0_attn_w_dt"])
    wyT = wslotr(inputs["L0_attn_w_y"])
    wyaT = wslotr(inputs["L0_attn_w_ya"])
    routerT = np.ascontiguousarray(np.asarray(inputs["L0_ffn_router_w"], np.float32).T)
    arr = lambda v: np.ascontiguousarray(np.asarray(v, np.float32))
    col8 = lambda v: np.ascontiguousarray(np.asarray(v, np.float32).reshape(8, 128).T)
    blnz = col8(inputs["L0_attn_b_ln_z"]); bdt = col8(inputs["L0_attn_b_dt"])
    by = col8(inputs["L0_attn_b_y"]); bya = col8(inputs["L0_attn_b_ya"])
    h0r = col8(inputs["L0_attn_h0"])
    w1 = np.ascontiguousarray(np.asarray(inputs["L0_ffn_w1"], np.float32)
                              .reshape(E, 8, 128, 8, 512).transpose(0, 3, 2, 1, 4).reshape(E, 8, 128, 4096))
    w2 = np.ascontiguousarray(np.asarray(inputs["L0_ffn_w2"], np.float32)
                              .reshape(E, 32, 128, DM).transpose(0, 2, 1, 3).reshape(E, 128, 32 * DM))
    b1r = np.ascontiguousarray(np.asarray(inputs["L0_ffn_b1"], np.float32).reshape(E, 32, 128).transpose(2, 0, 1).reshape(128, E * 32))
    b2n = arr(inputs["L0_ffn_b2"])

    identb = np.eye(128, dtype=np.float32).astype(bfd)
    identf = np.eye(128, dtype=np.float32)
    triu = np.triu(np.ones((128, 128), np.float32)).astype(bfd)
    ones128 = np.ones((128, 1), np.float32).astype(bfd)
    onesrow = np.ones((1, 128), np.float32)
    onesrowb = onesrow.astype(bfd)
    iotae = np.tile(np.arange(E, dtype=np.float32), (128, TILES))
    tokidi = (np.arange(TILES)[None, :] * 128 + np.arange(128)[:, None]).astype(np.int32)
    tokidf = tokidi.astype(np.float32)
    slotpos = np.zeros((128, NT), np.float32)
    for e in range(E):
        for t in range(CPT):
            slotpos[:, e * CPT + t] = t * 128 + np.arange(128)
    ident8 = np.eye(8, dtype=np.float32)

    if _NC_CACHE is None:
        _NC_CACHE = build_program()
    nc = _NC_CACHE

    in_maps = []
    for k in range(NCORES):
        selrow = np.zeros((1, 8), np.float32); selrow[0, k] = 1.0
        prevmask = np.zeros((8, 8), np.float32); prevmask[:, :k] = 1.0
        in_maps.append({
            "x": xs[k], "xr": xrs[k], "wlnzT": wlnzT, "wdtT": wdtT, "wyT": wyT, "wyaT": wyaT,
            "blnz": blnz, "bdt": bdt, "by": by, "bya": bya, "h0r": h0r,
            "routerT": routerT, "w1r": w1, "b1r": b1r, "w2r": w2, "b2n": b2n,
            "identb": identb, "identf": identf, "triu": triu,
            "ones128": ones128, "onesrow": onesrow, "onesrowb": onesrowb,
            "iotae": iotae, "tokidi": tokidi, "tokidf": tokidf, "slotpos": slotpos,
            "selrow": selrow, "prevmask": prevmask, "ident8": ident8,
        })
    res = run_bass_kernel_spmd(nc, in_maps, core_ids=list(range(NCORES)))
    out = np.concatenate([np.asarray(res.results[k]["out"], np.float32)[None] for k in range(NCORES)], 0)
    return out.reshape(16, 2048, 1024)
